# version 31
# baseline (speedup 1.0000x reference)
import os
import numpy as np
from contextlib import ExitStack

import concourse.bass as bass
import concourse.mybir as mybir
from concourse.bass_utils import run_bass_kernel_spmd

F32 = mybir.dt.float32
BF16 = mybir.dt.bfloat16
FP8 = mybir.dt.float8e4
AF = mybir.ActivationFunctionType
ALU = mybir.AluOpType

D, FT, DI, DIT = 1024, 8, 2048, 16
CD, CT, DIP, H, HD = 2304, 18, 4384, 32, 64
QC, QS = 256, 128
NSUB = QC // QS
EPS = 1e-5

_SET_IDS = {}


def _act_set_ids(arch):
    if arch not in _SET_IDS:
        from concourse.hw_specs import get_activation_tables
        tabs = list(get_activation_tables(arch).values())
        exp_ln = next(i for i, fns in enumerate(tabs)
                      if AF.Exp in fns and AF.Ln in fns)
        silu = next(i for i, fns in enumerate(tabs) if AF.Silu in fns)
        _SET_IDS[arch] = (exp_ln, silu)
    return _SET_IDS[arch]


def _load_set(scalar, set_id):
    return scalar.add_instruction(
        mybir.InstLoadActFuncSet(
            name=scalar.bass.get_next_instruction_name(),
            act_func_set_id=set_id, ins=[], outs=[]))


def rep_outer(ap2d, outer):
    P, n = ap2d.shape
    return ap2d[:, None, :].broadcast_to((P, outer, n))


def rep_inner(ap2d, inner):
    P, n = ap2d.shape
    return ap2d[:, :, None].broadcast_to((P, n, inner))


class Sems:
    def __init__(self, nc, es):
        self.nc, self.es, self.h, self.c = nc, es, {}, {}

    def _get(self, name):
        if name not in self.h:
            self.h[name] = self.es.enter_context(self.nc.semaphore(name))
            self.c[name] = 0
        return self.h[name]

    def inc(self, instr, name, n=1):
        instr.then_inc(self._get(name), n)
        self.c[name] += n
        return instr

    def wait(self, eng, name, lag=0):
        eng.wait_ge(self._get(name), self.c[name] - lag)


_STAGES = ["pro", "xtr", "ln1", "ffn1", "ln2", "ipx", "dtc", "z", "conv",
            "silu", "ssd", "opj", "ln3", "ffn2", "out"]


def build_nc(T=2048):
    stage = os.environ.get("KSTAGE", "out")
    slvl = _STAGES.index(stage)

    def on(lvl):
        return slvl >= _STAGES.index(lvl)

    nch = T // QC
    nc = bass.Bass("TRN2", target_bir_lowering=False,
                   dynamic_dma_scratch_size=4096)
    SET_EXP_LN, SET_SILU = _act_set_ids(nc.m.arch)

    x_h = nc.dram_tensor("x", [T, D], F32, kind="ExternalInput").ap()
    out_h = nc.dram_tensor("out", [T, D], F32, kind="ExternalOutput").ap()
    wip_h = nc.dram_tensor("w_ip", [DIP, D], F32, kind="ExternalInput").ap()
    wop_h = nc.dram_tensor("w_op", [D, DI], F32, kind="ExternalInput").ap()
    wff_h = nc.dram_tensor("w_ff", [D, D], F32, kind="ExternalInput").ap()
    bff_h = nc.dram_tensor("b_ff", [D], F32, kind="ExternalInput").ap()
    lnw_h = nc.dram_tensor("ln_w", [D], F32, kind="ExternalInput").ap()
    lnb_h = nc.dram_tensor("ln_b", [D], F32, kind="ExternalInput").ap()
    cw_h = nc.dram_tensor("conv_w", [CD, 4], F32, kind="ExternalInput").ap()
    cb_h = nc.dram_tensor("conv_b", [CD], F32, kind="ExternalInput").ap()
    al_h = nc.dram_tensor("a_log", [H], F32, kind="ExternalInput").ap()
    dv_h = nc.dram_tensor("d_vec", [H], F32, kind="ExternalInput").ap()
    dtb_h = nc.dram_tensor("dt_bias", [H], F32, kind="ExternalInput").ap()
    gam_h = nc.dram_tensor("gamma", [DI], F32, kind="ExternalInput").ap()

    es = ExitStack()
    sb = lambda n, sh, dt=F32: es.enter_context(nc.sbuf_tensor(n, sh, dt))
    S = Sems(nc, es)

    wipt = sb("wipt", [128, FT * DIP], BF16)
    wopt = sb("wopt", [128, DIT * D], BF16)
    wfft = sb("wfft", [128, FT * D], FP8)
    convw = sb("convw", [128, CT * 4])
    convb = sb("convb", [128, CT])
    lnw_f = sb("lnw_f", [128, FT])
    lnb_f = sb("lnb_f", [128, FT])
    bffh = sb("bffh", [128, FT])
    gam_f = sb("gam_f", [128, DIT])
    dtb_c = sb("dtb_c", [H, 1])
    aneg = sb("aneg", [H, 1])
    dvall = sb("dvall", [128, DIT])
    ddiag = sb("ddiag", [128, DIT * 128], BF16)
    tri_b = sb("tri_b", [128, 128], BF16)
    id_b = sb("id_b", [128, 128], BF16)
    id_f = sb("id_f", [128, 128], F32)
    onesr2 = sb("onesr2", [1, 128])
    ones2r = sb("ones2r", [2, 128], BF16)
    ones_c = sb("ones_c", [128, 1], BF16)
    land = [sb(f"land{i}", [128, 2 * D]) for i in range(2)]
    x_fm = sb("x_fm", [128, FT * QC])
    x_fm2 = sb("x_fm2", [128, FT * QC])
    x_fms = [x_fm, x_fm2]
    scrA = sb("scrA", [128, H * QS], BF16)
    x_bf = scrA[:, FT * QC:2 * FT * QC]
    u1_f8 = sb("u1_f8", [128, FT * QC], FP8)
    scr2 = sb("scr2", [128, DI], BF16)
    u2_bf = sb("u2_bf", [128, FT * QC], BF16)
    sdec = u2_bf
    rb_t = sb("rb_t", [128, QC], BF16)
    mrb_t = sb("mrb_t", [128, QC], BF16)
    rows = sb("rows", [1, 768])
    xbcp = sb("xbcp", [128, CT * (QC + 4)], BF16)
    yfm = xbcp[:, 0:DIT * QC]
    cacc = sb("cacc", [128, CT * QC], BF16)
    z_si = sb("z_si", [128, NSUB * DI], BF16)
    dt_f = sb("dt_f", [H, QC])
    a_f = sb("a_f", [H, QC])
    cum_f = sb("cum_f", [H, QC])
    ccf = a_f
    dt_tm = sb("dt_tm", [128, NSUB * H], BF16)
    cum_tm = sb("cum_tm", [128, NSUB * H])
    cexp_tm = sb("cexp_tm", [128, NSUB * H], BF16)
    cc_tm = sb("cc_tm", [128, NSUB * H], BF16)
    ccq_b = sb("ccq_b", [128, NSUB * H], BF16)
    stri = sb("stri", [128, QS], BF16)
    xdt = sb("xdt", [128, DI], BF16)
    btm = sb("btm", [128, 128], BF16)
    s_bf = sb("s_bf", [128, DI], BF16)
    y_bf = sb("y_bf", [128, DI], BF16)
    ssq = sb("ssq", [128, 1])
    rms_c = sb("rms_c", [128, 1])
    tl3 = sb("tl3", [128, CT * 3], BF16)
    badd_z = sb("badd_z", [1, DI], BF16)
    badd_fm = sb("badd_fm", [128, 19])
    lnb_b = sb("lnb_b", [128, FT], BF16)
    trin_b = sb("trin_b", [128, 128], BF16)
    rb1_t = sb("rb1_t", [128, QC], BF16)
    mu1_t = sb("mu1_t", [128, QC], BF16)
    cumhi = sb("cumhi", [H, QC], BF16)
    cumlo = sb("cumlo", [H, QC], BF16)
    neg32f = sb("neg32f", [32, 32])
    cst = sb("cst", [128, 16])
    _cvals = [EPS, 1.0, -0.5, -1.0, 1.0 / DI, 0.5, 448.0, -448.0,
              EPS * EPS]
    for _i, _v in enumerate(_cvals):
        nc.const_aps.register(F32, _v, cst[:, _i:_i + 1])
    pA = es.enter_context(nc.psum_tensor("pA", [128, 2048], F32))
    pB = es.enter_context(nc.psum_tensor("pB", [128, 2048], F32))

    blk = es.enter_context(nc.Block())

    _lt = [None]

    def mm(t, out, lhsT, rhs, first, last=True, perf_mode=None):
        i = t.matmul(out, lhsT, rhs, start=first, stop=last,
                     skip_group_check=True, perf_mode=perf_mode)
        _lt[0] = i
        return i

    DR = mybir.MatmulPerfMode.DoubleRow

    def ffn_mms(t, pp):
        for ct in range(FT):
            for k2 in range(4):
                mm(t, pp[:, ct * QC:(ct + 1) * QC],
                   bass.AP(wfft, 2 * k2 * D + ct * 128,
                           [[FT * D, 128], [D, 2], [1, 128]]),
                   bass.AP(u1_f8, 2 * k2 * QC,
                           [[FT * QC, 128], [QC, 2], [1, QC]]),
                   first=(ct % 2 == 0 and k2 == 0), last=(k2 == 3),
                   perf_mode=DR)

    def tr(t, out, in_, ident):
        i = t.transpose(out, in_, ident)
        _lt[0] = i
        return i

    wip = lambda i, a, b: wipt[:, i * DIP + a: i * DIP + b]
    wopd = lambda dk, a, b: wopt[:, dk * D + a: dk * D + b]
    wff = lambda i, a, b: wfft[:, i * D + a: i * D + b]

    def p_sync0(y):
        def dma(dst, src, tag="wsm"):
            S.inc(y.dma_start(out=dst, in_=src,
                              allow_slow_non_contiguous=True), tag, 16)
        dma(lnw_f[:], bass.AP(lnw_h.tensor, 0, [[1, 128], [128, FT]]))
        dma(lnb_f[:], bass.AP(lnb_h.tensor, 0, [[1, 128], [128, FT]]))
        dma(bffh[:], bass.AP(bff_h.tensor, 0, [[1, 128], [128, FT]]))
        dma(gam_f[:], bass.AP(gam_h.tensor, 0, [[1, 128], [128, DIT]]))
        dma(convb[:], bass.AP(cb_h.tensor, 0, [[1, 128], [128, CT]]))
        dma(convw[:], bass.AP(cw_h.tensor, 0, [[4, 128], [512, CT], [1, 4]]))
        dma(dtb_c[:], bass.AP(dtb_h.tensor, 0, [[1, H], [1, 1]]))
        dma(aneg[:], bass.AP(al_h.tensor, 0, [[1, H], [1, 1]]))
        for i in range(DIT):
            dma(dvall[0:64, i:i + 1],
                bass.AP(dv_h.tensor, 2 * i, [[0, 64], [1, 1]]))
            dma(dvall[64:128, i:i + 1],
                bass.AP(dv_h.tensor, 2 * i + 1, [[0, 64], [1, 1]]))
        for tt in range(2):
            dma(land[0][:, tt * D:(tt + 1) * D],
                x_h[tt * 128:(tt + 1) * 128, :], "xinA")

    blk.sync(p_sync0)

    def p_gp(g):
        f = land[1][:, 1024:1152]
        g.memset(f, 1.0)
        g.drain()
        g.affine_select(f, f, pattern=[[1, 128]], compare_op=ALU.is_ge,
                        fill=0.0, base=0, channel_multiplier=-1)
        g.drain()
        g.tensor_copy(tri_b[:], f)
        g.drain()
        g.affine_select(f, f, pattern=[[-1, 128]], compare_op=ALU.is_ge,
                        fill=0.0, base=0, channel_multiplier=1)
        g.drain()
        g.tensor_copy(id_b[:], f)
        g.tensor_copy(id_f[:], f)
        g.drain()
        g.memset(f, 0.0)
        g.drain()
        g.affine_select(f, f, pattern=[[-1, 128]], compare_op=ALU.is_ge,
                        fill=-1.0e6, base=0, channel_multiplier=1)
        g.drain()
        g.tensor_copy(trin_b[:], f)
        for _i, _v in enumerate(_cvals):
            g.memset(cst[:, _i:_i + 1], _v)
        g.memset(onesr2[0:1, 0:128], 1.0)
        g.memset(ones2r[0:2, 0:128], 1.0)
        g.memset(neg32f[0:32, 0:32], -1.0)
        g.drain()
        g.affine_select(neg32f[:], neg32f[:], pattern=[[1, 32]],
                        compare_op=ALU.is_ge, fill=0.0, base=0,
                        channel_multiplier=-1)
        g.drain()
        g.affine_select(neg32f[:], neg32f[:], pattern=[[-1, 32]],
                        compare_op=ALU.is_ge, fill=0.0, base=0,
                        channel_multiplier=1)
        g.drain()
        g.memset(ones_c[:], 1.0 / D)
        g.memset(s_bf[:], 0.0)
        for ct in range(CT):
            g.memset(xbcp[:, ct * (QC + 4):ct * (QC + 4) + 3], 0.0)
        g.drain()
        S.inc(g.drain(), "gpc", 1)

    blk.gpsimd(p_gp)

    def p_v0(v):
        S.wait(v, "gpc")
        S.wait(v, "wsm")
        v.tensor_tensor(
            ddiag[:].rearrange("p (i c) -> p i c", i=DIT),
            rep_outer(id_b[:], DIT),
            rep_inner(dvall[:], 128), ALU.mult)
        v.drain()
        S.inc(v.drain(), "dcon", 1)

    blk.vector(p_v0)

    def p_s0(s):
        _load_set(s, SET_EXP_LN)
        S.wait(s, "wsm")
        s.activation(aneg[:], aneg[:], AF.Exp)
        s.drain()
        s.mul(aneg[:], aneg[:], -1.0)
        s.mul(bffh[:], bffh[:], 0.5)
        s.drain()
        sc1 = land[1][:, 1152:1152 + FT]
        sc2 = land[1][:, 1152 + FT:1152 + 2 * FT]
        s.activation(sc1, lnw_f[:], AF.Abs)
        s.drain()
        s.activation(sc1, sc1, AF.Ln)
        s.drain()
        s.activation(sc1, sc1, AF.Exp, scale=-1.0)
        s.activation(sc2, lnw_f[:], AF.Sign)
        s.drain()
        S.inc(s.drain(), "acon", 1)

    blk.scalar(p_s0)

    w_jobs = [("ff", j) for j in range(FT)]
    w_jobs += [("ip", j) for j in range(35)]
    for j in range(FT):
        w_jobs += [("opA", j), ("opB", j)]

    def p_syncw(y):
        S.wait(y, "gpc")
        S.wait(y, "acon")
        for n, (kind, j) in enumerate(w_jobs):
            if n >= 1:
                y.wait_ge(S._get("stgf"), n)
            if kind == "ip" and j == 34:
                dst = land[1][0:32, 0:D]
                src = wip_h[34 * 128:DIP, :]
            else:
                dst = land[1][:, 0:D]
                if kind == "ff":
                    src = wff_h[j * 128:(j + 1) * 128, :]
                elif kind == "ip":
                    src = wip_h[j * 128:(j + 1) * 128, :]
                elif kind == "opA":
                    src = wop_h[j * 128:(j + 1) * 128, 0:D]
                else:
                    src = wop_h[j * 128:(j + 1) * 128, D:2 * D]
            S.inc(y.dma_start(out=dst, in_=src), "wst", 16)

    blk.sync(p_syncw)

    def p_tw(t):
        for n, (kind, j) in enumerate(w_jobs):
            t.wait_ge(S._get("wst"), 16 * (n + 1))
            if n >= 2:
                t.wait_ge(S._get("wev"), n - 1)
            src = land[1][:, 0:D]
            pp = pA if n % 2 == 0 else pB
            rows_in = 32 if (kind == "ip" and j == 34) else 128
            for i in range(FT):
                off = (i // 4) * 512 + (i % 4) * 128
                tr(t, pp[:, off:off + rows_in],
                            src[0:rows_in, i * 128:(i + 1) * 128],
                            id_f[0:rows_in, 0:rows_in])
            S.inc(_lt[0], "wtr", 1)
            S.inc(t.drain(), "stgf", 1)

        t.wait_ge(S._get("wev"), len(w_jobs))
        for tt2 in range(19):
            a0 = DI + tt2 * 128
            b0 = min(a0 + 128, DIP)
            for k in range(FT):
                mm(t, pA[0:b0 - a0, tt2:tt2 + 1], wip(k, a0, b0),
                   lnb_b[:, k:k + 1], first=(k == 0), last=(k == FT - 1))
        for nb in range(4):
            for k in range(FT):
                mm(t, pB[0:1, nb * 512:(nb + 1) * 512], lnb_b[:, k:k + 1],
                   wip(k, nb * 512, (nb + 1) * 512),
                   first=(k == 0), last=(k == FT - 1))
        S.inc(_lt[0], "badp", 1)

    blk.tensor(p_tw)

    def p_vw(v):
        S.wait(v, "acon")
        v.tensor_tensor(lnb_b[:], lnb_f[:], land[1][:, 1152:1152 + FT],
                        ALU.mult)
        v.drain()
        v.tensor_tensor(lnb_b[:], lnb_b[:],
                        land[1][:, 1152 + FT:1152 + 2 * FT], ALU.mult)
        v.drain()
        for n, (kind, j) in enumerate(w_jobs):
            v.wait_ge(S._get("wtr"), n + 1)
            pp = pA if n % 2 == 0 else pB
            rows_in = 32 if (kind == "ip" and j == 34) else 128
            for bank in range(2):
                base, i0 = bank * 512, bank * 4
                view = pp[:, base:base + 512].rearrange(
                    "p (i c) -> p i c", i=4)[:, :, 0:rows_in]
                if kind == "ff":
                    dstv = wfft[:].rearrange("p (i c) -> p i c", i=FT)[
                        :, i0:i0 + 4, j * 128:j * 128 + rows_in]
                    v.tensor_scalar(dstv, view, 448.0, -448.0,
                                    ALU.min, ALU.max)
                elif kind == "ip":
                    for ii in range(4):
                        dstv = wipt[:].rearrange("p (i c) -> p i c", i=FT)[
                            :, i0 + ii, j * 128:j * 128 + rows_in]
                        v.tensor_scalar(
                            dstv, pp[:, base + ii * 128:
                                     base + ii * 128 + rows_in],
                            lnw_f[:, i0 + ii:i0 + ii + 1], None, ALU.mult)
                else:
                    dk0 = (0 if kind == "opA" else 8) + i0
                    dstv = wopt[:].rearrange("p (i c) -> p i c", i=DIT)[
                        :, dk0:dk0 + 4, j * 128:j * 128 + rows_in]
                    v.tensor_tensor(dstv, view,
                                    rep_inner(gam_f[:, dk0:dk0 + 4], rows_in),
                                    ALU.mult)
            v.drain()
            S.inc(v.drain(), "wev", 1)
        S.wait(v, "badp")
        S.wait(v, "gpc")
        v.tensor_copy(badd_fm[:], pA[:, 0:19])
        for nb in range(4):
            v.tensor_copy(badd_z[0:1, nb * 512:(nb + 1) * 512],
                          pB[0:1, nb * 512:(nb + 1) * 512])
        v.drain()
        cwv = convw[:].rearrange("p (i k) -> p i k", i=CT)
        ws = land[1][:, 1024:1024 + CT]
        ws2 = land[1][:, 1024 + CT:1024 + 2 * CT]
        v.tensor_tensor(ws, cwv[:, :, 0], cwv[:, :, 1], ALU.add)
        v.tensor_tensor(ws2, cwv[:, :, 2], cwv[:, :, 3], ALU.add)
        v.drain()
        v.tensor_tensor(ws, ws, ws2, ALU.add)
        v.drain()
        v.tensor_tensor(ws, ws, badd_fm[:, 0:CT], ALU.mult)
        v.drain()
        v.tensor_tensor(convb[:], convb[:], ws, ALU.add)
        v.tensor_tensor(dtb_c[:], dtb_c[:], badd_fm[0:H, CT:CT + 1], ALU.add)
        v.tensor_scalar(
            xbcp[:].rearrange("p (i t) -> p i t", i=CT)[:, :, 0:3],
            rep_inner(badd_fm[:, 0:CT], 3), -1.0, None, ALU.mult)
        v.drain()
        S.inc(v.drain(), "wrdy", 1)

    blk.vector(p_vw)

    def xv(buf):
        return buf[:].rearrange("p (i t) -> p i t", i=FT)

    def emit_ln_stats(xmark, xf, po=0):
        def s_x2(s):
            S.wait(s, xmark)
            S.wait(s, "upre", lag=0)
            S.wait(s, "xres", lag=0)
            s.copy(x_bf, xf[:])
            s.drain()
            S.inc(s.drain(), "xbf", 1)
            S.wait(s, "stp", lag=0)
            s.activation(scrA[:, 0:FT * QC], x_bf, AF.Square)
            s.drain()
            S.inc(s.drain(), "x2", 1)

        def t_stats(t):
            S.wait(t, "xbf")
            S.wait(t, "x2")
            S.wait(t, "pBfree", lag=0)
            for k in range(FT):
                mm(t, pB[0:1, po:po + 512], ones_c[:],
                   bass.AP(scrA, k * QC,
                           [[H * QS, 128], [FT * QC, 2], [1, QC]]),
                   first=(k == 0), last=(k == FT - 1))
            S.inc(_lt[0], "stp", 1)

        blk.scalar(s_x2)
        blk.tensor(t_stats)

    def emit_ln_rest(fused, udst_f8, xf, po=0, ugate=None):
        def s_m1(s):
            S.wait(s, "stp")
            s.activation(rows[0:1, 0:256], pB[0:1, po + 256:po + 512],
                         AF.Square)
            s.drain()
            S.inc(s.drain(), "m1", 1)

        def v_m1(v):
            S.wait(v, "m1")
            v.tensor_tensor(rows[0:1, 256:512], pB[0:1, po:po + 256],
                            rows[0:1, 0:256], ALU.subtract)
            v.drain()
            S.inc(v.drain(), "m2", 1)

        blk.scalar(s_m1)
        blk.vector(v_m1)

        if not fused:
            def s_m2(s):
                S.wait(s, "m2")
                s.activation(rows[0:1, 0:256], rows[0:1, 256:512],
                             AF.Ln, bias=EPS)
                s.drain()
                s.activation(rows[0:1, 0:256], rows[0:1, 0:256],
                             AF.Exp, scale=-0.5)
                s.copy(rows[0:1, 256:512], pB[0:1, po + 256:po + 512])
                s.drain()
                S.inc(s.drain(), "m4", 1)

            blk.scalar(s_m2)
        else:
            def s_m2(s):
                S.wait(s, "m2")
                s.activation(rows[0:1, 512:768], rows[0:1, 256:512],
                             AF.Ln, bias=EPS)
                s.drain()
                s.activation(rows[0:1, 256:512], rows[0:1, 256:512],
                             AF.Ln, scale=1.0 + EPS, bias=EPS * EPS)
                s.drain()
                s.activation(rows[0:1, 512:768], rows[0:1, 512:768],
                             AF.Exp, scale=-0.5)
                s.activation(rows[0:1, 256:512], rows[0:1, 256:512],
                             AF.Exp, scale=-0.5)
                s.drain()
                s.copy(rows[0:1, 0:256], pB[0:1, po + 256:po + 512])
                s.drain()
                S.inc(s.drain(), "m5", 1)

            def t_bcf(t):
                S.wait(t, "m5")
                mm(t, pB[:, 512:1024], onesr2[0:1, 0:128],
                   rows[0:1, 0:512], first=True)
                mm(t, pB[:, 1024:1280], onesr2[0:1, 0:128],
                   rows[0:1, 512:768], first=True)
                S.inc(_lt[0], "bcp", 1)
                S.inc(t.drain(), "pBfree", 1)

            def v_bcf(v):
                S.wait(v, "bcp")
                v.tensor_copy(mu1_t[:], pB[:, 512:768])
                v.tensor_copy(rb_t[:], pB[:, 768:1024])
                v.tensor_copy(rb1_t[:], pB[:, 1024:1280])
                v.drain()
                v.tensor_tensor(mrb_t[:], mu1_t[:], rb_t[:], ALU.mult)
                v.drain()
                S.inc(v.drain(), "bce", 1)
                S.inc(v.drain(), "pBfree", 1)

            blk.scalar(s_m2)
            blk.tensor(t_bcf)
            blk.vector(v_bcf)

        if not fused:
            def t_bc(t):
                S.wait(t, "m4")
                mm(t, pB[:, po + 512:po + 1024], onesr2[0:1, 0:128],
                   rows[0:1, 0:512], first=True)
                S.inc(_lt[0], "bcp", 1)
                S.inc(t.drain(), "pBfree", 1)

            def v_bc(v):
                S.wait(v, "bcp")
                v.tensor_copy(rb_t[:], pB[:, po + 512:po + 768])
                v.tensor_copy(mu1_t[:], pB[:, po + 768:po + 1024])
                v.drain()
                v.tensor_tensor(mrb_t[:], mu1_t[:], rb_t[:], ALU.mult)
                v.drain()
                S.inc(v.drain(), "bce", 1)
                S.inc(v.drain(), "pBfree", 1)

            blk.tensor(t_bc)
            blk.vector(v_bc)

        def v_u(v):
            S.wait(v, "bce")
            if ugate is not None:
                S.wait(v, ugate)
            for i in range(FT):
                v.tensor_tensor(scr2[:, i * QC:(i + 1) * QC],
                                x_bf[:, i * QC:(i + 1) * QC], rb_t[:],
                                ALU.mult)
            v.drain()
            if udst_f8:
                for i in range(FT):
                    v.tensor_tensor(u1_f8[:, i * QC:(i + 1) * QC],
                                    scr2[:, i * QC:(i + 1) * QC], mrb_t[:],
                                    ALU.subtract)
                v.drain()
                S.inc(v.drain(), "ur", 1)
            else:
                for i in range(FT):
                    v.tensor_tensor(u2_bf[:, i * QC:(i + 1) * QC],
                                    scr2[:, i * QC:(i + 1) * QC], mrb_t[:],
                                    ALU.subtract)
                v.drain()
                S.inc(v.drain(), "ur", 1)
            S.inc(v.drain(), "upre", 1)

        blk.vector(v_u)

    def emit_ln(tag, fused, udst_f8, xmark, xf, po=0, ugate=None):
        emit_ln_stats(xmark, xf, po)
        emit_ln_rest(fused, udst_f8, xf, po, ugate)

    def emit_xtr(cn):
        lb = land[cn % 2]
        xfn = x_fms[cn % 2]
        for w in range(2):
            def xw_t(t, w=w, cn=cn):
                if w == 0:
                    S.wait(t, "xinA" if cn % 2 == 0 else "xinB")
                    if cn == 0:
                        S.wait(t, "wrdy")
                        S.wait(t, "dcon")
                    else:
                        S.wait(t, "supd", lag=0)
                else:
                    S.wait(t, "xev")
                for i in range(FT):
                    tr(t, pB[:, 1024 + i * 128:1024 + (i + 1) * 128],
                       lb[:, w * D + i * 128:w * D + (i + 1) * 128], id_f[:])
                S.inc(_lt[0], "xw", 1)
                if w == 1:
                    S.inc(t.drain(), "xtrp", 1)

            def xw_s(s, w=w, cn=cn):
                S.wait(s, "xw")
                dst = xfn[:].rearrange("p (i t) -> p i t", i=FT)[
                    :, :, w * 128:(w + 1) * 128]
                src_ = pB[:, 1024:2048].rearrange("p (i t) -> p i t", i=FT)
                s.copy(dst, src_)
                s.drain()
                S.inc(s.drain(), "xev", 1)
                if w == 1:
                    S.inc(s.drain(), "xin1", 1)

            blk.tensor(xw_t)
            blk.scalar(xw_s)

    def emit_front2(cn, c0=False):
        xfn = x_fms[cn % 2]

        f1c_b = S.c.get("f1c", 0)
        f1ec_b = S.c.get("f1ec", 0)
        xin2c_b = S.c.get("xin2c", 0)
        x2c2_b = S.c.get("x2c2", 0)

        def ph_ffn1_t(t, cn=cn):
            S.wait(t, "ur")
            if c0:
                S.wait(t, "pBfree", lag=0)
            else:
                S.wait(t, "bce", lag=0)
                S.wait(t, "pBfree", lag=0)
            for ct in range(FT):
                for k2 in range(4):
                    mm(t, pB[:, ct * QC:(ct + 1) * QC],
                       bass.AP(wfft, 2 * k2 * D + ct * 128,
                               [[FT * D, 128], [D, 2], [1, 128]]),
                       bass.AP(u1_f8, 2 * k2 * QC,
                               [[FT * QC, 128], [QC, 2], [1, QC]]),
                       first=(ct % 2 == 0 and k2 == 0), last=(k2 == 3),
                       perf_mode=DR)
                if ct % 2 == 1:
                    S.inc(_lt[0], "f1c", 1)

        def ph_ffn1_ep(s, cn=cn, f1c_b=f1c_b):
            lt = land[cn % 2]
            for p in range(4):
                s.wait_ge(S._get("f1c"), f1c_b + p + 1)
                for ct in (2 * p, 2 * p + 1):
                    s.activation(lt[:, ct * QC:(ct + 1) * QC],
                                 pB[:, ct * QC:(ct + 1) * QC], AF.Identity,
                                 scale=0.5, bias=bffh[:, ct:ct + 1])
                s.drain()
                S.inc(s.drain(), "f1ec", 1)
            S.inc(s.drain(), "pBfree", 1)

        def ph_ffn1_v(v, cn=cn, f1ec_b=f1ec_b):
            lt = land[cn % 2]
            for p in range(4):
                v.wait_ge(S._get("f1ec"), f1ec_b + p + 1)
                v.tensor_tensor(xfn[:, p * 512:(p + 1) * 512],
                                xfn[:, p * 512:(p + 1) * 512],
                                lt[:, p * 512:(p + 1) * 512], ALU.add)
                v.drain()
                S.inc(v.drain(), "xin2c", 1)
            S.inc(v.drain(), "xin2", 1)

        def ln2_x2_s(s, xin2c_b=xin2c_b):
            S.wait(s, "upre", lag=0)
            S.wait(s, "xres", lag=0)
            S.wait(s, "stp", lag=0)
            for p in range(4):
                s.wait_ge(S._get("xin2c"), xin2c_b + p + 1)
                s.copy(x_bf[:, p * 512:(p + 1) * 512],
                       xfn[:, p * 512:(p + 1) * 512])
                s.drain()
                s.activation(scrA[:, p * 512:(p + 1) * 512],
                             x_bf[:, p * 512:(p + 1) * 512], AF.Square)
                s.drain()
                S.inc(s.drain(), "x2c2", 1)
            S.inc(s.drain(), "xbf", 1)
            S.inc(s.drain(), "x2", 1)

        def ln2_stats_t(t, x2c2_b=x2c2_b):
            for p in range(4):
                t.wait_ge(S._get("x2c2"), x2c2_b + p + 1)
                for k in (2 * p, 2 * p + 1):
                    mm(t, pB[0:1, 0:512], ones_c[:],
                       bass.AP(scrA, k * QC,
                               [[H * QS, 128], [FT * QC, 2], [1, QC]]),
                       first=(k == 0), last=(k == FT - 1))
            S.inc(_lt[0], "stp", 1)

        blk.tensor(ph_ffn1_t)
        blk.scalar(ph_ffn1_ep)
        blk.vector(ph_ffn1_v)
        blk.scalar(ln2_x2_s)
        blk.tensor(ln2_stats_t)

    XB = QC + 4
    for c in range(nch):
        tagc = str(c)
        xf = x_fms[c % 2]

        if c == 0 and on("xtr"):
            emit_xtr(0)
            if on("ln1"):
                emit_ln("1", fused=False, udst_f8=True, xmark="xin1",
                        xf=xf, po=1024, ugate="f2p")
            emit_front2(0, c0=True)
            emit_ln_rest(fused=True, udst_f8=False, xf=xf, po=0)

        if c + 1 < nch:
            def ph_load(y, c=c):
                S.wait(y, "st", lag=0)
                S.wait(y, "xtrp", lag=0)
                if c == 0:
                    S.wait(y, "wrdy")
                for tt in range(2):
                    r0 = (c + 1) * QC + tt * 128
                    S.inc(y.dma_start(
                        out=land[(c + 1) % 2][:, tt * D:(tt + 1) * D],
                        in_=x_h[r0:r0 + 128, :]),
                        "xinA" if (c + 1) % 2 == 0 else "xinB", 16)
            blk.sync(ph_load)

        if not on("ipx"):
            continue
        if not on("dtc"):
            continue
        def ph_dt_t(t):
            S.wait(t, "ur")
            S.wait(t, "pAfree", lag=0)
            for k in range(FT):
                mm(t, pA[0:H, 0:QC], wip(k, DIP - H, DIP),
                   xv(u2_bf)[:, k, :], first=(k == 0), last=(k == FT - 1))
            S.inc(_lt[0], "dtp", 1)

        blk.tensor(ph_dt_t)

        groups = [list(range(4 * g, min(4 * g + 4, CT))) for g in range(5)]
        ipxe_b = S.c.get("ipxe", 0)
        ipxp_b = S.c.get("ipxp", 0)

        def ph_ipx_t_a(t):
            S.wait(t, "ur")
            S.wait(t, "pBfree", lag=0)
            for g in (0, 1):
                base = (g % 2) * 1024
                for n, ct in enumerate(groups[g]):
                    for k in range(FT):
                        mm(t, pB[:, base + n * QC:base + (n + 1) * QC],
                           wip(k, DI + ct * 128, DI + (ct + 1) * 128),
                           xv(u2_bf)[:, k, :],
                           first=(n % 2 == 0 and k == 0),
                           last=(k == FT - 1))
                S.inc(_lt[0], "ipxp", 1)

        blk.tensor(ph_ipx_t_a)

        def ph_dt_s(s):
            S.wait(s, "dtp")
            s.activation(a_f[:], pA[0:H, 0:QC], AF.Abs, bias=dtb_c[:, 0:1])
            s.drain()
            s.activation(a_f[:], a_f[:], AF.Exp, scale=-1.0)
            s.drain()
            s.activation(a_f[:], a_f[:], AF.Ln, bias=1.0)
            s.drain()
            s.activation(dt_f[:], pA[0:H, 0:QC], AF.Relu, bias=dtb_c[:, 0:1])
            s.drain()
            S.inc(s.drain(), "dtsp", 1)

        def ph_dt_v(v):
            S.wait(v, "dtsp")
            v.tensor_tensor(dt_f[:], dt_f[:], a_f[:], ALU.add)
            v.drain()
            S.inc(v.drain(), "pAfree", 1)
            v.tensor_scalar(a_f[:], dt_f[:], aneg[:, 0:1], None, ALU.mult)
            v.drain()
            for sbi in range(NSUB):
                sl = slice(sbi * QS, (sbi + 1) * QS)
                v.tensor_tensor_scan(cum_f[:, sl], a_f[:, sl], a_f[:, sl],
                                     0.0, ALU.add, ALU.bypass)
            v.drain()
            S.inc(v.drain(), "cumr", 1)

        blk.scalar(ph_dt_s)
        blk.vector(ph_dt_v)

        def ph_xres_v(v, c=c):
            S.wait(v, "ur", lag=0)
            S.wait(v, "st", lag=0)
            lt = land[c % 2][:].rearrange("p (i t) -> p i t", i=FT)
            xb3 = x_bf.rearrange("p (i t) -> p i t", i=FT)
            v.tensor_tensor(lt, xb3, rep_outer(mu1_t[:], FT), ALU.subtract)
            v.drain()
            v.tensor_tensor(lt, lt, rep_outer(rb1_t[:], FT), ALU.mult)
            v.drain()
            v.tensor_tensor(xf[:], xf[:], land[c % 2][:], ALU.add)
            v.drain()
            S.inc(v.drain(), "xres", 1)

        blk.vector(ph_xres_v)

        def ph_chl_g(g):
            S.wait(g, "cumr")
            g.tensor_copy(cumhi[:], cum_f[:])
            g.drain()
            g.tensor_tensor(cumlo[:], cum_f[:], cumhi[:], ALU.subtract)
            g.drain()

        blk.gpsimd(ph_chl_g)

        def ph_cc_s(s):
            S.wait(s, "cumr")
            for sbi in range(NSUB):
                sl = slice(sbi * QS, (sbi + 1) * QS)
                s.activation(ccf[:, sl], cum_f[:, sl], AF.Exp, scale=-1.0,
                             bias=cum_f[:, (sbi + 1) * QS - 1:(sbi + 1) * QS])
            s.drain()
            S.inc(s.drain(), "ccfr", 1)

        blk.scalar(ph_cc_s)

        def ph_tt_t(t):
            S.wait(t, "ccfr")
            S.wait(t, "pAfree", lag=0)
            for sbi in range(NSUB):
                sl0 = sbi * QS
                tr(t, pA[:, sbi * 512 + 0:sbi * 512 + H],
                            dt_f[:, sl0:sl0 + QS], id_f[0:H, 0:H])
                tr(t, pA[:, sbi * 512 + 128:sbi * 512 + 128 + H],
                            cum_f[:, sl0:sl0 + QS], id_f[0:H, 0:H])
                tr(t, pA[:, sbi * 512 + 256:sbi * 512 + 256 + H],
                            ccf[:, sl0:sl0 + QS], id_f[0:H, 0:H])
                tr(t, pA[0:1, sbi * 512 + 384:sbi * 512 + 384 + H],
                            cum_f[:, sl0 + QS - 1:sl0 + QS], id_f[0:H, 0:H])
            S.inc(_lt[0], "dttp", 1)

        def ph_tt_v(v):
            S.wait(v, "dttp")
            for sbi in range(NSUB):
                hs = slice(sbi * H, (sbi + 1) * H)
                v.tensor_copy(dt_tm[:, hs], pA[:, sbi * 512 + 0:sbi * 512 + H])
                v.tensor_copy(cum_tm[:, hs],
                              pA[:, sbi * 512 + 128:sbi * 512 + 128 + H])
                v.tensor_copy(cc_tm[:, hs],
                              pA[:, sbi * 512 + 256:sbi * 512 + 256 + H])
                v.tensor_copy(rows[0:1, 512 + 32 * sbi:512 + 32 * sbi + H],
                              pA[0:1, sbi * 512 + 384:sbi * 512 + 384 + H])
            v.drain()
            S.inc(v.drain(), "dtte", 1)

        blk.tensor(ph_tt_t)
        blk.vector(ph_tt_v)

        def ph_cq_s(s):
            S.wait(s, "dtte")
            s.activation(cexp_tm[:], cum_tm[:], AF.Exp)
            for sbi in range(NSUB):
                s.activation(rows[0:1, 512 + 32 * sbi:512 + 32 * sbi + H],
                             rows[0:1, 512 + 32 * sbi:512 + 32 * sbi + H],
                             AF.Exp)
            s.drain()
            S.inc(s.drain(), "cqe", 1)

        def ph_cq_t(t):
            S.wait(t, "cqe")
            for sbi in range(NSUB):
                mm(t, pA[:, 1024 + sbi * 128:1024 + sbi * 128 + H],
                   onesr2[0:1, 0:128],
                   rows[0:1, 512 + 32 * sbi:512 + 32 * sbi + H],
                   first=True)
            S.inc(_lt[0], "cqp", 1)

        def ph_cq_v(v):
            S.wait(v, "cqp")
            for sbi in range(NSUB):
                v.tensor_copy(ccq_b[:, sbi * H:(sbi + 1) * H],
                              pA[:, 1024 + sbi * 128:1024 + sbi * 128 + H])
            v.drain()
            S.inc(v.drain(), "cqv", 1)
            S.inc(v.drain(), "pAfree", 1)

        blk.scalar(ph_cq_s)
        blk.tensor(ph_cq_t)
        blk.vector(ph_cq_v)

        def ph_halo_v(v, c=c):
            if c > 0:
                S.wait(v, "cvp", lag=0)
                v.tensor_copy(
                    xbcp[:].rearrange("p (i t) -> p i t", i=CT)[:, :, 0:3],
                    tl3[:].rearrange("p (i t) -> p i t", i=CT))
                v.drain()
            S.inc(v.drain(), "halo", 1)

        blk.vector(ph_halo_v)

        def ph_ipx_t(t, ipxe_b=ipxe_b):
            for g in (2, 3, 4):
                t.wait_ge(S._get("ipxe"), ipxe_b + g - 1)
                base = (g % 2) * 1024
                for n, ct in enumerate(groups[g]):
                    for k in range(FT):
                        mm(t, pB[:, base + n * QC:base + (n + 1) * QC],
                           wip(k, DI + ct * 128, DI + (ct + 1) * 128),
                           xv(u2_bf)[:, k, :],
                           first=(n % 2 == 0 and k == 0),
                           last=(k == FT - 1))
                S.inc(_lt[0], "ipxp", 1)

        def ph_ipx_s(s, ipxp_b=ipxp_b):
            S.wait(s, "halo")
            S.wait(s, "opjp", lag=0)
            for g, cts in enumerate(groups):
                s.wait_ge(S._get("ipxp"), ipxp_b + g + 1)
                base = (g % 2) * 1024
                dst = xbcp[:].rearrange("p (i t) -> p i t", i=CT)[
                    :, cts[0]:cts[0] + len(cts), 3:3 + QC]
                src_ = pB[:, base:base + len(cts) * QC].rearrange(
                    "p (i t) -> p i t", i=len(cts))
                s.copy(dst, src_)
                s.drain()
                S.inc(s.drain(), "ipxe", 1)

        blk.tensor(ph_ipx_t)
        blk.scalar(ph_ipx_s)

        if not on("z"):
            continue
        def ph_z_t(t):
            S.wait(t, "ur")
            S.wait(t, "pAfree", lag=0)
            for zb in range(4):
                for k in range(FT):
                    mm(t, pA[:, zb * 512:(zb + 1) * 512],
                       u2_bf[:, k * QC:k * QC + QS],
                       wip(k, zb * 512, (zb + 1) * 512),
                       first=(k == 0), last=False)
                mm(t, pA[:, zb * 512:(zb + 1) * 512], ones2r[0:1, 0:128],
                   badd_z[0:1, zb * 512:(zb + 1) * 512],
                   first=False, last=True)
            S.inc(_lt[0], "z0p", 1)
            S.wait(t, "ipxe", lag=0)
            S.wait(t, "dtte", lag=0)
            for zb in range(4):
                for k in range(FT):
                    mm(t, pB[:, zb * 512:(zb + 1) * 512],
                       u2_bf[:, k * QC + QS:k * QC + 2 * QS],
                       wip(k, zb * 512, (zb + 1) * 512),
                       first=(k == 0), last=False)
                mm(t, pB[:, zb * 512:(zb + 1) * 512], ones2r[0:1, 0:128],
                   badd_z[0:1, zb * 512:(zb + 1) * 512],
                   first=False, last=True)
            S.inc(_lt[0], "z1p", 1)

        blk.tensor(ph_z_t)

        def ph_zev_s(s):
            S.wait(s, "z0p")
            _load_set(s, SET_SILU)
            s.activation(z_si[:, 0:DI], pA[:, :], AF.Silu)
            s.drain()
            S.inc(s.drain(), "z0e", 1)
            S.inc(s.drain(), "pAfree", 1)
            S.wait(s, "convd", lag=-1)
            s.activation(cacc[:], cacc[:], AF.Silu, bias=0.0)
            s.drain()
            S.inc(s.drain(), "xcsr", 1)
            S.wait(s, "z1p")
            s.activation(z_si[:, DI:2 * DI], pB[:, :], AF.Silu)
            s.drain()
            S.inc(s.drain(), "z1e", 1)
            S.inc(s.drain(), "pBfree", 1)
            _load_set(s, SET_EXP_LN)
            S.inc(s.drain(), "sexp", 1)

        blk.scalar(ph_zev_s)

        if not on("conv"):
            continue
        def ph_conv_v(v, ipxe_b=ipxe_b):
            S.wait(v, "halo")
            xb = xbcp[:].rearrange("p (i t) -> p i t", i=CT)
            ca = cacc[:].rearrange("p (i t) -> p i t", i=CT)
            for g, cts in enumerate(groups):
                v.wait_ge(S._get("ipxe"), ipxe_b + g + 1)
                for k in range(4):
                    for ct in cts:
                        in1 = (rep_inner(convb[:, ct:ct + 1], QC)[:, 0, :]
                               if k == 0 else ca[:, ct, :])
                        v.scalar_tensor_tensor(
                            ca[:, ct, :], xb[:, ct, k:k + QC],
                            convw[:, ct * 4 + k:ct * 4 + k + 1], in1,
                            ALU.mult, ALU.add)
                    v.drain()
            v.tensor_copy(tl3[:].rearrange("p (i t) -> p i t", i=CT),
                          xbcp[:].rearrange("p (i t) -> p i t", i=CT)
                          [:, :, QC:QC + 3])
            v.drain()
            S.inc(v.drain(), "convd", 1)
            S.inc(v.drain(), "cvp", 1)

        blk.vector(ph_conv_v)

        if not on("silu"):
            continue
        if not on("ssd"):
            continue
        Bts = [cacc[:, 16 * QC + i * QS:16 * QC + (i + 1) * QS]
               for i in range(NSUB)]
        Cts = [cacc[:, 17 * QC + i * QS:17 * QC + (i + 1) * QS]
               for i in range(NSUB)]

        def emit_cfl(sbi, extra=()):
            def ssd_cfl(g, sbi=sbi, extra=tuple(extra)):
                S.wait(g, "dtte")
                for nm in extra:
                    S.wait(g, nm, lag=0)
                for half in range(2):
                    hsl = slice(half * 16, (half + 1) * 16)
                    S.inc(g.dma_start(
                        out=scrA[0:1, half * DI:half * DI + 2048],
                        in_=cumhi[hsl, sbi * QS:(sbi + 1) * QS]), "cfl", 16)
                    S.inc(g.dma_start(
                        out=scrA[1:2, half * DI:half * DI + 2048],
                        in_=cumlo[hsl, sbi * QS:(sbi + 1) * QS]), "cfl", 16)
            blk.gpsimd(ssd_cfl)

        def mk_mm(t, sbi, half, pp):
            frows = scrA[0:2, half * DI:half * DI + 2048]
            tsl = slice(sbi * QS, (sbi + 1) * QS)
            for q in range(4):
                h0 = half * 16 + q * 4
                mm(t, pp[:, q * 512:(q + 1) * 512], ones2r[0:2, 0:128],
                   frows[0:2, q * 512:(q + 1) * 512], first=True, last=False)
                mm(t, pp[:, q * 512:(q + 1) * 512], trin_b[:],
                   rep_outer(id_b[:], 4), first=False, last=False)
                mm(t, pp[:, q * 512:(q + 1) * 512], cum_f[0:H, tsl],
                   rep_inner(neg32f[:, h0:h0 + 4], 128),
                   first=False, last=True)
            S.inc(_lt[0], "mkp%d" % sbi, 1)

        def mk_exp(s, sbi, half, pp):
            S.wait(s, "mkp%d" % sbi, lag=0)
            s.activation(scrA[:, half * DI:(half + 1) * DI], pp[:, :], AF.Exp)
            s.drain()
            S.inc(s.drain(), "mke%d" % sbi, 1)

        def mk0_t(t):
            S.wait(t, "cfl", lag=0)
            S.wait(t, "z0e", lag=0)
            mk_mm(t, 0, 0, pA)
            S.wait(t, "z1e", lag=0)
            mk_mm(t, 0, 1, pB)

        def mk0_s(s):
            S.wait(s, "sexp")
            S.wait(s, "stp", lag=0)
            S.wait(s, "upre", lag=0)
            S.wait(s, "xres", lag=0)
            mk_exp(s, 0, 0, pA)
            mk_exp(s, 0, 1, pB)

        emit_cfl(0, extra=("intrp", "x2", "stp", "upre", "xres"))
        blk.tensor(mk0_t)
        blk.scalar(mk0_s)

        def emit_sub(sbi):
            hs = slice(sbi * H, (sbi + 1) * H)
            Bt, Ctl = Bts[sbi], Cts[sbi]
            mk = "mke%d" % sbi

            def t1a(t, sbi=sbi, Bt=Bt, Ctl=Ctl, mk=mk):
                if sbi == 0:
                    S.wait(t, "xcsr")
                    S.wait(t, mk, lag=1)
                else:
                    S.wait(t, "yfme", lag=0)
                mm(t, pA[:, 0:QS], Bt, Ctl, first=True)
                S.inc(_lt[0], "scp", 1)
                mm(t, pA[:, 512:512 + QS], Bt, id_b[:], first=True)
                S.inc(_lt[0], "btp", 1)
                if sbi == 0:
                    S.wait(t, mk, lag=0)
                    S.wait(t, "z1e", lag=0)
                else:
                    S.wait(t, "supd", lag=0)
                    S.wait(t, "mke1", lag=0)
                for i in range(DIT):
                    mm(t, pB[:, i * 128:(i + 1) * 128],
                       cacc[:, i * QC + sbi * QS:i * QC + (sbi + 1) * QS],
                       id_b[:], first=(i % 4 == 0))
                S.inc(_lt[0], "xsp", 1)

            def v1a(v):
                S.wait(v, "scp")
                v.tensor_tensor(stri[:, 0:QS], pA[:, 0:QS], tri_b[:],
                                ALU.mult)
                v.drain()
                S.inc(v.drain(), "strir", 1)
                S.wait(v, "btp")
                v.tensor_copy(btm[:], pA[:, 512:512 + QS])
                v.drain()
                S.inc(v.drain(), "btme", 1)

            def v2w(v, mk=mk):
                S.wait(v, mk, lag=0)
                S.wait(v, "strir", lag=0)
                v.tensor_tensor(scrA[:].rearrange("p (h i) -> p h i", h=H),
                                scrA[:].rearrange("p (h i) -> p h i", h=H),
                                rep_outer(stri[:, 0:QS], H), ALU.mult)
                v.drain()
                S.inc(v.drain(), "wres", 1)

            def v1b(v, hs=hs):
                S.wait(v, "xsp")
                v.tensor_tensor(
                    xdt[:].rearrange("p (h e) -> p h e", h=H),
                    pB[:].rearrange("p (h e) -> p h e", h=H),
                    rep_inner(dt_tm[:, hs], HD), ALU.mult)
                v.drain()
                S.inc(v.drain(), "xdtr", 1)

            def t4(t, sbi=sbi, Ctl=Ctl):
                S.wait(t, "strir", lag=0)
                S.wait(t, "btme", lag=0)
                S.wait(t, "supd", lag=0)
                for b in range(4):
                    mm(t, pA[:, b * 512:(b + 1) * 512], Ctl,
                       s_bf[:, b * 512:(b + 1) * 512], first=True)
                S.inc(_lt[0], "intp", 1)
                S.wait(t, "wres", lag=0)
                S.wait(t, "xdtr", lag=0)
                for h in range(H):
                    mm(t, pB[:, h * HD:(h + 1) * HD],
                       scrA[:, h * 128:(h + 1) * 128],
                       xdt[:, h * HD:(h + 1) * HD],
                       first=(h % 8 == 0), last=False)
                for i in range(DIT):
                    mm(t, pB[:, i * 128:(i + 1) * 128],
                       cacc[:, i * QC + sbi * QS:i * QC + (sbi + 1) * QS],
                       ddiag[:, i * 128:(i + 1) * 128],
                       first=False, last=(i == DIT - 1))
                S.inc(_lt[0], "intrp", 1)

            def v4(v, sbi=sbi, hs=hs):
                S.wait(v, "intp")
                v.tensor_tensor(
                    scr2[:].rearrange("p (h e) -> p h e", h=H),
                    pA[:].rearrange("p (h e) -> p h e", h=H),
                    rep_inner(cexp_tm[:, hs], HD), ALU.mult)
                v.drain()
                S.inc(v.drain(), "s1r", 1)
                S.wait(v, "intrp")
                v.tensor_tensor(y_bf[:], pB[:, :], scr2[:], ALU.add)
                v.drain()
                S.inc(v.drain(), "yr", 1)
                v.tensor_tensor(scr2[:], y_bf[:],
                                z_si[:, sbi * DI:(sbi + 1) * DI], ALU.mult)
                v.drain()
                S.inc(v.drain(), "ygr", 1)
                v.tensor_tensor(
                    xdt[:].rearrange("p (h e) -> p h e", h=H),
                    xdt[:].rearrange("p (h e) -> p h e", h=H),
                    rep_inner(cc_tm[:, hs], HD), ALU.mult)
                v.drain()
                S.inc(v.drain(), "xdt2r", 1)

            def s5(s, c=c):
                S.wait(s, "ygr")
                S.wait(s, "xres", lag=0)
                s.activation(land[c % 2][:, 0:DI], scr2[:], AF.Square,
                             accum_out=ssq[:, 0:1])
                s.drain()
                s.activation(rms_c[:], ssq[:], AF.Ln, scale=1.0 / DI,
                             bias=EPS)
                s.drain()
                s.activation(rms_c[:], rms_c[:], AF.Exp, scale=-0.5)
                s.drain()
                s.activation(y_bf[:], scr2[:], AF.Identity,
                             scale=rms_c[:, 0:1])
                s.drain()
                S.inc(s.drain(), "ynr", 1)

            def t6a(t, sbi=sbi):
                S.wait(t, "ynr")
                S.wait(t, "s1r", lag=0)
                if sbi == 0:
                    S.wait(t, "mke1", lag=0)
                for i in range(DIT):
                    mm(t, pA[:, i * 128:(i + 1) * 128],
                       y_bf[:, i * 128:(i + 1) * 128], id_b[:],
                       first=(i % 4 == 0))
                S.inc(_lt[0], "ytp", 1)

            def t6b(t, sbi=sbi):
                S.wait(t, "xdt2r", lag=0)
                S.wait(t, "yr", lag=0)
                for b in range(4):
                    mm(t, pB[:, b * 512:(b + 1) * 512], btm[:],
                       xdt[:, b * 512:(b + 1) * 512], first=True)
                S.inc(_lt[0], "updp", 1)

            def s6(s, sbi=sbi):
                S.wait(s, "ytp")
                s.copy(yfm.rearrange("p (i t) -> p i t", i=DIT)
                       [:, :, sbi * QS:(sbi + 1) * QS],
                       pA[:].rearrange("p (i t) -> p i t", i=DIT))
                s.drain()
                S.inc(s.drain(), "yfme", 1)
                S.inc(s.drain(), "pAfree", 1)

            def v7a(v, hs=hs):
                v.tensor_tensor(
                    sdec[:].rearrange("p (h e) -> p h e", h=H),
                    s_bf[:].rearrange("p (h e) -> p h e", h=H),
                    rep_inner(ccq_b[:, hs], HD), ALU.mult)
                v.drain()
                S.inc(v.drain(), "sdcr", 1)

            def v7b(v):
                S.wait(v, "updp")
                v.tensor_tensor(s_bf[:], sdec[:], pB[:, :], ALU.add)
                v.drain()
                S.inc(v.drain(), "supd", 1)
                S.inc(v.drain(), "pBfree", 1)

            return t1a, v1a, v2w, v1b, t4, v4, s5, t6a, t6b, s6, v7a, v7b

        s0 = emit_sub(0)
        s1_ = emit_sub(1)

        blk.tensor(s0[0])
        blk.vector(s0[1])
        blk.vector(s0[2])
        blk.vector(s0[3])
        blk.tensor(s0[4])
        emit_cfl(1, extra=("intrp",))
        blk.vector(s0[5])
        def mk1_t0(t):
            S.wait(t, "cfl", lag=0)
            S.wait(t, "s1r", lag=0)
            mk_mm(t, 1, 0, pA)
        blk.tensor(mk1_t0)
        blk.scalar(s0[6])
        def mk1_s0(s):
            mk_exp(s, 1, 0, pA)
        blk.scalar(mk1_s0)
        blk.tensor(s0[8])
        blk.tensor(s0[7])
        blk.vector(s0[10])
        blk.vector(s0[11])
        def mk1_t1(t):
            S.wait(t, "supd", lag=0)
            mk_mm(t, 1, 1, pB)
        blk.tensor(mk1_t1)
        blk.scalar(s0[9])
        def mk1_s1(s):
            mk_exp(s, 1, 1, pB)
        blk.scalar(mk1_s1)

        blk.tensor(s1_[0])
        blk.vector(s1_[1])
        blk.vector(s1_[2])
        blk.vector(s1_[3])
        blk.tensor(s1_[4])
        blk.vector(s1_[5])
        blk.scalar(s1_[6])
        blk.tensor(s1_[7])
        blk.tensor(s1_[8])
        blk.scalar(s1_[9])
        blk.vector(s1_[10])
        blk.vector(s1_[11])

        opjc_b = S.c.get("opjc", 0)
        xin3c_b = S.c.get("xin3c", 0)
        x2c_b = S.c.get("x2c", 0)

        def ph_opj_t(t):
            S.wait(t, "yfme")
            S.wait(t, "pAfree", lag=0)
            for ct in range(FT):
                for k in range(DIT):
                    mm(t, pA[:, ct * QC:(ct + 1) * QC],
                       wopd(k, ct * 128, (ct + 1) * 128),
                       yfm[:, k * QC:(k + 1) * QC],
                       first=(ct % 2 == 0 and k == 0),
                       last=(k == DIT - 1))
                if ct % 2 == 1:
                    S.inc(_lt[0], "opjc", 1)
            S.inc(t.drain(), "opjp", 1)

        def ph_opj_v(v, opjc_b=opjc_b):
            S.wait(v, "xres", lag=0)
            for p in range(4):
                v.wait_ge(S._get("opjc"), opjc_b + p + 1)
                v.tensor_tensor(xf[:, p * 512:(p + 1) * 512],
                                xf[:, p * 512:(p + 1) * 512],
                                pA[:, p * 512:(p + 1) * 512], ALU.add)
                v.drain()
                S.inc(v.drain(), "xin3c", 1)
            S.inc(v.drain(), "xin3", 1)
            S.inc(v.drain(), "pAfree", 1)

        blk.tensor(ph_opj_t)
        blk.vector(ph_opj_v)

        if not on("ln3"):
            continue
        def ln3_x2_s(s, xin3c_b=xin3c_b):
            S.wait(s, "upre", lag=0)
            S.wait(s, "xres", lag=0)
            S.wait(s, "stp", lag=0)
            for p in range(4):
                s.wait_ge(S._get("xin3c"), xin3c_b + p + 1)
                s.copy(x_bf[:, p * 512:(p + 1) * 512],
                       xf[:, p * 512:(p + 1) * 512])
                s.drain()
                s.activation(scrA[:, p * 512:(p + 1) * 512],
                             x_bf[:, p * 512:(p + 1) * 512], AF.Square)
                s.drain()
                S.inc(s.drain(), "x2c", 1)
            S.inc(s.drain(), "xbf", 1)
            S.inc(s.drain(), "x2", 1)

        def ln3_stats_t(t, x2c_b=x2c_b):
            S.wait(t, "pBfree", lag=0)
            for p in range(4):
                t.wait_ge(S._get("x2c"), x2c_b + p + 1)
                for k in (2 * p, 2 * p + 1):
                    mm(t, pB[0:1, 0:512], ones_c[:],
                       bass.AP(scrA, k * QC,
                               [[H * QS, 128], [FT * QC, 2], [1, QC]]),
                       first=(k == 0), last=(k == FT - 1))
            S.inc(_lt[0], "stp", 1)

        blk.scalar(ln3_x2_s)
        blk.tensor(ln3_stats_t)
        emit_ln_rest(fused=False, udst_f8=True, xf=xf, po=0)
        pipe = c + 1 < nch and on("xtr") and on("ln1")
        if pipe:
            emit_xtr(c + 1)

        if not on("ffn2"):
            continue
        def ph_ffn2_t(t):
            S.wait(t, "ur")
            S.wait(t, "pAfree", lag=0)
            ffn_mms(t, pA)
            S.inc(_lt[0], "f2p", 1)

        blk.tensor(ph_ffn2_t)
        if pipe:
            emit_ln_stats("xin1", x_fms[(c + 1) % 2], po=1024)

        def ph_ffn2_v_s(s, c=c):
            S.wait(s, "f2p")
            lt = land[c % 2]
            for ct in range(FT):
                s.activation(lt[:, ct * QC:(ct + 1) * QC],
                             pA[:, ct * QC:(ct + 1) * QC], AF.Identity,
                             scale=0.5, bias=bffh[:, ct:ct + 1])
            s.drain()
            S.inc(s.drain(), "f2pE", 1)
            S.inc(s.drain(), "pAfree", 1)

        def ph_ffn2_v(v, c=c):
            S.wait(v, "f2pE")
            lt = land[c % 2]
            v.tensor_tensor(xf[:], xf[:], lt[:, 0:FT * QC], ALU.add)
            v.drain()
            S.inc(v.drain(), "x3r", 1)

        blk.scalar(ph_ffn2_v_s)
        blk.vector(ph_ffn2_v)
        if pipe:
            emit_ln_rest(fused=False, udst_f8=True, xf=x_fms[(c + 1) % 2],
                         po=1024, ugate="f2p")
            emit_front2(c + 1)

        if not on("out"):
            continue
        if pipe:
            emit_ln_rest(fused=True, udst_f8=False, xf=x_fms[(c + 1) % 2],
                         po=0)
        def ph_out_t(t):
            S.wait(t, "x3r")
            S.wait(t, "pAfree", lag=0)
            for tt in range(2):
                for i in range(FT):
                    tr(t, pA[:, tt * D + i * 128:tt * D + (i + 1) * 128],
                                xf[:, i * QC + tt * 128:
                                   i * QC + tt * 128 + 128], id_f[:])
            S.inc(_lt[0], "outp", 1)

        def ph_out_v(s, c=c):
            S.wait(s, "outp")
            s.copy(land[c % 2][:, :], pA[:, :])
            s.drain()
            S.inc(s.drain(), "oute", 1)
            S.inc(s.drain(), "pAfree", 1)

        blk.tensor(ph_out_t)
        blk.scalar(ph_out_v)

        def ph_store_y(y, c=c):
            S.wait(y, "oute")
            for tt in range(2):
                r0 = c * QC + tt * 128
                S.inc(y.dma_start(out=out_h[r0:r0 + 128, :],
                                  in_=land[c % 2][:, tt * D:(tt + 1) * D]),
                      "st", 16)
            if c == nch - 1:
                S.wait(y, "st")

        blk.sync(ph_store_y)

    es.close()
    nc.finalize()
    return nc


def kernel(L0_attn_x, L0_attn_ln_w, L0_attn_ln_b, L0_attn_in_proj_w,
           L0_attn_conv_w, L0_attn_conv_b, L0_attn_A_log, L0_attn_D,
           L0_attn_dt_bias, L0_attn_gamma, L0_attn_out_proj_w,
           L0_ffn_weight, L0_ffn_bias):
    B, T, _ = L0_attn_x.shape
    nc = build_nc(T)
    common = {
        "w_ip": np.ascontiguousarray(L0_attn_in_proj_w),
        "w_op": np.ascontiguousarray(L0_attn_out_proj_w),
        "w_ff": np.ascontiguousarray(L0_ffn_weight),
        "b_ff": np.ascontiguousarray(L0_ffn_bias),
        "ln_w": np.ascontiguousarray(L0_attn_ln_w),
        "ln_b": np.ascontiguousarray(L0_attn_ln_b),
        "conv_w": np.ascontiguousarray(L0_attn_conv_w),
        "conv_b": np.ascontiguousarray(L0_attn_conv_b),
        "a_log": np.ascontiguousarray(L0_attn_A_log),
        "d_vec": np.ascontiguousarray(L0_attn_D),
        "dt_bias": np.ascontiguousarray(L0_attn_dt_bias),
        "gamma": np.ascontiguousarray(L0_attn_gamma),
    }
    in_maps = [dict(common, x=np.ascontiguousarray(L0_attn_x[b]))
               for b in range(B)]
    res = run_bass_kernel_spmd(nc, in_maps, core_ids=list(range(B)))
    return np.stack([r["out"] for r in res.results], axis=0)


# version 32
# speedup vs baseline: 1.0324x; 1.0324x over previous
import os
import numpy as np
from contextlib import ExitStack

import concourse.bass as bass
import concourse.mybir as mybir
from concourse.bass_utils import run_bass_kernel_spmd

F32 = mybir.dt.float32
BF16 = mybir.dt.bfloat16
FP8 = mybir.dt.float8e4
AF = mybir.ActivationFunctionType
ALU = mybir.AluOpType

D, FT, DI, DIT = 1024, 8, 2048, 16
CD, CT, DIP, H, HD = 2304, 18, 4384, 32, 64
QC, QS = 256, 128
NSUB = QC // QS
EPS = 1e-5

_SET_IDS = {}


def _act_set_ids(arch):
    if arch not in _SET_IDS:
        from concourse.hw_specs import get_activation_tables
        tabs = list(get_activation_tables(arch).values())
        exp_ln = next(i for i, fns in enumerate(tabs)
                      if AF.Exp in fns and AF.Ln in fns)
        silu = next(i for i, fns in enumerate(tabs) if AF.Silu in fns)
        _SET_IDS[arch] = (exp_ln, silu)
    return _SET_IDS[arch]


def _load_set(scalar, set_id):
    return scalar.add_instruction(
        mybir.InstLoadActFuncSet(
            name=scalar.bass.get_next_instruction_name(),
            act_func_set_id=set_id, ins=[], outs=[]))


def rep_outer(ap2d, outer):
    P, n = ap2d.shape
    return ap2d[:, None, :].broadcast_to((P, outer, n))


def rep_inner(ap2d, inner):
    P, n = ap2d.shape
    return ap2d[:, :, None].broadcast_to((P, n, inner))


class Sems:
    def __init__(self, nc, es):
        self.nc, self.es, self.h, self.c = nc, es, {}, {}

    def _get(self, name):
        if name not in self.h:
            self.h[name] = self.es.enter_context(self.nc.semaphore(name))
            self.c[name] = 0
        return self.h[name]

    def inc(self, instr, name, n=1):
        instr.then_inc(self._get(name), n)
        self.c[name] += n
        return instr

    def wait(self, eng, name, lag=0):
        eng.wait_ge(self._get(name), self.c[name] - lag)


_STAGES = ["pro", "xtr", "ln1", "ffn1", "ln2", "ipx", "dtc", "z", "conv",
            "silu", "ssd", "opj", "ln3", "ffn2", "out"]


def build_nc(T=2048):
    stage = os.environ.get("KSTAGE", "out")
    slvl = _STAGES.index(stage)

    def on(lvl):
        return slvl >= _STAGES.index(lvl)

    nch = T // QC
    nc = bass.Bass("TRN2", target_bir_lowering=False,
                   dynamic_dma_scratch_size=4096)
    SET_EXP_LN, SET_SILU = _act_set_ids(nc.m.arch)

    x_h = nc.dram_tensor("x", [T, D], F32, kind="ExternalInput").ap()
    out_h = nc.dram_tensor("out", [T, D], F32, kind="ExternalOutput").ap()
    wip_h = nc.dram_tensor("w_ip", [DIP, D], F32, kind="ExternalInput").ap()
    wop_h = nc.dram_tensor("w_op", [D, DI], F32, kind="ExternalInput").ap()
    wff_h = nc.dram_tensor("w_ff", [D, D], F32, kind="ExternalInput").ap()
    bff_h = nc.dram_tensor("b_ff", [D], F32, kind="ExternalInput").ap()
    lnw_h = nc.dram_tensor("ln_w", [D], F32, kind="ExternalInput").ap()
    lnb_h = nc.dram_tensor("ln_b", [D], F32, kind="ExternalInput").ap()
    cw_h = nc.dram_tensor("conv_w", [CD, 4], F32, kind="ExternalInput").ap()
    cb_h = nc.dram_tensor("conv_b", [CD], F32, kind="ExternalInput").ap()
    al_h = nc.dram_tensor("a_log", [H], F32, kind="ExternalInput").ap()
    dv_h = nc.dram_tensor("d_vec", [H], F32, kind="ExternalInput").ap()
    dtb_h = nc.dram_tensor("dt_bias", [H], F32, kind="ExternalInput").ap()
    gam_h = nc.dram_tensor("gamma", [DI], F32, kind="ExternalInput").ap()

    es = ExitStack()
    sb = lambda n, sh, dt=F32: es.enter_context(nc.sbuf_tensor(n, sh, dt))
    S = Sems(nc, es)

    wipt = sb("wipt", [128, FT * DIP], BF16)
    wopt = sb("wopt", [128, DIT * D], BF16)
    wfft = sb("wfft", [128, FT * D], FP8)
    convw = sb("convw", [128, CT * 4])
    convb = sb("convb", [128, CT])
    lnw_f = sb("lnw_f", [128, FT])
    lnb_f = sb("lnb_f", [128, FT])
    bffh = sb("bffh", [128, FT])
    gam_f = sb("gam_f", [128, DIT])
    dtb_c = sb("dtb_c", [H, 1])
    aneg = sb("aneg", [H, 1])
    dvall = sb("dvall", [128, DIT])
    ddiag = sb("ddiag", [128, DIT * 128], BF16)
    tri_b = sb("tri_b", [128, 128], BF16)
    id_b = sb("id_b", [128, 128], BF16)
    id_f = sb("id_f", [128, 128], F32)
    onesr2 = sb("onesr2", [1, 128])
    ones2r = sb("ones2r", [2, 128], BF16)
    ones_c = sb("ones_c", [128, 1], BF16)
    land = [sb(f"land{i}", [128, 2 * D]) for i in range(2)]
    x_fm = sb("x_fm", [128, FT * QC])
    x_fm2 = sb("x_fm2", [128, FT * QC])
    x_fms = [x_fm, x_fm2]
    scrA = sb("scrA", [128, H * QS], BF16)
    x_bf = scrA[:, FT * QC:2 * FT * QC]
    u1_f8 = sb("u1_f8", [128, FT * QC], FP8)
    scr2 = sb("scr2", [128, DI], BF16)
    u2_bf = sb("u2_bf", [128, FT * QC], BF16)
    sdec = u2_bf
    rb_t = sb("rb_t", [128, QC], BF16)
    mrb_t = sb("mrb_t", [128, QC], BF16)
    rows = sb("rows", [1, 768])
    xbcp = sb("xbcp", [128, CT * (QC + 4)], BF16)
    yfm = xbcp[:, 0:DIT * QC]
    cacc = sb("cacc", [128, CT * QC], BF16)
    z_si = sb("z_si", [128, NSUB * DI], BF16)
    dt_f = sb("dt_f", [H, QC])
    a_f = sb("a_f", [H, QC])
    cum_f = sb("cum_f", [H, QC])
    ccf = a_f
    dt_tm = sb("dt_tm", [128, NSUB * H], BF16)
    cum_tm = sb("cum_tm", [128, NSUB * H])
    cexp_tm = sb("cexp_tm", [128, NSUB * H], BF16)
    cc_tm = sb("cc_tm", [128, NSUB * H], BF16)
    ccq_b = sb("ccq_b", [128, NSUB * H], BF16)
    stri = sb("stri", [128, QS], BF16)
    xdt = sb("xdt", [128, DI], BF16)
    btm = sb("btm", [128, 128], BF16)
    s_bf = sb("s_bf", [128, DI], BF16)
    y_bf = sb("y_bf", [128, DI], BF16)
    ssq = sb("ssq", [128, 1])
    rms_c = sb("rms_c", [128, 1])
    tl3 = sb("tl3", [128, CT * 3], BF16)
    badd_z = sb("badd_z", [1, DI], BF16)
    badd_fm = sb("badd_fm", [128, 19])
    lnb_b = sb("lnb_b", [128, FT], BF16)
    trin_b = sb("trin_b", [128, 128], BF16)
    rb1_t = sb("rb1_t", [128, QC], BF16)
    mu1_t = sb("mu1_t", [128, QC], BF16)
    cumhi = sb("cumhi", [H, QC], BF16)
    cumlo = sb("cumlo", [H, QC], BF16)
    neg_id32 = sb("neg_id32", [32, 32], BF16)
    cst = sb("cst", [128, 16])
    _cvals = [EPS, 1.0, -0.5, -1.0, 1.0 / DI, 0.5, 448.0, -448.0,
              EPS * EPS]
    for _i, _v in enumerate(_cvals):
        nc.const_aps.register(F32, _v, cst[:, _i:_i + 1])
    pA = es.enter_context(nc.psum_tensor("pA", [128, 2048], F32))
    pB = es.enter_context(nc.psum_tensor("pB", [128, 2048], F32))

    blk = es.enter_context(nc.Block())

    _lt = [None]

    def mm(t, out, lhsT, rhs, first, last=True, perf_mode=None):
        i = t.matmul(out, lhsT, rhs, start=first, stop=last,
                     skip_group_check=True, perf_mode=perf_mode)
        _lt[0] = i
        return i

    DR = mybir.MatmulPerfMode.DoubleRow

    def ffn_mms(t, pp):
        for ct in range(FT):
            for k2 in range(4):
                mm(t, pp[:, ct * QC:(ct + 1) * QC],
                   bass.AP(wfft, 2 * k2 * D + ct * 128,
                           [[FT * D, 128], [D, 2], [1, 128]]),
                   bass.AP(u1_f8, 2 * k2 * QC,
                           [[FT * QC, 128], [QC, 2], [1, QC]]),
                   first=(ct % 2 == 0 and k2 == 0), last=(k2 == 3),
                   perf_mode=DR)

    def tr(t, out, in_, ident):
        i = t.transpose(out, in_, ident)
        _lt[0] = i
        return i

    wip = lambda i, a, b: wipt[:, i * DIP + a: i * DIP + b]
    wopd = lambda dk, a, b: wopt[:, dk * D + a: dk * D + b]
    wff = lambda i, a, b: wfft[:, i * D + a: i * D + b]

    def p_sync0(y):
        def dma(dst, src, tag="wsm"):
            S.inc(y.dma_start(out=dst, in_=src,
                              allow_slow_non_contiguous=True), tag, 16)
        dma(lnw_f[:], bass.AP(lnw_h.tensor, 0, [[1, 128], [128, FT]]))
        dma(lnb_f[:], bass.AP(lnb_h.tensor, 0, [[1, 128], [128, FT]]))
        dma(bffh[:], bass.AP(bff_h.tensor, 0, [[1, 128], [128, FT]]))
        dma(gam_f[:], bass.AP(gam_h.tensor, 0, [[1, 128], [128, DIT]]))
        dma(convb[:], bass.AP(cb_h.tensor, 0, [[1, 128], [128, CT]]))
        dma(convw[:], bass.AP(cw_h.tensor, 0, [[4, 128], [512, CT], [1, 4]]))
        dma(dtb_c[:], bass.AP(dtb_h.tensor, 0, [[1, H], [1, 1]]))
        dma(aneg[:], bass.AP(al_h.tensor, 0, [[1, H], [1, 1]]))
        for i in range(DIT):
            dma(dvall[0:64, i:i + 1],
                bass.AP(dv_h.tensor, 2 * i, [[0, 64], [1, 1]]))
            dma(dvall[64:128, i:i + 1],
                bass.AP(dv_h.tensor, 2 * i + 1, [[0, 64], [1, 1]]))
        for tt in range(2):
            dma(land[0][:, tt * D:(tt + 1) * D],
                x_h[tt * 128:(tt + 1) * 128, :], "xinA")

    blk.sync(p_sync0)

    def p_gp(g):
        f = land[1][:, 1024:1152]
        g.memset(f, 1.0)
        g.drain()
        g.affine_select(f, f, pattern=[[1, 128]], compare_op=ALU.is_ge,
                        fill=0.0, base=0, channel_multiplier=-1)
        g.drain()
        g.tensor_copy(tri_b[:], f)
        g.drain()
        g.affine_select(f, f, pattern=[[-1, 128]], compare_op=ALU.is_ge,
                        fill=0.0, base=0, channel_multiplier=1)
        g.drain()
        g.tensor_copy(id_b[:], f)
        g.tensor_copy(id_f[:], f)
        g.drain()
        g.memset(f, 0.0)
        g.drain()
        g.affine_select(f, f, pattern=[[-1, 128]], compare_op=ALU.is_ge,
                        fill=-1.0e6, base=0, channel_multiplier=1)
        g.drain()
        g.tensor_copy(trin_b[:], f)
        for _i, _v in enumerate(_cvals):
            g.memset(cst[:, _i:_i + 1], _v)
        g.memset(onesr2[0:1, 0:128], 1.0)
        g.memset(ones2r[0:2, 0:128], 1.0)
        g.memset(neg_id32[0:32, 0:32], -1.0)
        g.drain()
        g.affine_select(neg_id32[:], neg_id32[:], pattern=[[1, 32]],
                        compare_op=ALU.is_ge, fill=0.0, base=0,
                        channel_multiplier=-1)
        g.drain()
        g.affine_select(neg_id32[:], neg_id32[:], pattern=[[-1, 32]],
                        compare_op=ALU.is_ge, fill=0.0, base=0,
                        channel_multiplier=1)
        g.drain()
        g.memset(ones_c[:], 1.0 / D)
        g.memset(s_bf[:], 0.0)
        for ct in range(CT):
            g.memset(xbcp[:, ct * (QC + 4):ct * (QC + 4) + 3], 0.0)
        g.drain()
        S.inc(g.drain(), "gpc", 1)

    blk.gpsimd(p_gp)

    def p_v0(v):
        S.wait(v, "gpc")
        S.wait(v, "wsm")
        v.tensor_tensor(
            ddiag[:].rearrange("p (i c) -> p i c", i=DIT),
            rep_outer(id_b[:], DIT),
            rep_inner(dvall[:], 128), ALU.mult)
        v.drain()
        S.inc(v.drain(), "dcon", 1)

    blk.vector(p_v0)

    def p_s0(s):
        _load_set(s, SET_EXP_LN)
        S.wait(s, "wsm")
        s.activation(aneg[:], aneg[:], AF.Exp)
        s.drain()
        s.mul(aneg[:], aneg[:], -1.0)
        s.mul(bffh[:], bffh[:], 0.5)
        s.drain()
        sc1 = land[1][:, 1152:1152 + FT]
        sc2 = land[1][:, 1152 + FT:1152 + 2 * FT]
        s.activation(sc1, lnw_f[:], AF.Abs)
        s.drain()
        s.activation(sc1, sc1, AF.Ln)
        s.drain()
        s.activation(sc1, sc1, AF.Exp, scale=-1.0)
        s.activation(sc2, lnw_f[:], AF.Sign)
        s.drain()
        S.inc(s.drain(), "acon", 1)

    blk.scalar(p_s0)

    w_jobs = [("ff", j) for j in range(FT)]
    w_jobs += [("ip", j) for j in range(35)]
    for j in range(FT):
        w_jobs += [("opA", j), ("opB", j)]

    def p_syncw(y):
        S.wait(y, "gpc")
        S.wait(y, "acon")
        for n, (kind, j) in enumerate(w_jobs):
            if n >= 1:
                y.wait_ge(S._get("stgf"), n)
            if kind == "ip" and j == 34:
                dst = land[1][0:32, 0:D]
                src = wip_h[34 * 128:DIP, :]
            else:
                dst = land[1][:, 0:D]
                if kind == "ff":
                    src = wff_h[j * 128:(j + 1) * 128, :]
                elif kind == "ip":
                    src = wip_h[j * 128:(j + 1) * 128, :]
                elif kind == "opA":
                    src = wop_h[j * 128:(j + 1) * 128, 0:D]
                else:
                    src = wop_h[j * 128:(j + 1) * 128, D:2 * D]
            S.inc(y.dma_start(out=dst, in_=src), "wst", 16)

    blk.sync(p_syncw)

    def p_tw(t):
        for n, (kind, j) in enumerate(w_jobs):
            t.wait_ge(S._get("wst"), 16 * (n + 1))
            if n >= 2:
                t.wait_ge(S._get("wev"), n - 1)
            src = land[1][:, 0:D]
            pp = pA if n % 2 == 0 else pB
            rows_in = 32 if (kind == "ip" and j == 34) else 128
            for i in range(FT):
                off = (i // 4) * 512 + (i % 4) * 128
                tr(t, pp[:, off:off + rows_in],
                            src[0:rows_in, i * 128:(i + 1) * 128],
                            id_f[0:rows_in, 0:rows_in])
            S.inc(_lt[0], "wtr", 1)
            S.inc(t.drain(), "stgf", 1)

        t.wait_ge(S._get("wev"), len(w_jobs))
        for tt2 in range(19):
            a0 = DI + tt2 * 128
            b0 = min(a0 + 128, DIP)
            for k in range(FT):
                mm(t, pA[0:b0 - a0, tt2:tt2 + 1], wip(k, a0, b0),
                   lnb_b[:, k:k + 1], first=(k == 0), last=(k == FT - 1))
        for nb in range(4):
            for k in range(FT):
                mm(t, pB[0:1, nb * 512:(nb + 1) * 512], lnb_b[:, k:k + 1],
                   wip(k, nb * 512, (nb + 1) * 512),
                   first=(k == 0), last=(k == FT - 1))
        S.inc(_lt[0], "badp", 1)

    blk.tensor(p_tw)

    def p_vw(v):
        S.wait(v, "acon")
        v.tensor_tensor(lnb_b[:], lnb_f[:], land[1][:, 1152:1152 + FT],
                        ALU.mult)
        v.drain()
        v.tensor_tensor(lnb_b[:], lnb_b[:],
                        land[1][:, 1152 + FT:1152 + 2 * FT], ALU.mult)
        v.drain()
        for n, (kind, j) in enumerate(w_jobs):
            v.wait_ge(S._get("wtr"), n + 1)
            pp = pA if n % 2 == 0 else pB
            rows_in = 32 if (kind == "ip" and j == 34) else 128
            for bank in range(2):
                base, i0 = bank * 512, bank * 4
                view = pp[:, base:base + 512].rearrange(
                    "p (i c) -> p i c", i=4)[:, :, 0:rows_in]
                if kind == "ff":
                    dstv = wfft[:].rearrange("p (i c) -> p i c", i=FT)[
                        :, i0:i0 + 4, j * 128:j * 128 + rows_in]
                    v.tensor_scalar(dstv, view, 448.0, -448.0,
                                    ALU.min, ALU.max)
                elif kind == "ip":
                    for ii in range(4):
                        dstv = wipt[:].rearrange("p (i c) -> p i c", i=FT)[
                            :, i0 + ii, j * 128:j * 128 + rows_in]
                        v.tensor_scalar(
                            dstv, pp[:, base + ii * 128:
                                     base + ii * 128 + rows_in],
                            lnw_f[:, i0 + ii:i0 + ii + 1], None, ALU.mult)
                else:
                    dk0 = (0 if kind == "opA" else 8) + i0
                    dstv = wopt[:].rearrange("p (i c) -> p i c", i=DIT)[
                        :, dk0:dk0 + 4, j * 128:j * 128 + rows_in]
                    v.tensor_tensor(dstv, view,
                                    rep_inner(gam_f[:, dk0:dk0 + 4], rows_in),
                                    ALU.mult)
            v.drain()
            S.inc(v.drain(), "wev", 1)
        S.wait(v, "badp")
        S.wait(v, "gpc")
        v.tensor_copy(badd_fm[:], pA[:, 0:19])
        for nb in range(4):
            v.tensor_copy(badd_z[0:1, nb * 512:(nb + 1) * 512],
                          pB[0:1, nb * 512:(nb + 1) * 512])
        v.drain()
        cwv = convw[:].rearrange("p (i k) -> p i k", i=CT)
        ws = land[1][:, 1024:1024 + CT]
        ws2 = land[1][:, 1024 + CT:1024 + 2 * CT]
        v.tensor_tensor(ws, cwv[:, :, 0], cwv[:, :, 1], ALU.add)
        v.tensor_tensor(ws2, cwv[:, :, 2], cwv[:, :, 3], ALU.add)
        v.drain()
        v.tensor_tensor(ws, ws, ws2, ALU.add)
        v.drain()
        v.tensor_tensor(ws, ws, badd_fm[:, 0:CT], ALU.mult)
        v.drain()
        v.tensor_tensor(convb[:], convb[:], ws, ALU.add)
        v.tensor_tensor(dtb_c[:], dtb_c[:], badd_fm[0:H, CT:CT + 1], ALU.add)
        v.tensor_scalar(
            xbcp[:].rearrange("p (i t) -> p i t", i=CT)[:, :, 0:3],
            rep_inner(badd_fm[:, 0:CT], 3), -1.0, None, ALU.mult)
        v.drain()
        S.inc(v.drain(), "wrdy", 1)

    blk.vector(p_vw)

    def xv(buf):
        return buf[:].rearrange("p (i t) -> p i t", i=FT)

    def emit_ln_stats(xmark, xf, po=0):
        def s_x2(s):
            S.wait(s, xmark)
            S.wait(s, "upre", lag=0)
            S.wait(s, "xres", lag=0)
            s.copy(x_bf, xf[:])
            s.drain()
            S.inc(s.drain(), "xbf", 1)
            S.wait(s, "stp", lag=0)
            s.activation(scrA[:, 0:FT * QC], x_bf, AF.Square)
            s.drain()
            S.inc(s.drain(), "x2", 1)

        def t_stats(t):
            S.wait(t, "xbf")
            S.wait(t, "x2")
            S.wait(t, "pBfree", lag=0)
            for k in range(FT):
                mm(t, pB[0:1, po:po + 512], ones_c[:],
                   bass.AP(scrA, k * QC,
                           [[H * QS, 128], [FT * QC, 2], [1, QC]]),
                   first=(k == 0), last=(k == FT - 1))
            S.inc(_lt[0], "stp", 1)

        blk.scalar(s_x2)
        blk.tensor(t_stats)

    def emit_ln_rest(fused, udst_f8, xf, po=0, ugate=None):
        def s_m1(s):
            S.wait(s, "stp")
            s.activation(rows[0:1, 0:256], pB[0:1, po + 256:po + 512],
                         AF.Square)
            s.drain()
            S.inc(s.drain(), "m1", 1)

        def v_m1(v):
            S.wait(v, "m1")
            v.tensor_tensor(rows[0:1, 256:512], pB[0:1, po:po + 256],
                            rows[0:1, 0:256], ALU.subtract)
            v.drain()
            S.inc(v.drain(), "m2", 1)

        blk.scalar(s_m1)
        blk.vector(v_m1)

        if not fused:
            def s_m2(s):
                S.wait(s, "m2")
                s.activation(rows[0:1, 0:256], rows[0:1, 256:512],
                             AF.Ln, bias=EPS)
                s.drain()
                s.activation(rows[0:1, 0:256], rows[0:1, 0:256],
                             AF.Exp, scale=-0.5)
                s.copy(rows[0:1, 256:512], pB[0:1, po + 256:po + 512])
                s.drain()
                S.inc(s.drain(), "m4", 1)

            blk.scalar(s_m2)
        else:
            def s_m2(s):
                S.wait(s, "m2")
                s.activation(rows[0:1, 512:768], rows[0:1, 256:512],
                             AF.Ln, bias=EPS)
                s.drain()
                s.activation(rows[0:1, 256:512], rows[0:1, 256:512],
                             AF.Ln, scale=1.0 + EPS, bias=EPS * EPS)
                s.drain()
                s.activation(rows[0:1, 512:768], rows[0:1, 512:768],
                             AF.Exp, scale=-0.5)
                s.activation(rows[0:1, 256:512], rows[0:1, 256:512],
                             AF.Exp, scale=-0.5)
                s.drain()
                s.copy(rows[0:1, 0:256], pB[0:1, po + 256:po + 512])
                s.drain()
                S.inc(s.drain(), "m5", 1)

            def t_bcf(t):
                S.wait(t, "m5")
                mm(t, pB[:, 512:1024], onesr2[0:1, 0:128],
                   rows[0:1, 0:512], first=True)
                mm(t, pB[:, 1024:1280], onesr2[0:1, 0:128],
                   rows[0:1, 512:768], first=True)
                S.inc(_lt[0], "bcp", 1)
                S.inc(t.drain(), "pBfree", 1)

            def v_bcf(v):
                S.wait(v, "bcp")
                v.tensor_copy(mu1_t[:], pB[:, 512:768])
                v.tensor_copy(rb_t[:], pB[:, 768:1024])
                v.tensor_copy(rb1_t[:], pB[:, 1024:1280])
                v.drain()
                v.tensor_tensor(mrb_t[:], mu1_t[:], rb_t[:], ALU.mult)
                v.drain()
                S.inc(v.drain(), "bce", 1)
                S.inc(v.drain(), "pBfree", 1)

            blk.scalar(s_m2)
            blk.tensor(t_bcf)
            blk.vector(v_bcf)

        if not fused:
            def t_bc(t):
                S.wait(t, "m4")
                mm(t, pB[:, po + 512:po + 1024], onesr2[0:1, 0:128],
                   rows[0:1, 0:512], first=True)
                S.inc(_lt[0], "bcp", 1)
                S.inc(t.drain(), "pBfree", 1)

            def v_bc(v):
                S.wait(v, "bcp")
                v.tensor_copy(rb_t[:], pB[:, po + 512:po + 768])
                v.tensor_copy(mu1_t[:], pB[:, po + 768:po + 1024])
                v.drain()
                v.tensor_tensor(mrb_t[:], mu1_t[:], rb_t[:], ALU.mult)
                v.drain()
                S.inc(v.drain(), "bce", 1)
                S.inc(v.drain(), "pBfree", 1)

            blk.tensor(t_bc)
            blk.vector(v_bc)

        def v_u(v):
            S.wait(v, "bce")
            if ugate is not None:
                S.wait(v, ugate)
            for i in range(FT):
                v.tensor_tensor(scr2[:, i * QC:(i + 1) * QC],
                                x_bf[:, i * QC:(i + 1) * QC], rb_t[:],
                                ALU.mult)
            v.drain()
            if udst_f8:
                for i in range(FT):
                    v.tensor_tensor(u1_f8[:, i * QC:(i + 1) * QC],
                                    scr2[:, i * QC:(i + 1) * QC], mrb_t[:],
                                    ALU.subtract)
                v.drain()
                S.inc(v.drain(), "ur", 1)
            else:
                for i in range(FT):
                    v.tensor_tensor(u2_bf[:, i * QC:(i + 1) * QC],
                                    scr2[:, i * QC:(i + 1) * QC], mrb_t[:],
                                    ALU.subtract)
                v.drain()
                S.inc(v.drain(), "ur", 1)
            S.inc(v.drain(), "upre", 1)

        blk.vector(v_u)

    def emit_ln(tag, fused, udst_f8, xmark, xf, po=0, ugate=None):
        emit_ln_stats(xmark, xf, po)
        emit_ln_rest(fused, udst_f8, xf, po, ugate)

    def emit_xtr(cn):
        lb = land[cn % 2]
        xfn = x_fms[cn % 2]
        for w in range(2):
            def xw_t(t, w=w, cn=cn):
                if w == 0:
                    S.wait(t, "xinA" if cn % 2 == 0 else "xinB")
                    if cn == 0:
                        S.wait(t, "wrdy")
                        S.wait(t, "dcon")
                    else:
                        S.wait(t, "supd", lag=0)
                else:
                    S.wait(t, "xev")
                for i in range(FT):
                    tr(t, pB[:, 1024 + i * 128:1024 + (i + 1) * 128],
                       lb[:, w * D + i * 128:w * D + (i + 1) * 128], id_f[:])
                S.inc(_lt[0], "xw", 1)
                if w == 1:
                    S.inc(t.drain(), "xtrp", 1)

            def xw_s(s, w=w, cn=cn):
                S.wait(s, "xw")
                dst = xfn[:].rearrange("p (i t) -> p i t", i=FT)[
                    :, :, w * 128:(w + 1) * 128]
                src_ = pB[:, 1024:2048].rearrange("p (i t) -> p i t", i=FT)
                s.copy(dst, src_)
                s.drain()
                S.inc(s.drain(), "xev", 1)
                if w == 1:
                    S.inc(s.drain(), "xin1", 1)

            blk.tensor(xw_t)
            blk.scalar(xw_s)

    def emit_front2(cn, c0=False):
        xfn = x_fms[cn % 2]

        f1c_b = S.c.get("f1c", 0)
        f1ec_b = S.c.get("f1ec", 0)
        xin2c_b = S.c.get("xin2c", 0)
        x2c2_b = S.c.get("x2c2", 0)

        def ph_ffn1_t(t, cn=cn):
            S.wait(t, "ur")
            if c0:
                S.wait(t, "pBfree", lag=0)
            else:
                S.wait(t, "bce", lag=0)
                S.wait(t, "pBfree", lag=0)
            for ct in range(FT):
                for k2 in range(4):
                    mm(t, pB[:, ct * QC:(ct + 1) * QC],
                       bass.AP(wfft, 2 * k2 * D + ct * 128,
                               [[FT * D, 128], [D, 2], [1, 128]]),
                       bass.AP(u1_f8, 2 * k2 * QC,
                               [[FT * QC, 128], [QC, 2], [1, QC]]),
                       first=(ct % 2 == 0 and k2 == 0), last=(k2 == 3),
                       perf_mode=DR)
                if ct % 2 == 1:
                    S.inc(_lt[0], "f1c", 1)

        def ph_ffn1_ep(s, cn=cn, f1c_b=f1c_b):
            lt = land[cn % 2]
            for p in range(4):
                s.wait_ge(S._get("f1c"), f1c_b + p + 1)
                for ct in (2 * p, 2 * p + 1):
                    s.activation(lt[:, ct * QC:(ct + 1) * QC],
                                 pB[:, ct * QC:(ct + 1) * QC], AF.Identity,
                                 scale=0.5, bias=bffh[:, ct:ct + 1])
                s.drain()
                S.inc(s.drain(), "f1ec", 1)
            S.inc(s.drain(), "pBfree", 1)

        def ph_ffn1_v(v, cn=cn, f1ec_b=f1ec_b):
            lt = land[cn % 2]
            for p in range(4):
                v.wait_ge(S._get("f1ec"), f1ec_b + p + 1)
                v.tensor_tensor(xfn[:, p * 512:(p + 1) * 512],
                                xfn[:, p * 512:(p + 1) * 512],
                                lt[:, p * 512:(p + 1) * 512], ALU.add)
                v.drain()
                S.inc(v.drain(), "xin2c", 1)
            S.inc(v.drain(), "xin2", 1)

        def ln2_x2_s(s, xin2c_b=xin2c_b):
            S.wait(s, "upre", lag=0)
            S.wait(s, "xres", lag=0)
            S.wait(s, "stp", lag=0)
            for p in range(4):
                s.wait_ge(S._get("xin2c"), xin2c_b + p + 1)
                s.copy(x_bf[:, p * 512:(p + 1) * 512],
                       xfn[:, p * 512:(p + 1) * 512])
                s.drain()
                s.activation(scrA[:, p * 512:(p + 1) * 512],
                             x_bf[:, p * 512:(p + 1) * 512], AF.Square)
                s.drain()
                S.inc(s.drain(), "x2c2", 1)
            S.inc(s.drain(), "xbf", 1)
            S.inc(s.drain(), "x2", 1)

        def ln2_stats_t(t, x2c2_b=x2c2_b):
            for p in range(4):
                t.wait_ge(S._get("x2c2"), x2c2_b + p + 1)
                for k in (2 * p, 2 * p + 1):
                    mm(t, pB[0:1, 0:512], ones_c[:],
                       bass.AP(scrA, k * QC,
                               [[H * QS, 128], [FT * QC, 2], [1, QC]]),
                       first=(k == 0), last=(k == FT - 1))
            S.inc(_lt[0], "stp", 1)

        blk.tensor(ph_ffn1_t)
        blk.scalar(ph_ffn1_ep)
        blk.vector(ph_ffn1_v)
        blk.scalar(ln2_x2_s)
        blk.tensor(ln2_stats_t)

    XB = QC + 4
    for c in range(nch):
        tagc = str(c)
        xf = x_fms[c % 2]

        if c == 0 and on("xtr"):
            emit_xtr(0)
            if on("ln1"):
                emit_ln("1", fused=False, udst_f8=True, xmark="xin1",
                        xf=xf, po=1024, ugate="f2p")
            emit_front2(0, c0=True)
            emit_ln_rest(fused=True, udst_f8=False, xf=xf, po=0)

        if c + 1 < nch:
            def ph_load(y, c=c):
                S.wait(y, "st", lag=0)
                S.wait(y, "xtrp", lag=0)
                if c == 0:
                    S.wait(y, "wrdy")
                for tt in range(2):
                    r0 = (c + 1) * QC + tt * 128
                    S.inc(y.dma_start(
                        out=land[(c + 1) % 2][:, tt * D:(tt + 1) * D],
                        in_=x_h[r0:r0 + 128, :]),
                        "xinA" if (c + 1) % 2 == 0 else "xinB", 16)
            blk.sync(ph_load)

        if not on("ipx"):
            continue
        if not on("dtc"):
            continue
        def ph_dt_t(t):
            S.wait(t, "ur")
            S.wait(t, "pAfree", lag=0)
            for k in range(FT):
                mm(t, pA[0:H, 0:QC], wip(k, DIP - H, DIP),
                   xv(u2_bf)[:, k, :], first=(k == 0), last=(k == FT - 1))
            S.inc(_lt[0], "dtp", 1)

        blk.tensor(ph_dt_t)

        groups = [list(range(4 * g, min(4 * g + 4, CT))) for g in range(5)]
        ipxe_b = S.c.get("ipxe", 0)
        ipxp_b = S.c.get("ipxp", 0)

        def ph_ipx_t_a(t):
            S.wait(t, "ur")
            S.wait(t, "pBfree", lag=0)
            for g in (0, 1):
                base = (g % 2) * 1024
                for n, ct in enumerate(groups[g]):
                    for k in range(FT):
                        mm(t, pB[:, base + n * QC:base + (n + 1) * QC],
                           wip(k, DI + ct * 128, DI + (ct + 1) * 128),
                           xv(u2_bf)[:, k, :],
                           first=(n % 2 == 0 and k == 0),
                           last=(k == FT - 1))
                S.inc(_lt[0], "ipxp", 1)

        blk.tensor(ph_ipx_t_a)

        def ph_dt_s(s):
            S.wait(s, "dtp")
            s.activation(a_f[:], pA[0:H, 0:QC], AF.Abs, bias=dtb_c[:, 0:1])
            s.drain()
            s.activation(a_f[:], a_f[:], AF.Exp, scale=-1.0)
            s.drain()
            s.activation(a_f[:], a_f[:], AF.Ln, bias=1.0)
            s.drain()
            s.activation(dt_f[:], pA[0:H, 0:QC], AF.Relu, bias=dtb_c[:, 0:1])
            s.drain()
            S.inc(s.drain(), "dtsp", 1)

        def ph_dt_v(v):
            S.wait(v, "dtsp")
            v.tensor_tensor(dt_f[:], dt_f[:], a_f[:], ALU.add)
            v.drain()
            S.inc(v.drain(), "pAfree", 1)
            v.tensor_scalar(a_f[:], dt_f[:], aneg[:, 0:1], None, ALU.mult)
            v.drain()
            for sbi in range(NSUB):
                sl = slice(sbi * QS, (sbi + 1) * QS)
                v.tensor_tensor_scan(cum_f[:, sl], a_f[:, sl], a_f[:, sl],
                                     0.0, ALU.add, ALU.bypass)
            v.drain()
            S.inc(v.drain(), "cumr", 1)

        blk.scalar(ph_dt_s)
        blk.vector(ph_dt_v)

        def ph_xres_v(v, c=c):
            S.wait(v, "ur", lag=0)
            S.wait(v, "st", lag=0)
            lt = land[c % 2][:].rearrange("p (i t) -> p i t", i=FT)
            xb3 = x_bf.rearrange("p (i t) -> p i t", i=FT)
            v.tensor_tensor(lt, xb3, rep_outer(mu1_t[:], FT), ALU.subtract)
            v.drain()
            v.tensor_tensor(lt, lt, rep_outer(rb1_t[:], FT), ALU.mult)
            v.drain()
            v.tensor_tensor(xf[:], xf[:], land[c % 2][:], ALU.add)
            v.drain()
            S.inc(v.drain(), "xres", 1)

        blk.vector(ph_xres_v)

        def ph_chl_g(g):
            S.wait(g, "cumr")
            g.tensor_copy(cumhi[:], cum_f[:])
            g.drain()
            g.tensor_tensor(cumlo[:], cum_f[:], cumhi[:], ALU.subtract)
            g.drain()

        blk.gpsimd(ph_chl_g)

        def ph_cc_s(s):
            S.wait(s, "cumr")
            for sbi in range(NSUB):
                sl = slice(sbi * QS, (sbi + 1) * QS)
                s.activation(ccf[:, sl], cum_f[:, sl], AF.Exp, scale=-1.0,
                             bias=cum_f[:, (sbi + 1) * QS - 1:(sbi + 1) * QS])
            s.drain()
            S.inc(s.drain(), "ccfr", 1)

        blk.scalar(ph_cc_s)

        def ph_tt_t(t):
            S.wait(t, "ccfr")
            S.wait(t, "pAfree", lag=0)
            for sbi in range(NSUB):
                sl0 = sbi * QS
                tr(t, pA[:, sbi * 512 + 0:sbi * 512 + H],
                            dt_f[:, sl0:sl0 + QS], id_f[0:H, 0:H])
                tr(t, pA[:, sbi * 512 + 128:sbi * 512 + 128 + H],
                            cum_f[:, sl0:sl0 + QS], id_f[0:H, 0:H])
                tr(t, pA[:, sbi * 512 + 256:sbi * 512 + 256 + H],
                            ccf[:, sl0:sl0 + QS], id_f[0:H, 0:H])
                tr(t, pA[0:1, sbi * 512 + 384:sbi * 512 + 384 + H],
                            cum_f[:, sl0 + QS - 1:sl0 + QS], id_f[0:H, 0:H])
            S.inc(_lt[0], "dttp", 1)

        def ph_tt_v(v):
            S.wait(v, "dttp")
            for sbi in range(NSUB):
                hs = slice(sbi * H, (sbi + 1) * H)
                v.tensor_copy(dt_tm[:, hs], pA[:, sbi * 512 + 0:sbi * 512 + H])
                v.tensor_copy(cum_tm[:, hs],
                              pA[:, sbi * 512 + 128:sbi * 512 + 128 + H])
                v.tensor_copy(cc_tm[:, hs],
                              pA[:, sbi * 512 + 256:sbi * 512 + 256 + H])
                v.tensor_copy(rows[0:1, 512 + 32 * sbi:512 + 32 * sbi + H],
                              pA[0:1, sbi * 512 + 384:sbi * 512 + 384 + H])
            v.drain()
            S.inc(v.drain(), "dtte", 1)

        blk.tensor(ph_tt_t)
        blk.vector(ph_tt_v)

        def ph_cq_s(s):
            S.wait(s, "dtte")
            s.activation(cexp_tm[:], cum_tm[:], AF.Exp)
            for sbi in range(NSUB):
                s.activation(rows[0:1, 512 + 32 * sbi:512 + 32 * sbi + H],
                             rows[0:1, 512 + 32 * sbi:512 + 32 * sbi + H],
                             AF.Exp)
            s.drain()
            S.inc(s.drain(), "cqe", 1)

        def ph_cq_t(t):
            S.wait(t, "cqe")
            for sbi in range(NSUB):
                mm(t, pA[:, 1024 + sbi * 128:1024 + sbi * 128 + H],
                   onesr2[0:1, 0:128],
                   rows[0:1, 512 + 32 * sbi:512 + 32 * sbi + H],
                   first=True)
            S.inc(_lt[0], "cqp", 1)

        def ph_cq_v(v):
            S.wait(v, "cqp")
            for sbi in range(NSUB):
                v.tensor_copy(ccq_b[:, sbi * H:(sbi + 1) * H],
                              pA[:, 1024 + sbi * 128:1024 + sbi * 128 + H])
            v.drain()
            S.inc(v.drain(), "cqv", 1)
            S.inc(v.drain(), "pAfree", 1)

        blk.scalar(ph_cq_s)
        blk.tensor(ph_cq_t)
        blk.vector(ph_cq_v)

        def ph_halo_v(v, c=c):
            if c > 0:
                S.wait(v, "cvp", lag=0)
                v.tensor_copy(
                    xbcp[:].rearrange("p (i t) -> p i t", i=CT)[:, :, 0:3],
                    tl3[:].rearrange("p (i t) -> p i t", i=CT))
                v.drain()
            S.inc(v.drain(), "halo", 1)

        blk.vector(ph_halo_v)

        def ph_ipx_t(t, ipxe_b=ipxe_b):
            for g in (2, 3, 4):
                t.wait_ge(S._get("ipxe"), ipxe_b + g - 1)
                base = (g % 2) * 1024
                for n, ct in enumerate(groups[g]):
                    for k in range(FT):
                        mm(t, pB[:, base + n * QC:base + (n + 1) * QC],
                           wip(k, DI + ct * 128, DI + (ct + 1) * 128),
                           xv(u2_bf)[:, k, :],
                           first=(n % 2 == 0 and k == 0),
                           last=(k == FT - 1))
                S.inc(_lt[0], "ipxp", 1)

        def ph_ipx_s(s, ipxp_b=ipxp_b):
            S.wait(s, "halo")
            S.wait(s, "opjp", lag=0)
            for g, cts in enumerate(groups):
                s.wait_ge(S._get("ipxp"), ipxp_b + g + 1)
                base = (g % 2) * 1024
                dst = xbcp[:].rearrange("p (i t) -> p i t", i=CT)[
                    :, cts[0]:cts[0] + len(cts), 3:3 + QC]
                src_ = pB[:, base:base + len(cts) * QC].rearrange(
                    "p (i t) -> p i t", i=len(cts))
                s.copy(dst, src_)
                s.drain()
                S.inc(s.drain(), "ipxe", 1)

        blk.tensor(ph_ipx_t)
        blk.scalar(ph_ipx_s)

        if not on("z"):
            continue
        def ph_z_t(t):
            S.wait(t, "ur")
            S.wait(t, "pAfree", lag=0)
            for zb in range(4):
                for k in range(FT):
                    mm(t, pA[:, zb * 512:(zb + 1) * 512],
                       u2_bf[:, k * QC:k * QC + QS],
                       wip(k, zb * 512, (zb + 1) * 512),
                       first=(k == 0), last=False)
                mm(t, pA[:, zb * 512:(zb + 1) * 512], ones2r[0:1, 0:128],
                   badd_z[0:1, zb * 512:(zb + 1) * 512],
                   first=False, last=True)
            S.inc(_lt[0], "z0p", 1)
            S.wait(t, "ipxe", lag=0)
            S.wait(t, "dtte", lag=0)
            for zb in range(4):
                for k in range(FT):
                    mm(t, pB[:, zb * 512:(zb + 1) * 512],
                       u2_bf[:, k * QC + QS:k * QC + 2 * QS],
                       wip(k, zb * 512, (zb + 1) * 512),
                       first=(k == 0), last=False)
                mm(t, pB[:, zb * 512:(zb + 1) * 512], ones2r[0:1, 0:128],
                   badd_z[0:1, zb * 512:(zb + 1) * 512],
                   first=False, last=True)
            S.inc(_lt[0], "z1p", 1)

        blk.tensor(ph_z_t)

        def ph_zev_s(s):
            S.wait(s, "z0p")
            _load_set(s, SET_SILU)
            s.activation(z_si[:, 0:DI], pA[:, :], AF.Silu)
            s.drain()
            S.inc(s.drain(), "z0e", 1)
            S.inc(s.drain(), "pAfree", 1)
            S.wait(s, "convd", lag=-1)
            s.activation(cacc[:], cacc[:], AF.Silu, bias=0.0)
            s.drain()
            S.inc(s.drain(), "xcsr", 1)
            S.wait(s, "z1p")
            s.activation(z_si[:, DI:2 * DI], pB[:, :], AF.Silu)
            s.drain()
            S.inc(s.drain(), "z1e", 1)
            S.inc(s.drain(), "pBfree", 1)
            _load_set(s, SET_EXP_LN)
            S.inc(s.drain(), "sexp", 1)

        blk.scalar(ph_zev_s)

        if not on("conv"):
            continue
        def ph_conv_v(v, ipxe_b=ipxe_b):
            S.wait(v, "halo")
            xb = xbcp[:].rearrange("p (i t) -> p i t", i=CT)
            ca = cacc[:].rearrange("p (i t) -> p i t", i=CT)
            for g, cts in enumerate(groups):
                v.wait_ge(S._get("ipxe"), ipxe_b + g + 1)
                for k in range(4):
                    for ct in cts:
                        in1 = (rep_inner(convb[:, ct:ct + 1], QC)[:, 0, :]
                               if k == 0 else ca[:, ct, :])
                        v.scalar_tensor_tensor(
                            ca[:, ct, :], xb[:, ct, k:k + QC],
                            convw[:, ct * 4 + k:ct * 4 + k + 1], in1,
                            ALU.mult, ALU.add)
                    v.drain()
            v.tensor_copy(tl3[:].rearrange("p (i t) -> p i t", i=CT),
                          xbcp[:].rearrange("p (i t) -> p i t", i=CT)
                          [:, :, QC:QC + 3])
            v.drain()
            S.inc(v.drain(), "convd", 1)
            S.inc(v.drain(), "cvp", 1)

        blk.vector(ph_conv_v)

        if not on("silu"):
            continue
        if not on("ssd"):
            continue
        Bts = [cacc[:, 16 * QC + i * QS:16 * QC + (i + 1) * QS]
               for i in range(NSUB)]
        Cts = [cacc[:, 17 * QC + i * QS:17 * QC + (i + 1) * QS]
               for i in range(NSUB)]

        def emit_cfl(sbi, extra=()):
            def ssd_cfl(g, sbi=sbi, extra=tuple(extra)):
                S.wait(g, "dtte")
                for nm in extra:
                    S.wait(g, nm, lag=0)
                for half in range(2):
                    hsl = slice(half * 16, (half + 1) * 16)
                    S.inc(g.dma_start(
                        out=scrA[0:1, half * DI:half * DI + 2048],
                        in_=cumhi[hsl, sbi * QS:(sbi + 1) * QS]), "cfl", 16)
                    S.inc(g.dma_start(
                        out=scrA[1:2, half * DI:half * DI + 2048],
                        in_=cumlo[hsl, sbi * QS:(sbi + 1) * QS]), "cfl", 16)
            blk.gpsimd(ssd_cfl)

        def mk_mm(t, sbi, half, pp):
            frows = scrA[0:2, half * DI:half * DI + 2048]
            tsl = slice(sbi * QS, (sbi + 1) * QS)
            for q in range(4):
                h0 = half * 16 + q * 4
                mm(t, pp[:, q * 512:(q + 1) * 512], ones2r[0:2, 0:128],
                   frows[0:2, q * 512:(q + 1) * 512], first=True, last=False)
                mm(t, pp[:, q * 512:(q + 1) * 512], trin_b[:],
                   rep_outer(id_b[:], 4), first=False, last=False)
                mm(t, pp[:, q * 512:(q + 1) * 512], cumhi[0:H, tsl],
                   rep_inner(neg_id32[:, h0:h0 + 4], 128),
                   first=False, last=False)
                mm(t, pp[:, q * 512:(q + 1) * 512], cumlo[0:H, tsl],
                   rep_inner(neg_id32[:, h0:h0 + 4], 128),
                   first=False, last=True)
            S.inc(_lt[0], "mkp%d" % sbi, 1)

        def mk_exp(s, sbi, half, pp):
            S.wait(s, "mkp%d" % sbi, lag=0)
            s.activation(scrA[:, half * DI:(half + 1) * DI], pp[:, :], AF.Exp)
            s.drain()
            S.inc(s.drain(), "mke%d" % sbi, 1)

        def mk0_t(t):
            S.wait(t, "cfl", lag=0)
            S.wait(t, "z0e", lag=0)
            mk_mm(t, 0, 0, pA)
            S.wait(t, "z1e", lag=0)
            mk_mm(t, 0, 1, pB)

        def mk0_s(s):
            S.wait(s, "sexp")
            S.wait(s, "stp", lag=0)
            S.wait(s, "upre", lag=0)
            S.wait(s, "xres", lag=0)
            mk_exp(s, 0, 0, pA)
            mk_exp(s, 0, 1, pB)

        emit_cfl(0, extra=("intrp", "x2", "stp", "upre", "xres"))
        blk.tensor(mk0_t)
        blk.scalar(mk0_s)

        def emit_sub(sbi):
            hs = slice(sbi * H, (sbi + 1) * H)
            Bt, Ctl = Bts[sbi], Cts[sbi]
            mk = "mke%d" % sbi

            def t1a(t, sbi=sbi, Bt=Bt, Ctl=Ctl, mk=mk):
                if sbi == 0:
                    S.wait(t, "xcsr")
                    S.wait(t, mk, lag=1)
                else:
                    S.wait(t, "yfme", lag=0)
                mm(t, pA[:, 0:QS], Bt, Ctl, first=True)
                S.inc(_lt[0], "scp", 1)
                mm(t, pA[:, 512:512 + QS], Bt, id_b[:], first=True)
                S.inc(_lt[0], "btp", 1)
                if sbi == 0:
                    S.wait(t, mk, lag=0)
                    S.wait(t, "z1e", lag=0)
                else:
                    S.wait(t, "supd", lag=0)
                    S.wait(t, "mke1", lag=0)
                for i in range(DIT):
                    mm(t, pB[:, i * 128:(i + 1) * 128],
                       cacc[:, i * QC + sbi * QS:i * QC + (sbi + 1) * QS],
                       id_b[:], first=(i % 4 == 0))
                S.inc(_lt[0], "xsp", 1)

            def v1a(v):
                S.wait(v, "scp")
                v.tensor_tensor(stri[:, 0:QS], pA[:, 0:QS], tri_b[:],
                                ALU.mult)
                v.drain()
                S.inc(v.drain(), "strir", 1)
                S.wait(v, "btp")
                v.tensor_copy(btm[:], pA[:, 512:512 + QS])
                v.drain()
                S.inc(v.drain(), "btme", 1)

            def v2w(v, mk=mk):
                S.wait(v, mk, lag=0)
                S.wait(v, "strir", lag=0)
                v.tensor_tensor(scrA[:].rearrange("p (h i) -> p h i", h=H),
                                scrA[:].rearrange("p (h i) -> p h i", h=H),
                                rep_outer(stri[:, 0:QS], H), ALU.mult)
                v.drain()
                S.inc(v.drain(), "wres", 1)

            def v1b(v, hs=hs):
                S.wait(v, "xsp")
                v.tensor_tensor(
                    xdt[:].rearrange("p (h e) -> p h e", h=H),
                    pB[:].rearrange("p (h e) -> p h e", h=H),
                    rep_inner(dt_tm[:, hs], HD), ALU.mult)
                v.drain()
                S.inc(v.drain(), "xdtr", 1)

            def t4(t, sbi=sbi, Ctl=Ctl):
                S.wait(t, "strir", lag=0)
                S.wait(t, "btme", lag=0)
                S.wait(t, "supd", lag=0)
                for b in range(4):
                    mm(t, pA[:, b * 512:(b + 1) * 512], Ctl,
                       s_bf[:, b * 512:(b + 1) * 512], first=True)
                S.inc(_lt[0], "intp", 1)
                S.wait(t, "wres", lag=0)
                S.wait(t, "xdtr", lag=0)
                for h in range(H):
                    mm(t, pB[:, h * HD:(h + 1) * HD],
                       scrA[:, h * 128:(h + 1) * 128],
                       xdt[:, h * HD:(h + 1) * HD],
                       first=(h % 8 == 0), last=False)
                for i in range(DIT):
                    mm(t, pB[:, i * 128:(i + 1) * 128],
                       cacc[:, i * QC + sbi * QS:i * QC + (sbi + 1) * QS],
                       ddiag[:, i * 128:(i + 1) * 128],
                       first=False, last=(i == DIT - 1))
                S.inc(_lt[0], "intrp", 1)

            def v4(v, sbi=sbi, hs=hs):
                S.wait(v, "intp")
                v.tensor_tensor(
                    scr2[:].rearrange("p (h e) -> p h e", h=H),
                    pA[:].rearrange("p (h e) -> p h e", h=H),
                    rep_inner(cexp_tm[:, hs], HD), ALU.mult)
                v.drain()
                S.inc(v.drain(), "s1r", 1)
                S.wait(v, "intrp")
                v.tensor_tensor(y_bf[:], pB[:, :], scr2[:], ALU.add)
                v.drain()
                S.inc(v.drain(), "yr", 1)
                v.tensor_tensor(scr2[:], y_bf[:],
                                z_si[:, sbi * DI:(sbi + 1) * DI], ALU.mult)
                v.drain()
                S.inc(v.drain(), "ygr", 1)
                v.tensor_tensor(
                    xdt[:].rearrange("p (h e) -> p h e", h=H),
                    xdt[:].rearrange("p (h e) -> p h e", h=H),
                    rep_inner(cc_tm[:, hs], HD), ALU.mult)
                v.drain()
                S.inc(v.drain(), "xdt2r", 1)

            def s5(s, c=c):
                S.wait(s, "ygr")
                S.wait(s, "xres", lag=0)
                s.activation(land[c % 2][:, 0:DI], scr2[:], AF.Square,
                             accum_out=ssq[:, 0:1])
                s.drain()
                s.activation(rms_c[:], ssq[:], AF.Ln, scale=1.0 / DI,
                             bias=EPS)
                s.drain()
                s.activation(rms_c[:], rms_c[:], AF.Exp, scale=-0.5)
                s.drain()
                s.activation(y_bf[:], scr2[:], AF.Identity,
                             scale=rms_c[:, 0:1])
                s.drain()
                S.inc(s.drain(), "ynr", 1)

            def t6a(t, sbi=sbi):
                S.wait(t, "ynr")
                S.wait(t, "s1r", lag=0)
                if sbi == 0:
                    S.wait(t, "mke1", lag=0)
                for i in range(DIT):
                    mm(t, pA[:, i * 128:(i + 1) * 128],
                       y_bf[:, i * 128:(i + 1) * 128], id_b[:],
                       first=(i % 4 == 0))
                S.inc(_lt[0], "ytp", 1)

            def t6b(t, sbi=sbi):
                S.wait(t, "xdt2r", lag=0)
                S.wait(t, "yr", lag=0)
                for b in range(4):
                    mm(t, pB[:, b * 512:(b + 1) * 512], btm[:],
                       xdt[:, b * 512:(b + 1) * 512], first=True)
                S.inc(_lt[0], "updp", 1)

            def s6(s, sbi=sbi):
                S.wait(s, "ytp")
                s.copy(yfm.rearrange("p (i t) -> p i t", i=DIT)
                       [:, :, sbi * QS:(sbi + 1) * QS],
                       pA[:].rearrange("p (i t) -> p i t", i=DIT))
                s.drain()
                S.inc(s.drain(), "yfme", 1)
                S.inc(s.drain(), "pAfree", 1)

            def v7a(v, hs=hs):
                v.tensor_tensor(
                    sdec[:].rearrange("p (h e) -> p h e", h=H),
                    s_bf[:].rearrange("p (h e) -> p h e", h=H),
                    rep_inner(ccq_b[:, hs], HD), ALU.mult)
                v.drain()
                S.inc(v.drain(), "sdcr", 1)

            def v7b(v):
                S.wait(v, "updp")
                v.tensor_tensor(s_bf[:], sdec[:], pB[:, :], ALU.add)
                v.drain()
                S.inc(v.drain(), "supd", 1)
                S.inc(v.drain(), "pBfree", 1)

            return t1a, v1a, v2w, v1b, t4, v4, s5, t6a, t6b, s6, v7a, v7b

        s0 = emit_sub(0)
        s1_ = emit_sub(1)

        blk.tensor(s0[0])
        blk.vector(s0[1])
        blk.vector(s0[2])
        blk.vector(s0[3])
        blk.tensor(s0[4])
        emit_cfl(1, extra=("intrp",))
        blk.vector(s0[5])
        def mk1_t0(t):
            S.wait(t, "cfl", lag=0)
            S.wait(t, "s1r", lag=0)
            mk_mm(t, 1, 0, pA)
        blk.tensor(mk1_t0)
        blk.scalar(s0[6])
        def mk1_s0(s):
            mk_exp(s, 1, 0, pA)
        blk.scalar(mk1_s0)
        blk.tensor(s0[8])
        blk.tensor(s0[7])
        blk.vector(s0[10])
        blk.vector(s0[11])
        def mk1_t1(t):
            S.wait(t, "supd", lag=0)
            mk_mm(t, 1, 1, pB)
        blk.tensor(mk1_t1)
        blk.scalar(s0[9])
        def mk1_s1(s):
            mk_exp(s, 1, 1, pB)
        blk.scalar(mk1_s1)

        blk.tensor(s1_[0])
        blk.vector(s1_[1])
        blk.vector(s1_[2])
        blk.vector(s1_[3])
        blk.tensor(s1_[4])
        blk.vector(s1_[5])
        blk.scalar(s1_[6])
        blk.tensor(s1_[7])
        blk.tensor(s1_[8])
        blk.scalar(s1_[9])
        blk.vector(s1_[10])
        blk.vector(s1_[11])

        opjc_b = S.c.get("opjc", 0)
        xin3c_b = S.c.get("xin3c", 0)
        x2c_b = S.c.get("x2c", 0)

        def ph_opj_t(t):
            S.wait(t, "yfme")
            S.wait(t, "pAfree", lag=0)
            for ct in range(FT):
                for k in range(DIT):
                    mm(t, pA[:, ct * QC:(ct + 1) * QC],
                       wopd(k, ct * 128, (ct + 1) * 128),
                       yfm[:, k * QC:(k + 1) * QC],
                       first=(ct % 2 == 0 and k == 0),
                       last=(k == DIT - 1))
                if ct % 2 == 1:
                    S.inc(_lt[0], "opjc", 1)
            S.inc(t.drain(), "opjp", 1)

        def ph_opj_v(v, opjc_b=opjc_b):
            S.wait(v, "xres", lag=0)
            for p in range(4):
                v.wait_ge(S._get("opjc"), opjc_b + p + 1)
                v.tensor_tensor(xf[:, p * 512:(p + 1) * 512],
                                xf[:, p * 512:(p + 1) * 512],
                                pA[:, p * 512:(p + 1) * 512], ALU.add)
                v.drain()
                S.inc(v.drain(), "xin3c", 1)
            S.inc(v.drain(), "xin3", 1)
            S.inc(v.drain(), "pAfree", 1)

        blk.tensor(ph_opj_t)
        blk.vector(ph_opj_v)

        if not on("ln3"):
            continue
        def ln3_x2_s(s, xin3c_b=xin3c_b):
            S.wait(s, "upre", lag=0)
            S.wait(s, "xres", lag=0)
            S.wait(s, "stp", lag=0)
            for p in range(4):
                s.wait_ge(S._get("xin3c"), xin3c_b + p + 1)
                s.copy(x_bf[:, p * 512:(p + 1) * 512],
                       xf[:, p * 512:(p + 1) * 512])
                s.drain()
                s.activation(scrA[:, p * 512:(p + 1) * 512],
                             x_bf[:, p * 512:(p + 1) * 512], AF.Square)
                s.drain()
                S.inc(s.drain(), "x2c", 1)
            S.inc(s.drain(), "xbf", 1)
            S.inc(s.drain(), "x2", 1)

        def ln3_stats_t(t, x2c_b=x2c_b):
            S.wait(t, "pBfree", lag=0)
            for p in range(4):
                t.wait_ge(S._get("x2c"), x2c_b + p + 1)
                for k in (2 * p, 2 * p + 1):
                    mm(t, pB[0:1, 0:512], ones_c[:],
                       bass.AP(scrA, k * QC,
                               [[H * QS, 128], [FT * QC, 2], [1, QC]]),
                       first=(k == 0), last=(k == FT - 1))
            S.inc(_lt[0], "stp", 1)

        blk.scalar(ln3_x2_s)
        blk.tensor(ln3_stats_t)
        emit_ln_rest(fused=False, udst_f8=True, xf=xf, po=0)
        pipe = c + 1 < nch and on("xtr") and on("ln1")
        if pipe:
            emit_xtr(c + 1)

        if not on("ffn2"):
            continue
        def ph_ffn2_t(t):
            S.wait(t, "ur")
            S.wait(t, "pAfree", lag=0)
            ffn_mms(t, pA)
            S.inc(_lt[0], "f2p", 1)

        blk.tensor(ph_ffn2_t)
        if pipe:
            emit_ln_stats("xin1", x_fms[(c + 1) % 2], po=1024)

        def ph_ffn2_v_s(s, c=c):
            S.wait(s, "f2p")
            lt = land[c % 2]
            for ct in range(FT):
                s.activation(lt[:, ct * QC:(ct + 1) * QC],
                             pA[:, ct * QC:(ct + 1) * QC], AF.Identity,
                             scale=0.5, bias=bffh[:, ct:ct + 1])
            s.drain()
            S.inc(s.drain(), "f2pE", 1)
            S.inc(s.drain(), "pAfree", 1)

        def ph_ffn2_v(v, c=c):
            S.wait(v, "f2pE")
            lt = land[c % 2]
            v.tensor_tensor(xf[:], xf[:], lt[:, 0:FT * QC], ALU.add)
            v.drain()
            S.inc(v.drain(), "x3r", 1)

        blk.scalar(ph_ffn2_v_s)
        blk.vector(ph_ffn2_v)
        if pipe:
            emit_ln_rest(fused=False, udst_f8=True, xf=x_fms[(c + 1) % 2],
                         po=1024, ugate="f2p")
            emit_front2(c + 1)

        if not on("out"):
            continue
        if pipe:
            emit_ln_rest(fused=True, udst_f8=False, xf=x_fms[(c + 1) % 2],
                         po=0)
        def ph_out_t(t):
            S.wait(t, "x3r")
            S.wait(t, "pAfree", lag=0)
            for tt in range(2):
                for i in range(FT):
                    tr(t, pA[:, tt * D + i * 128:tt * D + (i + 1) * 128],
                                xf[:, i * QC + tt * 128:
                                   i * QC + tt * 128 + 128], id_f[:])
            S.inc(_lt[0], "outp", 1)

        def ph_out_v(s, c=c):
            S.wait(s, "outp")
            s.copy(land[c % 2][:, :], pA[:, :])
            s.drain()
            S.inc(s.drain(), "oute", 1)
            S.inc(s.drain(), "pAfree", 1)

        blk.tensor(ph_out_t)
        blk.scalar(ph_out_v)

        def ph_store_y(y, c=c):
            S.wait(y, "oute")
            for tt in range(2):
                r0 = c * QC + tt * 128
                S.inc(y.dma_start(out=out_h[r0:r0 + 128, :],
                                  in_=land[c % 2][:, tt * D:(tt + 1) * D]),
                      "st", 16)
            if c == nch - 1:
                S.wait(y, "st")

        blk.sync(ph_store_y)

    es.close()
    nc.finalize()
    return nc


def kernel(L0_attn_x, L0_attn_ln_w, L0_attn_ln_b, L0_attn_in_proj_w,
           L0_attn_conv_w, L0_attn_conv_b, L0_attn_A_log, L0_attn_D,
           L0_attn_dt_bias, L0_attn_gamma, L0_attn_out_proj_w,
           L0_ffn_weight, L0_ffn_bias):
    B, T, _ = L0_attn_x.shape
    nc = build_nc(T)
    common = {
        "w_ip": np.ascontiguousarray(L0_attn_in_proj_w),
        "w_op": np.ascontiguousarray(L0_attn_out_proj_w),
        "w_ff": np.ascontiguousarray(L0_ffn_weight),
        "b_ff": np.ascontiguousarray(L0_ffn_bias),
        "ln_w": np.ascontiguousarray(L0_attn_ln_w),
        "ln_b": np.ascontiguousarray(L0_attn_ln_b),
        "conv_w": np.ascontiguousarray(L0_attn_conv_w),
        "conv_b": np.ascontiguousarray(L0_attn_conv_b),
        "a_log": np.ascontiguousarray(L0_attn_A_log),
        "d_vec": np.ascontiguousarray(L0_attn_D),
        "dt_bias": np.ascontiguousarray(L0_attn_dt_bias),
        "gamma": np.ascontiguousarray(L0_attn_gamma),
    }
    in_maps = [dict(common, x=np.ascontiguousarray(L0_attn_x[b]))
               for b in range(B)]
    res = run_bass_kernel_spmd(nc, in_maps, core_ids=list(range(B)))
    return np.stack([r["out"] for r in res.results], axis=0)


# version 33
# speedup vs baseline: 1.0707x; 1.0371x over previous
import os
import numpy as np
from contextlib import ExitStack

import concourse.bass as bass
import concourse.mybir as mybir
from concourse.bass_utils import run_bass_kernel_spmd

F32 = mybir.dt.float32
BF16 = mybir.dt.bfloat16
FP8 = mybir.dt.float8e4
AF = mybir.ActivationFunctionType
ALU = mybir.AluOpType

D, FT, DI, DIT = 1024, 8, 2048, 16
CD, CT, DIP, H, HD = 2304, 18, 4384, 32, 64
QC, QS = 256, 128
NSUB = QC // QS
EPS = 1e-5

_SET_IDS = {}


def _act_set_ids(arch):
    if arch not in _SET_IDS:
        from concourse.hw_specs import get_activation_tables
        tabs = list(get_activation_tables(arch).values())
        exp_ln = next(i for i, fns in enumerate(tabs)
                      if AF.Exp in fns and AF.Ln in fns)
        silu = next(i for i, fns in enumerate(tabs) if AF.Silu in fns)
        _SET_IDS[arch] = (exp_ln, silu)
    return _SET_IDS[arch]


def _load_set(scalar, set_id):
    return scalar.add_instruction(
        mybir.InstLoadActFuncSet(
            name=scalar.bass.get_next_instruction_name(),
            act_func_set_id=set_id, ins=[], outs=[]))


def rep_outer(ap2d, outer):
    P, n = ap2d.shape
    return ap2d[:, None, :].broadcast_to((P, outer, n))


def rep_inner(ap2d, inner):
    P, n = ap2d.shape
    return ap2d[:, :, None].broadcast_to((P, n, inner))


class Sems:
    def __init__(self, nc, es):
        self.nc, self.es, self.h, self.c = nc, es, {}, {}

    def _get(self, name):
        if name not in self.h:
            self.h[name] = self.es.enter_context(self.nc.semaphore(name))
            self.c[name] = 0
        return self.h[name]

    def inc(self, instr, name, n=1):
        instr.then_inc(self._get(name), n)
        self.c[name] += n
        return instr

    def wait(self, eng, name, lag=0):
        eng.wait_ge(self._get(name), self.c[name] - lag)


_STAGES = ["pro", "xtr", "ln1", "ffn1", "ln2", "ipx", "dtc", "z", "conv",
            "silu", "ssd", "opj", "ln3", "ffn2", "out"]


def build_nc(T=2048):
    stage = os.environ.get("KSTAGE", "out")
    slvl = _STAGES.index(stage)

    def on(lvl):
        return slvl >= _STAGES.index(lvl)

    nch = T // QC
    nc = bass.Bass("TRN2", target_bir_lowering=False,
                   dynamic_dma_scratch_size=4096)
    SET_EXP_LN, SET_SILU = _act_set_ids(nc.m.arch)

    x_h = nc.dram_tensor("x", [T, D], F32, kind="ExternalInput").ap()
    out_h = nc.dram_tensor("out", [T, D], F32, kind="ExternalOutput").ap()
    wip_h = nc.dram_tensor("w_ip", [DIP, D], F32, kind="ExternalInput").ap()
    wop_h = nc.dram_tensor("w_op", [D, DI], F32, kind="ExternalInput").ap()
    wff_h = nc.dram_tensor("w_ff", [D, D], F32, kind="ExternalInput").ap()
    bff_h = nc.dram_tensor("b_ff", [D], F32, kind="ExternalInput").ap()
    lnw_h = nc.dram_tensor("ln_w", [D], F32, kind="ExternalInput").ap()
    lnb_h = nc.dram_tensor("ln_b", [D], F32, kind="ExternalInput").ap()
    cw_h = nc.dram_tensor("conv_w", [CD, 4], F32, kind="ExternalInput").ap()
    cb_h = nc.dram_tensor("conv_b", [CD], F32, kind="ExternalInput").ap()
    al_h = nc.dram_tensor("a_log", [H], F32, kind="ExternalInput").ap()
    dv_h = nc.dram_tensor("d_vec", [H], F32, kind="ExternalInput").ap()
    dtb_h = nc.dram_tensor("dt_bias", [H], F32, kind="ExternalInput").ap()
    gam_h = nc.dram_tensor("gamma", [DI], F32, kind="ExternalInput").ap()

    es = ExitStack()
    sb = lambda n, sh, dt=F32: es.enter_context(nc.sbuf_tensor(n, sh, dt))
    S = Sems(nc, es)

    wipt = sb("wipt", [128, FT * DIP], BF16)
    wopt = sb("wopt", [128, DIT * D], BF16)
    wfft = sb("wfft", [128, FT * D], FP8)
    convw = sb("convw", [128, CT * 4])
    convb = sb("convb", [128, CT])
    lnw_f = sb("lnw_f", [128, FT])
    lnb_f = sb("lnb_f", [128, FT])
    bffh = sb("bffh", [128, FT])
    gam_f = sb("gam_f", [128, DIT])
    dtb_c = sb("dtb_c", [H, 1])
    aneg = sb("aneg", [H, 1])
    dvall = sb("dvall", [128, DIT])
    ddiag = sb("ddiag", [128, DIT * 128], BF16)
    tri_b = sb("tri_b", [128, 128], BF16)
    id_b = sb("id_b", [128, 128], BF16)
    id_f = sb("id_f", [128, 128], F32)
    onesr2 = sb("onesr2", [1, 128])
    ones2r = sb("ones2r", [2, 128], BF16)
    ones_c = sb("ones_c", [128, 1], BF16)
    land = [sb(f"land{i}", [128, 2 * D]) for i in range(2)]
    x_fm = sb("x_fm", [128, FT * QC])
    x_fm2 = sb("x_fm2", [128, FT * QC])
    x_fms = [x_fm, x_fm2]
    scrA = sb("scrA", [128, H * QS], BF16)
    x_bf = scrA[:, FT * QC:2 * FT * QC]
    u1_f8 = sb("u1_f8", [128, FT * QC], FP8)
    scr2 = sb("scr2", [128, DI], BF16)
    u2_bf = sb("u2_bf", [128, FT * QC], BF16)
    sdec = u2_bf
    rb_t = sb("rb_t", [128, QC], BF16)
    mrb_t = sb("mrb_t", [128, QC], BF16)
    rows = sb("rows", [1, 768])
    xbcp = sb("xbcp", [128, CT * (QC + 4)], BF16)
    yfm = xbcp[:, 0:DIT * QC]
    cacc = sb("cacc", [128, CT * QC], BF16)
    z_si = sb("z_si", [128, NSUB * DI], BF16)
    dt_f = sb("dt_f", [H, QC])
    a_f = sb("a_f", [H, QC])
    cum_f = sb("cum_f", [H, QC])
    ccf = a_f
    dt_tm = sb("dt_tm", [128, NSUB * H], BF16)
    cum_tm = sb("cum_tm", [128, NSUB * H])
    cexp_tm = sb("cexp_tm", [128, NSUB * H], BF16)
    cc_tm = sb("cc_tm", [128, NSUB * H], BF16)
    ccq_b = sb("ccq_b", [128, NSUB * H], BF16)
    stri = sb("stri", [128, QS], BF16)
    xdt = sb("xdt", [128, DI], BF16)
    btm = sb("btm", [128, 128], BF16)
    s_bf = sb("s_bf", [128, DI], BF16)
    y_bf = sb("y_bf", [128, DI], BF16)
    ssq = sb("ssq", [128, 1])
    rms_c = sb("rms_c", [128, 1])
    tl3 = sb("tl3", [128, CT * 3], BF16)
    badd_z = sb("badd_z", [1, DI], BF16)
    badd_fm = sb("badd_fm", [128, 19])
    lnb_b = sb("lnb_b", [128, FT], BF16)
    trin_b = sb("trin_b", [128, 128], BF16)
    rb1_t = sb("rb1_t", [128, QC], BF16)
    mu1_t = sb("mu1_t", [128, QC], BF16)
    cumhi = sb("cumhi", [H, QC], BF16)
    cumlo = sb("cumlo", [H, QC], BF16)
    neg_id32 = sb("neg_id32", [32, 32], BF16)
    cst = sb("cst", [128, 16])
    _cvals = [EPS, 1.0, -0.5, -1.0, 1.0 / DI, 0.5, 448.0, -448.0,
              EPS * EPS]
    for _i, _v in enumerate(_cvals):
        nc.const_aps.register(F32, _v, cst[:, _i:_i + 1])
    pA = es.enter_context(nc.psum_tensor("pA", [128, 2048], F32))
    pB = es.enter_context(nc.psum_tensor("pB", [128, 2048], F32))

    blk = es.enter_context(nc.Block())

    _lt = [None]

    def mm(t, out, lhsT, rhs, first, last=True, perf_mode=None):
        i = t.matmul(out, lhsT, rhs, start=first, stop=last,
                     skip_group_check=True, perf_mode=perf_mode)
        _lt[0] = i
        return i

    DR = mybir.MatmulPerfMode.DoubleRow

    def ffn_mms(t, pp):
        for ct in range(FT):
            for k2 in range(4):
                mm(t, pp[:, ct * QC:(ct + 1) * QC],
                   bass.AP(wfft, 2 * k2 * D + ct * 128,
                           [[FT * D, 128], [D, 2], [1, 128]]),
                   bass.AP(u1_f8, 2 * k2 * QC,
                           [[FT * QC, 128], [QC, 2], [1, QC]]),
                   first=(ct % 2 == 0 and k2 == 0), last=(k2 == 3),
                   perf_mode=DR)

    def tr(t, out, in_, ident):
        i = t.transpose(out, in_, ident)
        _lt[0] = i
        return i

    wip = lambda i, a, b: wipt[:, i * DIP + a: i * DIP + b]
    wopd = lambda dk, a, b: wopt[:, dk * D + a: dk * D + b]
    wff = lambda i, a, b: wfft[:, i * D + a: i * D + b]

    def p_sync0(y):
        def dma(dst, src, tag="wsm"):
            S.inc(y.dma_start(out=dst, in_=src,
                              allow_slow_non_contiguous=True), tag, 16)
        dma(lnw_f[:], bass.AP(lnw_h.tensor, 0, [[1, 128], [128, FT]]))
        dma(lnb_f[:], bass.AP(lnb_h.tensor, 0, [[1, 128], [128, FT]]))
        dma(bffh[:], bass.AP(bff_h.tensor, 0, [[1, 128], [128, FT]]))
        dma(gam_f[:], bass.AP(gam_h.tensor, 0, [[1, 128], [128, DIT]]))
        dma(convb[:], bass.AP(cb_h.tensor, 0, [[1, 128], [128, CT]]))
        dma(convw[:], bass.AP(cw_h.tensor, 0, [[4, 128], [512, CT], [1, 4]]))
        dma(dtb_c[:], bass.AP(dtb_h.tensor, 0, [[1, H], [1, 1]]))
        dma(aneg[:], bass.AP(al_h.tensor, 0, [[1, H], [1, 1]]))
        for i in range(DIT):
            dma(dvall[0:64, i:i + 1],
                bass.AP(dv_h.tensor, 2 * i, [[0, 64], [1, 1]]))
            dma(dvall[64:128, i:i + 1],
                bass.AP(dv_h.tensor, 2 * i + 1, [[0, 64], [1, 1]]))
        for tt in range(2):
            dma(land[0][:, tt * D:(tt + 1) * D],
                x_h[tt * 128:(tt + 1) * 128, :], "xinA")

    blk.sync(p_sync0)

    def p_gp(g):
        f = land[1][:, 1024:1152]
        g.memset(f, 1.0)
        g.drain()
        g.affine_select(f, f, pattern=[[1, 128]], compare_op=ALU.is_ge,
                        fill=0.0, base=0, channel_multiplier=-1)
        g.drain()
        g.tensor_copy(tri_b[:], f)
        g.drain()
        g.affine_select(f, f, pattern=[[-1, 128]], compare_op=ALU.is_ge,
                        fill=0.0, base=0, channel_multiplier=1)
        g.drain()
        g.tensor_copy(id_b[:], f)
        g.tensor_copy(id_f[:], f)
        g.drain()
        g.memset(f, 0.0)
        g.drain()
        g.affine_select(f, f, pattern=[[-1, 128]], compare_op=ALU.is_ge,
                        fill=-1.0e6, base=0, channel_multiplier=1)
        g.drain()
        g.tensor_copy(trin_b[:], f)
        for _i, _v in enumerate(_cvals):
            g.memset(cst[:, _i:_i + 1], _v)
        g.memset(onesr2[0:1, 0:128], 1.0)
        g.memset(ones2r[0:2, 0:128], 1.0)
        g.memset(neg_id32[0:32, 0:32], -1.0)
        g.drain()
        g.affine_select(neg_id32[:], neg_id32[:], pattern=[[1, 32]],
                        compare_op=ALU.is_ge, fill=0.0, base=0,
                        channel_multiplier=-1)
        g.drain()
        g.affine_select(neg_id32[:], neg_id32[:], pattern=[[-1, 32]],
                        compare_op=ALU.is_ge, fill=0.0, base=0,
                        channel_multiplier=1)
        g.drain()
        g.memset(ones_c[:], 1.0 / D)
        g.memset(s_bf[:], 0.0)
        for ct in range(CT):
            g.memset(xbcp[:, ct * (QC + 4):ct * (QC + 4) + 3], 0.0)
        g.drain()
        S.inc(g.drain(), "gpc", 1)

    blk.gpsimd(p_gp)

    def p_v0(v):
        S.wait(v, "gpc")
        S.wait(v, "wsm")
        v.tensor_tensor(
            ddiag[:].rearrange("p (i c) -> p i c", i=DIT),
            rep_outer(id_b[:], DIT),
            rep_inner(dvall[:], 128), ALU.mult)
        v.drain()
        S.inc(v.drain(), "dcon", 1)

    blk.vector(p_v0)

    def p_s0(s):
        _load_set(s, SET_EXP_LN)
        S.wait(s, "wsm")
        s.activation(aneg[:], aneg[:], AF.Exp)
        s.drain()
        s.mul(aneg[:], aneg[:], -1.0)
        s.mul(bffh[:], bffh[:], 0.5)
        s.drain()
        sc1 = land[1][:, 1152:1152 + FT]
        sc2 = land[1][:, 1152 + FT:1152 + 2 * FT]
        s.activation(sc1, lnw_f[:], AF.Abs)
        s.drain()
        s.activation(sc1, sc1, AF.Ln)
        s.drain()
        s.activation(sc1, sc1, AF.Exp, scale=-1.0)
        s.activation(sc2, lnw_f[:], AF.Sign)
        s.drain()
        S.inc(s.drain(), "acon", 1)

    blk.scalar(p_s0)

    w_jobs = [("ff", j) for j in range(FT)]
    w_jobs += [("ip", j) for j in range(35)]
    for j in range(FT):
        w_jobs += [("opA", j), ("opB", j)]

    def p_syncw(y):
        S.wait(y, "gpc")
        S.wait(y, "acon")
        for n, (kind, j) in enumerate(w_jobs):
            if n >= 1:
                y.wait_ge(S._get("stgf"), n)
            if kind == "ip" and j == 34:
                dst = land[1][0:32, 0:D]
                src = wip_h[34 * 128:DIP, :]
            else:
                dst = land[1][:, 0:D]
                if kind == "ff":
                    src = wff_h[j * 128:(j + 1) * 128, :]
                elif kind == "ip":
                    src = wip_h[j * 128:(j + 1) * 128, :]
                elif kind == "opA":
                    src = wop_h[j * 128:(j + 1) * 128, 0:D]
                else:
                    src = wop_h[j * 128:(j + 1) * 128, D:2 * D]
            S.inc(y.dma_start(out=dst, in_=src), "wst", 16)

    blk.sync(p_syncw)

    def p_tw(t):
        for n, (kind, j) in enumerate(w_jobs):
            t.wait_ge(S._get("wst"), 16 * (n + 1))
            if n >= 2:
                t.wait_ge(S._get("wev"), n - 1)
            src = land[1][:, 0:D]
            pp = pA if n % 2 == 0 else pB
            rows_in = 32 if (kind == "ip" and j == 34) else 128
            for i in range(FT):
                off = (i // 4) * 512 + (i % 4) * 128
                tr(t, pp[:, off:off + rows_in],
                            src[0:rows_in, i * 128:(i + 1) * 128],
                            id_f[0:rows_in, 0:rows_in])
            S.inc(_lt[0], "wtr", 1)
            S.inc(t.drain(), "stgf", 1)

        t.wait_ge(S._get("wev"), len(w_jobs))
        for tt2 in range(19):
            a0 = DI + tt2 * 128
            b0 = min(a0 + 128, DIP)
            for k in range(FT):
                mm(t, pA[0:b0 - a0, tt2:tt2 + 1], wip(k, a0, b0),
                   lnb_b[:, k:k + 1], first=(k == 0), last=(k == FT - 1))
        for nb in range(4):
            for k in range(FT):
                mm(t, pB[0:1, nb * 512:(nb + 1) * 512], lnb_b[:, k:k + 1],
                   wip(k, nb * 512, (nb + 1) * 512),
                   first=(k == 0), last=(k == FT - 1))
        S.inc(_lt[0], "badp", 1)

    blk.tensor(p_tw)

    def p_vw(v):
        S.wait(v, "acon")
        v.tensor_tensor(lnb_b[:], lnb_f[:], land[1][:, 1152:1152 + FT],
                        ALU.mult)
        v.drain()
        v.tensor_tensor(lnb_b[:], lnb_b[:],
                        land[1][:, 1152 + FT:1152 + 2 * FT], ALU.mult)
        v.drain()
        for n, (kind, j) in enumerate(w_jobs):
            v.wait_ge(S._get("wtr"), n + 1)
            pp = pA if n % 2 == 0 else pB
            rows_in = 32 if (kind == "ip" and j == 34) else 128
            for bank in range(2):
                base, i0 = bank * 512, bank * 4
                view = pp[:, base:base + 512].rearrange(
                    "p (i c) -> p i c", i=4)[:, :, 0:rows_in]
                if kind == "ff":
                    dstv = wfft[:].rearrange("p (i c) -> p i c", i=FT)[
                        :, i0:i0 + 4, j * 128:j * 128 + rows_in]
                    v.tensor_scalar(dstv, view, 448.0, -448.0,
                                    ALU.min, ALU.max)
                elif kind == "ip":
                    for ii in range(4):
                        dstv = wipt[:].rearrange("p (i c) -> p i c", i=FT)[
                            :, i0 + ii, j * 128:j * 128 + rows_in]
                        v.tensor_scalar(
                            dstv, pp[:, base + ii * 128:
                                     base + ii * 128 + rows_in],
                            lnw_f[:, i0 + ii:i0 + ii + 1], None, ALU.mult)
                else:
                    dk0 = (0 if kind == "opA" else 8) + i0
                    dstv = wopt[:].rearrange("p (i c) -> p i c", i=DIT)[
                        :, dk0:dk0 + 4, j * 128:j * 128 + rows_in]
                    v.tensor_tensor(dstv, view,
                                    rep_inner(gam_f[:, dk0:dk0 + 4], rows_in),
                                    ALU.mult)
            v.drain()
            S.inc(v.drain(), "wev", 1)
        S.wait(v, "badp")
        S.wait(v, "gpc")
        v.tensor_copy(badd_fm[:], pA[:, 0:19])
        for nb in range(4):
            v.tensor_copy(badd_z[0:1, nb * 512:(nb + 1) * 512],
                          pB[0:1, nb * 512:(nb + 1) * 512])
        v.drain()
        cwv = convw[:].rearrange("p (i k) -> p i k", i=CT)
        ws = land[1][:, 1024:1024 + CT]
        ws2 = land[1][:, 1024 + CT:1024 + 2 * CT]
        v.tensor_tensor(ws, cwv[:, :, 0], cwv[:, :, 1], ALU.add)
        v.tensor_tensor(ws2, cwv[:, :, 2], cwv[:, :, 3], ALU.add)
        v.drain()
        v.tensor_tensor(ws, ws, ws2, ALU.add)
        v.drain()
        v.tensor_tensor(ws, ws, badd_fm[:, 0:CT], ALU.mult)
        v.drain()
        v.tensor_tensor(convb[:], convb[:], ws, ALU.add)
        v.tensor_tensor(dtb_c[:], dtb_c[:], badd_fm[0:H, CT:CT + 1], ALU.add)
        v.tensor_scalar(
            xbcp[:].rearrange("p (i t) -> p i t", i=CT)[:, :, 0:3],
            rep_inner(badd_fm[:, 0:CT], 3), -1.0, None, ALU.mult)
        v.drain()
        S.inc(v.drain(), "wrdy", 1)

    blk.vector(p_vw)

    def xv(buf):
        return buf[:].rearrange("p (i t) -> p i t", i=FT)

    def emit_ln_stats(xmark, xf, po=0):
        def s_x2(s):
            S.wait(s, xmark)
            S.wait(s, "upre", lag=0)
            S.wait(s, "xres", lag=0)
            s.copy(x_bf, xf[:])
            s.drain()
            S.inc(s.drain(), "xbf", 1)
            S.wait(s, "stp", lag=0)
            s.activation(scrA[:, 0:FT * QC], x_bf, AF.Square)
            s.drain()
            S.inc(s.drain(), "x2", 1)

        def t_stats(t):
            S.wait(t, "xbf")
            S.wait(t, "x2")
            S.wait(t, "pBfree", lag=0)
            for k in range(FT):
                mm(t, pB[0:1, po:po + 512], ones_c[:],
                   bass.AP(scrA, k * QC,
                           [[H * QS, 128], [FT * QC, 2], [1, QC]]),
                   first=(k == 0), last=(k == FT - 1))
            S.inc(_lt[0], "stp", 1)

        blk.scalar(s_x2)
        blk.tensor(t_stats)

    def emit_ln_rest(fused, udst_f8, xf, po=0, ugate=None):
        def s_m1(s):
            S.wait(s, "stp")
            s.activation(rows[0:1, 0:256], pB[0:1, po + 256:po + 512],
                         AF.Square)
            s.drain()
            S.inc(s.drain(), "m1", 1)

        def v_m1(v):
            S.wait(v, "m1")
            v.tensor_tensor(rows[0:1, 256:512], pB[0:1, po:po + 256],
                            rows[0:1, 0:256], ALU.subtract)
            v.drain()
            S.inc(v.drain(), "m2", 1)

        blk.scalar(s_m1)
        blk.vector(v_m1)

        if not fused:
            def s_m2(s):
                S.wait(s, "m2")
                s.activation(rows[0:1, 0:256], rows[0:1, 256:512],
                             AF.Ln, bias=EPS)
                s.drain()
                s.activation(rows[0:1, 0:256], rows[0:1, 0:256],
                             AF.Exp, scale=-0.5)
                s.copy(rows[0:1, 256:512], pB[0:1, po + 256:po + 512])
                s.drain()
                S.inc(s.drain(), "m4", 1)

            blk.scalar(s_m2)
        else:
            def s_m2(s):
                S.wait(s, "m2")
                s.activation(rows[0:1, 512:768], rows[0:1, 256:512],
                             AF.Ln, bias=EPS)
                s.drain()
                s.activation(rows[0:1, 256:512], rows[0:1, 256:512],
                             AF.Ln, scale=1.0 + EPS, bias=EPS * EPS)
                s.drain()
                s.activation(rows[0:1, 512:768], rows[0:1, 512:768],
                             AF.Exp, scale=-0.5)
                s.activation(rows[0:1, 256:512], rows[0:1, 256:512],
                             AF.Exp, scale=-0.5)
                s.drain()
                s.copy(rows[0:1, 0:256], pB[0:1, po + 256:po + 512])
                s.drain()
                S.inc(s.drain(), "m5", 1)

            def t_bcf(t):
                S.wait(t, "m5")
                mm(t, pB[:, 512:1024], onesr2[0:1, 0:128],
                   rows[0:1, 0:512], first=True)
                mm(t, pB[:, 1024:1280], onesr2[0:1, 0:128],
                   rows[0:1, 512:768], first=True)
                S.inc(_lt[0], "bcp", 1)
                S.inc(t.drain(), "pBfree", 1)

            def v_bcf(v):
                S.wait(v, "bcp")
                v.tensor_copy(mu1_t[:], pB[:, 512:768])
                v.tensor_copy(rb_t[:], pB[:, 768:1024])
                v.tensor_copy(rb1_t[:], pB[:, 1024:1280])
                v.drain()
                v.tensor_tensor(mrb_t[:], mu1_t[:], rb_t[:], ALU.mult)
                v.drain()
                S.inc(v.drain(), "bce", 1)
                S.inc(v.drain(), "pBfree", 1)

            blk.scalar(s_m2)
            blk.tensor(t_bcf)
            blk.vector(v_bcf)

        if not fused:
            def t_bc(t):
                S.wait(t, "m4")
                mm(t, pB[:, po + 512:po + 1024], onesr2[0:1, 0:128],
                   rows[0:1, 0:512], first=True)
                S.inc(_lt[0], "bcp", 1)
                S.inc(t.drain(), "pBfree", 1)

            def v_bc(v):
                S.wait(v, "bcp")
                v.tensor_copy(rb_t[:], pB[:, po + 512:po + 768])
                v.tensor_copy(mu1_t[:], pB[:, po + 768:po + 1024])
                v.drain()
                v.tensor_tensor(mrb_t[:], mu1_t[:], rb_t[:], ALU.mult)
                v.drain()
                S.inc(v.drain(), "bce", 1)
                S.inc(v.drain(), "pBfree", 1)

            blk.tensor(t_bc)
            blk.vector(v_bc)

        def v_u(v):
            S.wait(v, "bce")
            if ugate is not None:
                S.wait(v, ugate)
            for i in range(FT):
                v.tensor_tensor(scr2[:, i * QC:(i + 1) * QC],
                                x_bf[:, i * QC:(i + 1) * QC], rb_t[:],
                                ALU.mult)
            v.drain()
            if udst_f8:
                for i in range(FT):
                    v.tensor_tensor(u1_f8[:, i * QC:(i + 1) * QC],
                                    scr2[:, i * QC:(i + 1) * QC], mrb_t[:],
                                    ALU.subtract)
                v.drain()
                S.inc(v.drain(), "ur", 1)
            else:
                for i in range(FT):
                    v.tensor_tensor(u2_bf[:, i * QC:(i + 1) * QC],
                                    scr2[:, i * QC:(i + 1) * QC], mrb_t[:],
                                    ALU.subtract)
                v.drain()
                S.inc(v.drain(), "ur", 1)
            S.inc(v.drain(), "upre", 1)

        blk.vector(v_u)

    def emit_ln(tag, fused, udst_f8, xmark, xf, po=0, ugate=None):
        emit_ln_stats(xmark, xf, po)
        emit_ln_rest(fused, udst_f8, xf, po, ugate)

    def emit_xtr(cn):
        lb = land[cn % 2]
        xfn = x_fms[cn % 2]
        for w in range(2):
            def xw_t(t, w=w, cn=cn):
                if w == 0:
                    S.wait(t, "xinA" if cn % 2 == 0 else "xinB")
                    if cn == 0:
                        S.wait(t, "wrdy")
                        S.wait(t, "dcon")
                    else:
                        S.wait(t, "supd", lag=0)
                else:
                    S.wait(t, "xev")
                for i in range(FT):
                    tr(t, pB[:, 1024 + i * 128:1024 + (i + 1) * 128],
                       lb[:, w * D + i * 128:w * D + (i + 1) * 128], id_f[:])
                S.inc(_lt[0], "xw", 1)
                if w == 1:
                    S.inc(t.drain(), "xtrp", 1)

            def xw_s(s, w=w, cn=cn):
                S.wait(s, "xw")
                dst = xfn[:].rearrange("p (i t) -> p i t", i=FT)[
                    :, :, w * 128:(w + 1) * 128]
                src_ = pB[:, 1024:2048].rearrange("p (i t) -> p i t", i=FT)
                s.copy(dst, src_)
                s.drain()
                S.inc(s.drain(), "xev", 1)
                if w == 1:
                    S.inc(s.drain(), "xin1", 1)

            blk.tensor(xw_t)
            blk.scalar(xw_s)

    def emit_front2(cn, c0=False):
        xfn = x_fms[cn % 2]

        f1c_b = S.c.get("f1c", 0)
        f1ec_b = S.c.get("f1ec", 0)
        xin2c_b = S.c.get("xin2c", 0)
        x2c2_b = S.c.get("x2c2", 0)

        def ph_ffn1_t(t, cn=cn):
            S.wait(t, "ur")
            if c0:
                S.wait(t, "pBfree", lag=0)
            else:
                S.wait(t, "bce", lag=0)
                S.wait(t, "pBfree", lag=0)
            for ct in range(FT):
                for k2 in range(4):
                    mm(t, pB[:, ct * QC:(ct + 1) * QC],
                       bass.AP(wfft, 2 * k2 * D + ct * 128,
                               [[FT * D, 128], [D, 2], [1, 128]]),
                       bass.AP(u1_f8, 2 * k2 * QC,
                               [[FT * QC, 128], [QC, 2], [1, QC]]),
                       first=(ct % 2 == 0 and k2 == 0), last=(k2 == 3),
                       perf_mode=DR)
                if ct % 2 == 1:
                    S.inc(_lt[0], "f1c", 1)

        def ph_ffn1_ep(s, cn=cn, f1c_b=f1c_b):
            lt = land[cn % 2]
            for p in range(4):
                s.wait_ge(S._get("f1c"), f1c_b + p + 1)
                for ct in (2 * p, 2 * p + 1):
                    s.activation(lt[:, ct * QC:(ct + 1) * QC],
                                 pB[:, ct * QC:(ct + 1) * QC], AF.Identity,
                                 scale=0.5, bias=bffh[:, ct:ct + 1])
                s.drain()
                S.inc(s.drain(), "f1ec", 1)
            S.inc(s.drain(), "pBfree", 1)

        def ph_ffn1_v(v, cn=cn, f1ec_b=f1ec_b):
            lt = land[cn % 2]
            for p in range(4):
                v.wait_ge(S._get("f1ec"), f1ec_b + p + 1)
                v.tensor_tensor(xfn[:, p * 512:(p + 1) * 512],
                                xfn[:, p * 512:(p + 1) * 512],
                                lt[:, p * 512:(p + 1) * 512], ALU.add)
                v.drain()
                S.inc(v.drain(), "xin2c", 1)
            S.inc(v.drain(), "xin2", 1)

        def ln2_x2_s(s, xin2c_b=xin2c_b):
            S.wait(s, "upre", lag=0)
            S.wait(s, "xres", lag=0)
            S.wait(s, "stp", lag=0)
            for p in range(4):
                s.wait_ge(S._get("xin2c"), xin2c_b + p + 1)
                s.copy(x_bf[:, p * 512:(p + 1) * 512],
                       xfn[:, p * 512:(p + 1) * 512])
                s.drain()
                s.activation(scrA[:, p * 512:(p + 1) * 512],
                             x_bf[:, p * 512:(p + 1) * 512], AF.Square)
                s.drain()
                S.inc(s.drain(), "x2c2", 1)
            S.inc(s.drain(), "xbf", 1)
            S.inc(s.drain(), "x2", 1)

        def ln2_stats_t(t, x2c2_b=x2c2_b):
            for p in range(4):
                t.wait_ge(S._get("x2c2"), x2c2_b + p + 1)
                for k in (2 * p, 2 * p + 1):
                    mm(t, pB[0:1, 0:512], ones_c[:],
                       bass.AP(scrA, k * QC,
                               [[H * QS, 128], [FT * QC, 2], [1, QC]]),
                       first=(k == 0), last=(k == FT - 1))
            S.inc(_lt[0], "stp", 1)

        blk.tensor(ph_ffn1_t)
        blk.scalar(ph_ffn1_ep)
        blk.vector(ph_ffn1_v)
        blk.scalar(ln2_x2_s)
        blk.tensor(ln2_stats_t)

    XB = QC + 4
    for c in range(nch):
        tagc = str(c)
        xf = x_fms[c % 2]

        if c == 0 and on("xtr"):
            emit_xtr(0)
            if on("ln1"):
                emit_ln("1", fused=False, udst_f8=True, xmark="xin1",
                        xf=xf, po=1024, ugate="f2p")
            emit_front2(0, c0=True)
            emit_ln_rest(fused=True, udst_f8=False, xf=xf, po=0)

        if c + 1 < nch:
            def ph_load(y, c=c):
                S.wait(y, "st", lag=0)
                S.wait(y, "xtrp", lag=0)
                if c == 0:
                    S.wait(y, "wrdy")
                for tt in range(2):
                    r0 = (c + 1) * QC + tt * 128
                    S.inc(y.dma_start(
                        out=land[(c + 1) % 2][:, tt * D:(tt + 1) * D],
                        in_=x_h[r0:r0 + 128, :]),
                        "xinA" if (c + 1) % 2 == 0 else "xinB", 16)
            blk.sync(ph_load)

        if not on("ipx"):
            continue
        if not on("dtc"):
            continue
        def ph_dt_t(t):
            S.wait(t, "ur")
            S.wait(t, "pAfree", lag=0)
            for k in range(FT):
                mm(t, pA[0:H, 0:QC], wip(k, DIP - H, DIP),
                   xv(u2_bf)[:, k, :], first=(k == 0), last=(k == FT - 1))
            S.inc(_lt[0], "dtp", 1)

        blk.tensor(ph_dt_t)

        groups = [list(range(4 * g, min(4 * g + 4, CT))) for g in range(5)]
        ipxe_b = S.c.get("ipxe", 0)
        ipxp_b = S.c.get("ipxp", 0)

        def ph_ipx_t_a(t):
            S.wait(t, "ur")
            S.wait(t, "pBfree", lag=0)
            for g in (0, 1):
                base = (g % 2) * 1024
                for n, ct in enumerate(groups[g]):
                    for k in range(FT):
                        mm(t, pB[:, base + n * QC:base + (n + 1) * QC],
                           wip(k, DI + ct * 128, DI + (ct + 1) * 128),
                           xv(u2_bf)[:, k, :],
                           first=(n % 2 == 0 and k == 0),
                           last=(k == FT - 1))
                S.inc(_lt[0], "ipxp", 1)

        blk.tensor(ph_ipx_t_a)

        def ph_dt_s(s):
            S.wait(s, "dtp")
            s.activation(a_f[:], pA[0:H, 0:QC], AF.Abs, bias=dtb_c[:, 0:1])
            s.drain()
            s.activation(a_f[:], a_f[:], AF.Exp, scale=-1.0)
            s.drain()
            s.activation(a_f[:], a_f[:], AF.Ln, bias=1.0)
            s.drain()
            s.activation(dt_f[:], pA[0:H, 0:QC], AF.Relu, bias=dtb_c[:, 0:1])
            s.drain()
            S.inc(s.drain(), "dtsp", 1)

        def ph_dt_v(v):
            S.wait(v, "dtsp")
            v.tensor_tensor(dt_f[:], dt_f[:], a_f[:], ALU.add)
            v.drain()
            S.inc(v.drain(), "pAfree", 1)
            v.tensor_scalar(a_f[:], dt_f[:], aneg[:, 0:1], None, ALU.mult)
            v.drain()
            for sbi in range(NSUB):
                sl = slice(sbi * QS, (sbi + 1) * QS)
                v.tensor_tensor_scan(cum_f[:, sl], a_f[:, sl], a_f[:, sl],
                                     0.0, ALU.add, ALU.bypass)
            v.drain()
            S.inc(v.drain(), "cumr", 1)

        blk.scalar(ph_dt_s)
        blk.vector(ph_dt_v)

        def ph_xres_v(v, c=c):
            S.wait(v, "ur", lag=0)
            S.wait(v, "st", lag=0)
            lt = land[c % 2][:].rearrange("p (i t) -> p i t", i=FT)
            xb3 = x_bf.rearrange("p (i t) -> p i t", i=FT)
            v.tensor_tensor(lt, xb3, rep_outer(mu1_t[:], FT), ALU.subtract)
            v.drain()
            v.tensor_tensor(lt, lt, rep_outer(rb1_t[:], FT), ALU.mult)
            v.drain()
            v.tensor_tensor(xf[:], xf[:], land[c % 2][:], ALU.add)
            v.drain()
            S.inc(v.drain(), "xres", 1)

        blk.vector(ph_xres_v)

        def ph_chl_g(g):
            S.wait(g, "cumr")
            g.tensor_copy(cumhi[:], cum_f[:])
            g.drain()
            g.tensor_tensor(cumlo[:], cum_f[:], cumhi[:], ALU.subtract)
            g.drain()

        blk.gpsimd(ph_chl_g)

        def ph_cc_s(s):
            S.wait(s, "cumr")
            for sbi in range(NSUB):
                sl = slice(sbi * QS, (sbi + 1) * QS)
                s.activation(ccf[:, sl], cum_f[:, sl], AF.Exp, scale=-1.0,
                             bias=cum_f[:, (sbi + 1) * QS - 1:(sbi + 1) * QS])
            s.drain()
            S.inc(s.drain(), "ccfr", 1)

        blk.scalar(ph_cc_s)

        def ph_tt_t(t):
            S.wait(t, "ccfr")
            S.wait(t, "pAfree", lag=0)
            for sbi in range(NSUB):
                sl0 = sbi * QS
                tr(t, pA[:, sbi * 512 + 0:sbi * 512 + H],
                            dt_f[:, sl0:sl0 + QS], id_f[0:H, 0:H])
                tr(t, pA[:, sbi * 512 + 128:sbi * 512 + 128 + H],
                            cum_f[:, sl0:sl0 + QS], id_f[0:H, 0:H])
                tr(t, pA[:, sbi * 512 + 256:sbi * 512 + 256 + H],
                            ccf[:, sl0:sl0 + QS], id_f[0:H, 0:H])
                tr(t, pA[0:1, sbi * 512 + 384:sbi * 512 + 384 + H],
                            cum_f[:, sl0 + QS - 1:sl0 + QS], id_f[0:H, 0:H])
            S.inc(_lt[0], "dttp", 1)

        def ph_tt_v(v):
            S.wait(v, "dttp")
            for sbi in range(NSUB):
                hs = slice(sbi * H, (sbi + 1) * H)
                v.tensor_copy(dt_tm[:, hs], pA[:, sbi * 512 + 0:sbi * 512 + H])
                v.tensor_copy(cum_tm[:, hs],
                              pA[:, sbi * 512 + 128:sbi * 512 + 128 + H])
                v.tensor_copy(cc_tm[:, hs],
                              pA[:, sbi * 512 + 256:sbi * 512 + 256 + H])
                v.tensor_copy(rows[0:1, 512 + 32 * sbi:512 + 32 * sbi + H],
                              pA[0:1, sbi * 512 + 384:sbi * 512 + 384 + H])
            v.drain()
            S.inc(v.drain(), "dtte", 1)

        blk.tensor(ph_tt_t)
        blk.vector(ph_tt_v)

        def ph_cq_s(s):
            S.wait(s, "dtte")
            s.activation(cexp_tm[:], cum_tm[:], AF.Exp)
            for sbi in range(NSUB):
                s.activation(rows[0:1, 512 + 32 * sbi:512 + 32 * sbi + H],
                             rows[0:1, 512 + 32 * sbi:512 + 32 * sbi + H],
                             AF.Exp)
            s.drain()
            S.inc(s.drain(), "cqe", 1)

        def ph_cq_t(t):
            S.wait(t, "cqe")
            for sbi in range(NSUB):
                mm(t, pA[:, 1024 + sbi * 128:1024 + sbi * 128 + H],
                   onesr2[0:1, 0:128],
                   rows[0:1, 512 + 32 * sbi:512 + 32 * sbi + H],
                   first=True)
            S.inc(_lt[0], "cqp", 1)

        def ph_cq_v(v):
            S.wait(v, "cqp")
            for sbi in range(NSUB):
                v.tensor_copy(ccq_b[:, sbi * H:(sbi + 1) * H],
                              pA[:, 1024 + sbi * 128:1024 + sbi * 128 + H])
            v.drain()
            S.inc(v.drain(), "cqv", 1)
            S.inc(v.drain(), "pAfree", 1)

        blk.scalar(ph_cq_s)
        blk.tensor(ph_cq_t)
        blk.vector(ph_cq_v)

        def ph_halo_v(v, c=c):
            if c > 0:
                S.wait(v, "cvp", lag=0)
                v.tensor_copy(
                    xbcp[:].rearrange("p (i t) -> p i t", i=CT)[:, :, 0:3],
                    tl3[:].rearrange("p (i t) -> p i t", i=CT))
                v.drain()
            S.inc(v.drain(), "halo", 1)

        blk.vector(ph_halo_v)

        def ph_ipx_t(t, ipxe_b=ipxe_b):
            for g in (2, 3, 4):
                t.wait_ge(S._get("ipxe"), ipxe_b + g - 1)
                base = (g % 2) * 1024
                for n, ct in enumerate(groups[g]):
                    for k in range(FT):
                        mm(t, pB[:, base + n * QC:base + (n + 1) * QC],
                           wip(k, DI + ct * 128, DI + (ct + 1) * 128),
                           xv(u2_bf)[:, k, :],
                           first=(n % 2 == 0 and k == 0),
                           last=(k == FT - 1))
                S.inc(_lt[0], "ipxp", 1)

        def ph_ipx_s(s, ipxp_b=ipxp_b):
            S.wait(s, "halo")
            S.wait(s, "opjp", lag=0)
            for g, cts in enumerate(groups):
                s.wait_ge(S._get("ipxp"), ipxp_b + g + 1)
                base = (g % 2) * 1024
                dst = xbcp[:].rearrange("p (i t) -> p i t", i=CT)[
                    :, cts[0]:cts[0] + len(cts), 3:3 + QC]
                src_ = pB[:, base:base + len(cts) * QC].rearrange(
                    "p (i t) -> p i t", i=len(cts))
                s.copy(dst, src_)
                s.drain()
                S.inc(s.drain(), "ipxe", 1)

        blk.tensor(ph_ipx_t)
        blk.scalar(ph_ipx_s)

        if not on("z"):
            continue
        def ph_z_t(t):
            S.wait(t, "ur")
            S.wait(t, "pAfree", lag=0)
            for zb in range(4):
                for k in range(FT):
                    mm(t, pA[:, zb * 512:(zb + 1) * 512],
                       u2_bf[:, k * QC:k * QC + QS],
                       wip(k, zb * 512, (zb + 1) * 512),
                       first=(k == 0), last=False)
                mm(t, pA[:, zb * 512:(zb + 1) * 512], ones2r[0:1, 0:128],
                   badd_z[0:1, zb * 512:(zb + 1) * 512],
                   first=False, last=True)
            S.inc(_lt[0], "z0p", 1)
            S.wait(t, "ipxe", lag=0)
            S.wait(t, "dtte", lag=0)
            for zb in range(4):
                for k in range(FT):
                    mm(t, pB[:, zb * 512:(zb + 1) * 512],
                       u2_bf[:, k * QC + QS:k * QC + 2 * QS],
                       wip(k, zb * 512, (zb + 1) * 512),
                       first=(k == 0), last=False)
                mm(t, pB[:, zb * 512:(zb + 1) * 512], ones2r[0:1, 0:128],
                   badd_z[0:1, zb * 512:(zb + 1) * 512],
                   first=False, last=True)
            S.inc(_lt[0], "z1p", 1)

        blk.tensor(ph_z_t)

        def ph_zev_s(s):
            S.wait(s, "z0p")
            _load_set(s, SET_SILU)
            s.activation(z_si[:, 0:DI], pA[:, :], AF.Silu)
            s.drain()
            S.inc(s.drain(), "z0e", 1)
            S.inc(s.drain(), "pAfree", 1)
            S.wait(s, "z1p")
            s.activation(z_si[:, DI:2 * DI], pB[:, :], AF.Silu)
            s.drain()
            S.inc(s.drain(), "z1e", 1)
            S.inc(s.drain(), "pBfree", 1)

        blk.scalar(ph_zev_s)

        if not on("conv"):
            continue
        def ph_conv_v(v, ipxe_b=ipxe_b):
            S.wait(v, "halo")
            xb = xbcp[:].rearrange("p (i t) -> p i t", i=CT)
            ca = cacc[:].rearrange("p (i t) -> p i t", i=CT)
            for g, cts in enumerate(groups):
                v.wait_ge(S._get("ipxe"), ipxe_b + g + 1)
                for k in range(4):
                    for ct in cts:
                        in1 = (rep_inner(convb[:, ct:ct + 1], QC)[:, 0, :]
                               if k == 0 else ca[:, ct, :])
                        v.scalar_tensor_tensor(
                            ca[:, ct, :], xb[:, ct, k:k + QC],
                            convw[:, ct * 4 + k:ct * 4 + k + 1], in1,
                            ALU.mult, ALU.add)
                    v.drain()
            v.tensor_copy(tl3[:].rearrange("p (i t) -> p i t", i=CT),
                          xbcp[:].rearrange("p (i t) -> p i t", i=CT)
                          [:, :, QC:QC + 3])
            v.drain()
            S.inc(v.drain(), "convd", 1)
            S.inc(v.drain(), "cvp", 1)

        blk.vector(ph_conv_v)

        if not on("silu"):
            continue
        def ph_silu_s(s):
            S.wait(s, "convd")
            s.activation(cacc[:], cacc[:], AF.Silu, bias=0.0)
            s.drain()
            S.inc(s.drain(), "xcsr", 1)
            _load_set(s, SET_EXP_LN)
            S.inc(s.drain(), "sexp", 1)

        blk.scalar(ph_silu_s)

        if not on("ssd"):
            continue
        Bts = [cacc[:, 16 * QC + i * QS:16 * QC + (i + 1) * QS]
               for i in range(NSUB)]
        Cts = [cacc[:, 17 * QC + i * QS:17 * QC + (i + 1) * QS]
               for i in range(NSUB)]

        def emit_cfl(sbi, extra=()):
            def ssd_cfl(g, sbi=sbi, extra=tuple(extra)):
                S.wait(g, "dtte")
                for nm in extra:
                    S.wait(g, nm, lag=0)
                for half in range(2):
                    hsl = slice(half * 16, (half + 1) * 16)
                    S.inc(g.dma_start(
                        out=scrA[0:1, half * DI:half * DI + 2048],
                        in_=cumhi[hsl, sbi * QS:(sbi + 1) * QS]), "cfl", 16)
                    S.inc(g.dma_start(
                        out=scrA[1:2, half * DI:half * DI + 2048],
                        in_=cumlo[hsl, sbi * QS:(sbi + 1) * QS]), "cfl", 16)
            blk.gpsimd(ssd_cfl)

        def mk_mm(t, sbi, half, pp):
            frows = scrA[0:2, half * DI:half * DI + 2048]
            tsl = slice(sbi * QS, (sbi + 1) * QS)
            for q in range(4):
                h0 = half * 16 + q * 4
                mm(t, pp[:, q * 512:(q + 1) * 512], ones2r[0:2, 0:128],
                   frows[0:2, q * 512:(q + 1) * 512], first=True, last=False)
                mm(t, pp[:, q * 512:(q + 1) * 512], trin_b[:],
                   rep_outer(id_b[:], 4), first=False, last=False)
                mm(t, pp[:, q * 512:(q + 1) * 512], cumhi[0:H, tsl],
                   rep_inner(neg_id32[:, h0:h0 + 4], 128),
                   first=False, last=False)
                mm(t, pp[:, q * 512:(q + 1) * 512], cumlo[0:H, tsl],
                   rep_inner(neg_id32[:, h0:h0 + 4], 128),
                   first=False, last=True)
            S.inc(_lt[0], "mkp%d" % sbi, 1)

        def mk_exp(s, sbi, half, pp):
            S.wait(s, "mkp%d" % sbi, lag=0)
            s.activation(scrA[:, half * DI:(half + 1) * DI], pp[:, :], AF.Exp)
            s.drain()
            S.inc(s.drain(), "mke%d" % sbi, 1)

        def mk0_t(t):
            S.wait(t, "cfl", lag=0)
            S.wait(t, "z0e", lag=0)
            mk_mm(t, 0, 0, pA)
            S.wait(t, "z1e", lag=0)
            mk_mm(t, 0, 1, pB)

        def mk0_s(s):
            S.wait(s, "sexp")
            S.wait(s, "stp", lag=0)
            S.wait(s, "upre", lag=0)
            S.wait(s, "xres", lag=0)
            mk_exp(s, 0, 0, pA)
            mk_exp(s, 0, 1, pB)

        emit_cfl(0, extra=("intrp", "x2", "stp", "upre", "xres"))
        blk.tensor(mk0_t)
        blk.scalar(mk0_s)

        def emit_sub(sbi):
            hs = slice(sbi * H, (sbi + 1) * H)
            Bt, Ctl = Bts[sbi], Cts[sbi]
            mk = "mke%d" % sbi

            def t1a(t, sbi=sbi, Bt=Bt, Ctl=Ctl, mk=mk):
                if sbi == 0:
                    S.wait(t, "xcsr")
                    S.wait(t, mk, lag=1)
                else:
                    S.wait(t, "yfme", lag=0)
                mm(t, pA[:, 0:QS], Bt, Ctl, first=True)
                S.inc(_lt[0], "scp", 1)
                mm(t, pA[:, 512:512 + QS], Bt, id_b[:], first=True)
                S.inc(_lt[0], "btp", 1)
                if sbi == 0:
                    S.wait(t, mk, lag=0)
                    S.wait(t, "z1e", lag=0)
                else:
                    S.wait(t, "supd", lag=0)
                    S.wait(t, "mke1", lag=0)
                for i in range(DIT):
                    mm(t, pB[:, i * 128:(i + 1) * 128],
                       cacc[:, i * QC + sbi * QS:i * QC + (sbi + 1) * QS],
                       id_b[:], first=(i % 4 == 0))
                S.inc(_lt[0], "xsp", 1)

            def v1a(v):
                S.wait(v, "scp")
                v.tensor_tensor(stri[:, 0:QS], pA[:, 0:QS], tri_b[:],
                                ALU.mult)
                v.drain()
                S.inc(v.drain(), "strir", 1)
                S.wait(v, "btp")
                v.tensor_copy(btm[:], pA[:, 512:512 + QS])
                v.drain()
                S.inc(v.drain(), "btme", 1)

            def v2w(v, mk=mk):
                S.wait(v, mk, lag=0)
                S.wait(v, "strir", lag=0)
                v.tensor_tensor(scrA[:].rearrange("p (h i) -> p h i", h=H),
                                scrA[:].rearrange("p (h i) -> p h i", h=H),
                                rep_outer(stri[:, 0:QS], H), ALU.mult)
                v.drain()
                S.inc(v.drain(), "wres", 1)

            def v1b(v, hs=hs):
                S.wait(v, "xsp")
                v.tensor_tensor(
                    xdt[:].rearrange("p (h e) -> p h e", h=H),
                    pB[:].rearrange("p (h e) -> p h e", h=H),
                    rep_inner(dt_tm[:, hs], HD), ALU.mult)
                v.drain()
                S.inc(v.drain(), "xdtr", 1)

            def t4(t, sbi=sbi, Ctl=Ctl):
                S.wait(t, "strir", lag=0)
                S.wait(t, "btme", lag=0)
                S.wait(t, "supd", lag=0)
                for b in range(4):
                    mm(t, pA[:, b * 512:(b + 1) * 512], Ctl,
                       s_bf[:, b * 512:(b + 1) * 512], first=True)
                S.inc(_lt[0], "intp", 1)
                S.wait(t, "wres", lag=0)
                S.wait(t, "xdtr", lag=0)
                for h in range(H):
                    mm(t, pB[:, h * HD:(h + 1) * HD],
                       scrA[:, h * 128:(h + 1) * 128],
                       xdt[:, h * HD:(h + 1) * HD],
                       first=(h % 8 == 0), last=False)
                for i in range(DIT):
                    mm(t, pB[:, i * 128:(i + 1) * 128],
                       cacc[:, i * QC + sbi * QS:i * QC + (sbi + 1) * QS],
                       ddiag[:, i * 128:(i + 1) * 128],
                       first=False, last=(i == DIT - 1))
                S.inc(_lt[0], "intrp", 1)

            def v4(v, sbi=sbi, hs=hs):
                S.wait(v, "intp")
                v.tensor_tensor(
                    scr2[:].rearrange("p (h e) -> p h e", h=H),
                    pA[:].rearrange("p (h e) -> p h e", h=H),
                    rep_inner(cexp_tm[:, hs], HD), ALU.mult)
                v.drain()
                S.inc(v.drain(), "s1r", 1)
                S.wait(v, "intrp")
                v.tensor_tensor(y_bf[:], pB[:, :], scr2[:], ALU.add)
                v.drain()
                S.inc(v.drain(), "yr", 1)
                v.tensor_tensor(scr2[:], y_bf[:],
                                z_si[:, sbi * DI:(sbi + 1) * DI], ALU.mult)
                v.drain()
                S.inc(v.drain(), "ygr", 1)
                v.tensor_tensor(
                    xdt[:].rearrange("p (h e) -> p h e", h=H),
                    xdt[:].rearrange("p (h e) -> p h e", h=H),
                    rep_inner(cc_tm[:, hs], HD), ALU.mult)
                v.drain()
                S.inc(v.drain(), "xdt2r", 1)

            def s5(s, c=c):
                S.wait(s, "ygr")
                S.wait(s, "xres", lag=0)
                s.activation(land[c % 2][:, 0:DI], scr2[:], AF.Square,
                             accum_out=ssq[:, 0:1])
                s.drain()
                s.activation(rms_c[:], ssq[:], AF.Ln, scale=1.0 / DI,
                             bias=EPS)
                s.drain()
                s.activation(rms_c[:], rms_c[:], AF.Exp, scale=-0.5)
                s.drain()
                s.activation(y_bf[:], scr2[:], AF.Identity,
                             scale=rms_c[:, 0:1])
                s.drain()
                S.inc(s.drain(), "ynr", 1)

            def t6a(t, sbi=sbi):
                S.wait(t, "ynr")
                S.wait(t, "s1r", lag=0)
                if sbi == 0:
                    S.wait(t, "mke1", lag=0)
                for i in range(DIT):
                    mm(t, pA[:, i * 128:(i + 1) * 128],
                       y_bf[:, i * 128:(i + 1) * 128], id_b[:],
                       first=(i % 4 == 0))
                S.inc(_lt[0], "ytp", 1)

            def t6b(t, sbi=sbi):
                S.wait(t, "xdt2r", lag=0)
                S.wait(t, "yr", lag=0)
                for b in range(4):
                    mm(t, pB[:, b * 512:(b + 1) * 512], btm[:],
                       xdt[:, b * 512:(b + 1) * 512], first=True)
                S.inc(_lt[0], "updp", 1)

            def s6(s, sbi=sbi):
                S.wait(s, "ytp")
                s.copy(yfm.rearrange("p (i t) -> p i t", i=DIT)
                       [:, :, sbi * QS:(sbi + 1) * QS],
                       pA[:].rearrange("p (i t) -> p i t", i=DIT))
                s.drain()
                S.inc(s.drain(), "yfme", 1)
                S.inc(s.drain(), "pAfree", 1)

            def v7a(v, hs=hs):
                v.tensor_tensor(
                    sdec[:].rearrange("p (h e) -> p h e", h=H),
                    s_bf[:].rearrange("p (h e) -> p h e", h=H),
                    rep_inner(ccq_b[:, hs], HD), ALU.mult)
                v.drain()
                S.inc(v.drain(), "sdcr", 1)

            def v7b(v):
                S.wait(v, "updp")
                v.tensor_tensor(s_bf[:], sdec[:], pB[:, :], ALU.add)
                v.drain()
                S.inc(v.drain(), "supd", 1)
                S.inc(v.drain(), "pBfree", 1)

            return t1a, v1a, v2w, v1b, t4, v4, s5, t6a, t6b, s6, v7a, v7b

        s0 = emit_sub(0)
        s1_ = emit_sub(1)

        blk.tensor(s0[0])
        blk.vector(s0[1])
        blk.vector(s0[2])
        blk.vector(s0[3])
        blk.tensor(s0[4])
        emit_cfl(1, extra=("intrp",))
        blk.vector(s0[5])
        def mk1_t0(t):
            S.wait(t, "cfl", lag=0)
            S.wait(t, "s1r", lag=0)
            mk_mm(t, 1, 0, pA)
        blk.tensor(mk1_t0)
        blk.scalar(s0[6])
        def mk1_s0(s):
            mk_exp(s, 1, 0, pA)
        blk.scalar(mk1_s0)
        blk.tensor(s0[8])
        blk.tensor(s0[7])
        blk.vector(s0[10])
        blk.vector(s0[11])
        def mk1_t1(t):
            S.wait(t, "supd", lag=0)
            mk_mm(t, 1, 1, pB)
        blk.tensor(mk1_t1)
        blk.scalar(s0[9])
        def mk1_s1(s):
            mk_exp(s, 1, 1, pB)
        blk.scalar(mk1_s1)

        blk.tensor(s1_[0])
        blk.vector(s1_[1])
        blk.vector(s1_[2])
        blk.vector(s1_[3])
        blk.tensor(s1_[4])
        blk.vector(s1_[5])
        blk.scalar(s1_[6])
        blk.tensor(s1_[7])
        blk.tensor(s1_[8])
        blk.scalar(s1_[9])
        blk.vector(s1_[10])
        blk.vector(s1_[11])

        opjc_b = S.c.get("opjc", 0)
        xin3c_b = S.c.get("xin3c", 0)
        x2c_b = S.c.get("x2c", 0)

        def ph_opj_t(t):
            S.wait(t, "yfme")
            S.wait(t, "pAfree", lag=0)
            for ct in range(FT):
                for k in range(DIT):
                    mm(t, pA[:, ct * QC:(ct + 1) * QC],
                       wopd(k, ct * 128, (ct + 1) * 128),
                       yfm[:, k * QC:(k + 1) * QC],
                       first=(ct % 2 == 0 and k == 0),
                       last=(k == DIT - 1))
                if ct % 2 == 1:
                    S.inc(_lt[0], "opjc", 1)
            S.inc(t.drain(), "opjp", 1)

        def ph_opj_v(v, opjc_b=opjc_b):
            S.wait(v, "xres", lag=0)
            for p in range(4):
                v.wait_ge(S._get("opjc"), opjc_b + p + 1)
                v.tensor_tensor(xf[:, p * 512:(p + 1) * 512],
                                xf[:, p * 512:(p + 1) * 512],
                                pA[:, p * 512:(p + 1) * 512], ALU.add)
                v.drain()
                S.inc(v.drain(), "xin3c", 1)
            S.inc(v.drain(), "xin3", 1)
            S.inc(v.drain(), "pAfree", 1)

        blk.tensor(ph_opj_t)
        blk.vector(ph_opj_v)

        if not on("ln3"):
            continue
        def ln3_x2_s(s, xin3c_b=xin3c_b):
            S.wait(s, "upre", lag=0)
            S.wait(s, "xres", lag=0)
            S.wait(s, "stp", lag=0)
            for p in range(4):
                s.wait_ge(S._get("xin3c"), xin3c_b + p + 1)
                s.copy(x_bf[:, p * 512:(p + 1) * 512],
                       xf[:, p * 512:(p + 1) * 512])
                s.drain()
                s.activation(scrA[:, p * 512:(p + 1) * 512],
                             x_bf[:, p * 512:(p + 1) * 512], AF.Square)
                s.drain()
                S.inc(s.drain(), "x2c", 1)
            S.inc(s.drain(), "xbf", 1)
            S.inc(s.drain(), "x2", 1)

        def ln3_stats_t(t, x2c_b=x2c_b):
            S.wait(t, "pBfree", lag=0)
            for p in range(4):
                t.wait_ge(S._get("x2c"), x2c_b + p + 1)
                for k in (2 * p, 2 * p + 1):
                    mm(t, pB[0:1, 0:512], ones_c[:],
                       bass.AP(scrA, k * QC,
                               [[H * QS, 128], [FT * QC, 2], [1, QC]]),
                       first=(k == 0), last=(k == FT - 1))
            S.inc(_lt[0], "stp", 1)

        blk.scalar(ln3_x2_s)
        blk.tensor(ln3_stats_t)
        emit_ln_rest(fused=False, udst_f8=True, xf=xf, po=0)
        pipe = c + 1 < nch and on("xtr") and on("ln1")
        if pipe:
            emit_xtr(c + 1)

        if not on("ffn2"):
            continue
        f2c_b = S.c.get("f2c", 0)
        f2ec_b = S.c.get("f2ec", 0)

        def ph_ffn2_t(t):
            S.wait(t, "ur")
            S.wait(t, "pAfree", lag=0)
            for ct in range(FT):
                for k2 in range(4):
                    mm(t, pA[:, ct * QC:(ct + 1) * QC],
                       bass.AP(wfft, 2 * k2 * D + ct * 128,
                               [[FT * D, 128], [D, 2], [1, 128]]),
                       bass.AP(u1_f8, 2 * k2 * QC,
                               [[FT * QC, 128], [QC, 2], [1, QC]]),
                       first=(ct % 2 == 0 and k2 == 0), last=(k2 == 3),
                       perf_mode=DR)
                if ct % 2 == 1:
                    S.inc(_lt[0], "f2c", 1)
            S.inc(t.drain(), "f2p", 1)

        blk.tensor(ph_ffn2_t)
        if pipe:
            emit_ln_stats("xin1", x_fms[(c + 1) % 2], po=1024)

        def ph_ffn2_v_s(s, c=c, f2c_b=f2c_b):
            lt = land[c % 2]
            for p in range(4):
                s.wait_ge(S._get("f2c"), f2c_b + p + 1)
                for ct in (2 * p, 2 * p + 1):
                    s.activation(lt[:, ct * QC:(ct + 1) * QC],
                                 pA[:, ct * QC:(ct + 1) * QC], AF.Identity,
                                 scale=0.5, bias=bffh[:, ct:ct + 1])
                s.drain()
                S.inc(s.drain(), "f2ec", 1)
            S.inc(s.drain(), "pAfree", 1)

        def ph_ffn2_v(v, c=c, f2ec_b=f2ec_b):
            lt = land[c % 2]
            for p in range(4):
                v.wait_ge(S._get("f2ec"), f2ec_b + p + 1)
                v.tensor_tensor(xf[:, p * 512:(p + 1) * 512],
                                xf[:, p * 512:(p + 1) * 512],
                                lt[:, p * 512:(p + 1) * 512], ALU.add)
                v.drain()
            S.inc(v.drain(), "x3r", 1)

        blk.scalar(ph_ffn2_v_s)
        blk.vector(ph_ffn2_v)
        if pipe:
            emit_ln_rest(fused=False, udst_f8=True, xf=x_fms[(c + 1) % 2],
                         po=1024, ugate="f2p")
            emit_front2(c + 1)

        if not on("out"):
            continue
        if pipe:
            emit_ln_rest(fused=True, udst_f8=False, xf=x_fms[(c + 1) % 2],
                         po=0)
        def ph_out_t(t):
            S.wait(t, "x3r")
            S.wait(t, "pAfree", lag=0)
            for tt in range(2):
                for i in range(FT):
                    tr(t, pA[:, tt * D + i * 128:tt * D + (i + 1) * 128],
                                xf[:, i * QC + tt * 128:
                                   i * QC + tt * 128 + 128], id_f[:])
            S.inc(_lt[0], "outp", 1)

        def ph_out_v(s, c=c):
            S.wait(s, "outp")
            s.copy(land[c % 2][:, :], pA[:, :])
            s.drain()
            S.inc(s.drain(), "oute", 1)
            S.inc(s.drain(), "pAfree", 1)

        blk.tensor(ph_out_t)
        blk.scalar(ph_out_v)

        def ph_store_y(y, c=c):
            S.wait(y, "oute")
            for tt in range(2):
                r0 = c * QC + tt * 128
                S.inc(y.dma_start(out=out_h[r0:r0 + 128, :],
                                  in_=land[c % 2][:, tt * D:(tt + 1) * D]),
                      "st", 16)
            if c == nch - 1:
                S.wait(y, "st")

        blk.sync(ph_store_y)

    es.close()
    nc.finalize()
    return nc


def kernel(L0_attn_x, L0_attn_ln_w, L0_attn_ln_b, L0_attn_in_proj_w,
           L0_attn_conv_w, L0_attn_conv_b, L0_attn_A_log, L0_attn_D,
           L0_attn_dt_bias, L0_attn_gamma, L0_attn_out_proj_w,
           L0_ffn_weight, L0_ffn_bias):
    B, T, _ = L0_attn_x.shape
    nc = build_nc(T)
    common = {
        "w_ip": np.ascontiguousarray(L0_attn_in_proj_w),
        "w_op": np.ascontiguousarray(L0_attn_out_proj_w),
        "w_ff": np.ascontiguousarray(L0_ffn_weight),
        "b_ff": np.ascontiguousarray(L0_ffn_bias),
        "ln_w": np.ascontiguousarray(L0_attn_ln_w),
        "ln_b": np.ascontiguousarray(L0_attn_ln_b),
        "conv_w": np.ascontiguousarray(L0_attn_conv_w),
        "conv_b": np.ascontiguousarray(L0_attn_conv_b),
        "a_log": np.ascontiguousarray(L0_attn_A_log),
        "d_vec": np.ascontiguousarray(L0_attn_D),
        "dt_bias": np.ascontiguousarray(L0_attn_dt_bias),
        "gamma": np.ascontiguousarray(L0_attn_gamma),
    }
    in_maps = [dict(common, x=np.ascontiguousarray(L0_attn_x[b]))
               for b in range(B)]
    res = run_bass_kernel_spmd(nc, in_maps, core_ids=list(range(B)))
    return np.stack([r["out"] for r in res.results], axis=0)


# version 36 (final)
# speedup vs baseline: 1.1718x; 1.0944x over previous
import os
import numpy as np
from contextlib import ExitStack

import concourse.bass as bass
import concourse.mybir as mybir
from concourse.bass_utils import run_bass_kernel_spmd

F32 = mybir.dt.float32
BF16 = mybir.dt.bfloat16
FP8 = mybir.dt.float8e4
AF = mybir.ActivationFunctionType
ALU = mybir.AluOpType

D, FT, DI, DIT = 1024, 8, 2048, 16
CD, CT, DIP, H, HD = 2304, 18, 4384, 32, 64
QC, QS = 256, 128
NSUB = QC // QS
EPS = 1e-5

_SET_IDS = {}


def _act_set_ids(arch):
    if arch not in _SET_IDS:
        from concourse.hw_specs import get_activation_tables
        tabs = list(get_activation_tables(arch).values())
        exp_ln = next(i for i, fns in enumerate(tabs)
                      if AF.Exp in fns and AF.Ln in fns)
        silu = next(i for i, fns in enumerate(tabs) if AF.Silu in fns)
        _SET_IDS[arch] = (exp_ln, silu)
    return _SET_IDS[arch]


def _load_set(scalar, set_id):
    return scalar.add_instruction(
        mybir.InstLoadActFuncSet(
            name=scalar.bass.get_next_instruction_name(),
            act_func_set_id=set_id, ins=[], outs=[]))


def rep_outer(ap2d, outer):
    P, n = ap2d.shape
    return ap2d[:, None, :].broadcast_to((P, outer, n))


def rep_inner(ap2d, inner):
    P, n = ap2d.shape
    return ap2d[:, :, None].broadcast_to((P, n, inner))


class Sems:
    def __init__(self, nc, es):
        self.nc, self.es, self.h, self.c = nc, es, {}, {}

    def _get(self, name):
        if name not in self.h:
            self.h[name] = self.es.enter_context(self.nc.semaphore(name))
            self.c[name] = 0
        return self.h[name]

    def inc(self, instr, name, n=1):
        instr.then_inc(self._get(name), n)
        self.c[name] += n
        return instr

    def wait(self, eng, name, lag=0):
        eng.wait_ge(self._get(name), self.c[name] - lag)


_STAGES = ["pro", "xtr", "ln1", "ffn1", "ln2", "ipx", "dtc", "z", "conv",
            "silu", "ssd", "opj", "ln3", "ffn2", "out"]


def build_nc(T=2048):
    stage = os.environ.get("KSTAGE", "out")
    slvl = _STAGES.index(stage)

    def on(lvl):
        return slvl >= _STAGES.index(lvl)

    nch = T // QC
    nc = bass.Bass("TRN2", target_bir_lowering=False,
                   dynamic_dma_scratch_size=4096)
    SET_EXP_LN, SET_SILU = _act_set_ids(nc.m.arch)

    x_h = nc.dram_tensor("x", [T, D], F32, kind="ExternalInput").ap()
    out_h = nc.dram_tensor("out", [T, D], F32, kind="ExternalOutput").ap()
    wip_h = nc.dram_tensor("w_ip", [DIP, D], F32, kind="ExternalInput").ap()
    wop_h = nc.dram_tensor("w_op", [D, DI], F32, kind="ExternalInput").ap()
    wff_h = nc.dram_tensor("w_ff", [D, D], F32, kind="ExternalInput").ap()
    bff_h = nc.dram_tensor("b_ff", [D], F32, kind="ExternalInput").ap()
    lnw_h = nc.dram_tensor("ln_w", [D], F32, kind="ExternalInput").ap()
    lnb_h = nc.dram_tensor("ln_b", [D], F32, kind="ExternalInput").ap()
    cw_h = nc.dram_tensor("conv_w", [CD, 4], F32, kind="ExternalInput").ap()
    cb_h = nc.dram_tensor("conv_b", [CD], F32, kind="ExternalInput").ap()
    al_h = nc.dram_tensor("a_log", [H], F32, kind="ExternalInput").ap()
    dv_h = nc.dram_tensor("d_vec", [H], F32, kind="ExternalInput").ap()
    dtb_h = nc.dram_tensor("dt_bias", [H], F32, kind="ExternalInput").ap()
    gam_h = nc.dram_tensor("gamma", [DI], F32, kind="ExternalInput").ap()

    es = ExitStack()
    sb = lambda n, sh, dt=F32: es.enter_context(nc.sbuf_tensor(n, sh, dt))
    S = Sems(nc, es)

    wipt = sb("wipt", [128, FT * DIP], BF16)
    wopt = sb("wopt", [128, DIT * D], BF16)
    wfft = sb("wfft", [128, FT * D], FP8)
    convw = sb("convw", [128, CT * 4])
    convb = sb("convb", [128, CT])
    lnw_f = sb("lnw_f", [128, FT])
    lnb_f = sb("lnb_f", [128, FT])
    bffh = sb("bffh", [128, FT])
    gam_f = sb("gam_f", [128, DIT])
    dtb_c = sb("dtb_c", [H, 1])
    aneg = sb("aneg", [H, 1])
    dvall = sb("dvall", [128, DIT])
    ddiag = sb("ddiag", [128, DIT * 128], BF16)
    tri_b = sb("tri_b", [128, 128], BF16)
    id_b = sb("id_b", [128, 128], BF16)
    id_f = sb("id_f", [128, 128], F32)
    onesr2 = sb("onesr2", [1, 128])
    ones2r = sb("ones2r", [2, 128], BF16)
    ones_c = sb("ones_c", [128, 1], BF16)
    land = [sb(f"land{i}", [128, 2 * D]) for i in range(2)]
    x_fm = sb("x_fm", [128, FT * QC])
    x_fm2 = sb("x_fm2", [128, FT * QC])
    x_fms = [x_fm, x_fm2]
    scrA = sb("scrA", [128, H * QS], BF16)
    x_bf = scrA[:, FT * QC:2 * FT * QC]
    u1_f8 = sb("u1_f8", [128, FT * QC], FP8)
    scr2 = sb("scr2", [128, DI], BF16)
    u2_bf = sb("u2_bf", [128, FT * QC], BF16)
    sdec = u2_bf
    rb_t = sb("rb_t", [128, QC], BF16)
    mrb_t = sb("mrb_t", [128, QC], BF16)
    rows = sb("rows", [1, 768])
    xbcp = sb("xbcp", [128, CT * (QC + 4)], BF16)
    yfm = xbcp[:, 0:DIT * QC]
    cacc = sb("cacc", [128, CT * QC], BF16)
    z_si = sb("z_si", [128, NSUB * DI], BF16)
    dt_f = sb("dt_f", [H, QC])
    a_f = sb("a_f", [H, QC])
    cum_f = sb("cum_f", [H, QC])
    ccf = a_f
    dt_tm = sb("dt_tm", [128, NSUB * H], BF16)
    cum_tm = sb("cum_tm", [128, NSUB * H])
    cexp_tm = sb("cexp_tm", [128, NSUB * H], BF16)
    cc_tm = sb("cc_tm", [128, NSUB * H], BF16)
    ccq_b = sb("ccq_b", [128, NSUB * H], BF16)
    stri = sb("stri", [128, QS], BF16)
    xdt = sb("xdt", [128, DI], BF16)
    btm = sb("btm", [128, 128], BF16)
    s_bf = sb("s_bf", [128, DI], BF16)
    y_bf = sb("y_bf", [128, DI], BF16)
    ssq = sb("ssq", [128, 1])
    rms_c = sb("rms_c", [128, 1])
    tl3 = sb("tl3", [128, CT * 3], BF16)
    badd_z = sb("badd_z", [1, DI], BF16)
    badd_fm = sb("badd_fm", [128, 19])
    lnb_b = sb("lnb_b", [128, FT], BF16)
    trin_b = sb("trin_b", [128, 128], BF16)
    rb1_t = sb("rb1_t", [128, QC], BF16)
    mu1_t = sb("mu1_t", [128, QC], BF16)
    cumhi = sb("cumhi", [H, QC], BF16)
    cumlo = sb("cumlo", [H, QC], BF16)
    neg_id32 = sb("neg_id32", [32, 32], BF16)
    cst = sb("cst", [128, 16])
    _cvals = [EPS, 1.0, -0.5, -1.0, 1.0 / DI, 0.5, 448.0, -448.0,
              EPS * EPS]
    for _i, _v in enumerate(_cvals):
        nc.const_aps.register(F32, _v, cst[:, _i:_i + 1])
    pA = es.enter_context(nc.psum_tensor("pA", [128, 2048], F32))
    pB = es.enter_context(nc.psum_tensor("pB", [128, 2048], F32))

    blk = es.enter_context(nc.Block())

    _lt = [None]

    def mm(t, out, lhsT, rhs, first, last=True, perf_mode=None):
        i = t.matmul(out, lhsT, rhs, start=first, stop=last,
                     skip_group_check=True, perf_mode=perf_mode)
        _lt[0] = i
        return i

    DR = mybir.MatmulPerfMode.DoubleRow

    def ffn_mms(t, pp):
        for ct in range(FT):
            for k2 in range(4):
                mm(t, pp[:, ct * QC:(ct + 1) * QC],
                   bass.AP(wfft, 2 * k2 * D + ct * 128,
                           [[FT * D, 128], [D, 2], [1, 128]]),
                   bass.AP(u1_f8, 2 * k2 * QC,
                           [[FT * QC, 128], [QC, 2], [1, QC]]),
                   first=(ct % 2 == 0 and k2 == 0), last=(k2 == 3),
                   perf_mode=DR)

    def tr(t, out, in_, ident):
        i = t.transpose(out, in_, ident)
        _lt[0] = i
        return i

    wip = lambda i, a, b: wipt[:, i * DIP + a: i * DIP + b]
    wopd = lambda dk, a, b: wopt[:, dk * D + a: dk * D + b]
    wff = lambda i, a, b: wfft[:, i * D + a: i * D + b]

    def p_sync0(y):
        def dma(dst, src, tag="wsm"):
            S.inc(y.dma_start(out=dst, in_=src,
                              allow_slow_non_contiguous=True), tag, 16)
        dma(lnw_f[:], bass.AP(lnw_h.tensor, 0, [[1, 128], [128, FT]]))
        dma(lnb_f[:], bass.AP(lnb_h.tensor, 0, [[1, 128], [128, FT]]))
        dma(bffh[:], bass.AP(bff_h.tensor, 0, [[1, 128], [128, FT]]))
        dma(gam_f[:], bass.AP(gam_h.tensor, 0, [[1, 128], [128, DIT]]))
        dma(convb[:], bass.AP(cb_h.tensor, 0, [[1, 128], [128, CT]]))
        dma(convw[:], bass.AP(cw_h.tensor, 0, [[4, 128], [512, CT], [1, 4]]))
        dma(dtb_c[:], bass.AP(dtb_h.tensor, 0, [[1, H], [1, 1]]))
        dma(aneg[:], bass.AP(al_h.tensor, 0, [[1, H], [1, 1]]))
        for i in range(DIT):
            dma(dvall[0:64, i:i + 1],
                bass.AP(dv_h.tensor, 2 * i, [[0, 64], [1, 1]]))
            dma(dvall[64:128, i:i + 1],
                bass.AP(dv_h.tensor, 2 * i + 1, [[0, 64], [1, 1]]))
        for tt in range(2):
            dma(land[0][:, tt * D:(tt + 1) * D],
                x_h[tt * 128:(tt + 1) * 128, :], "xinA")

    blk.sync(p_sync0)

    def p_gp(g):
        f = land[1][:, 1024:1152]
        g.memset(f, 1.0)
        g.drain()
        g.affine_select(f, f, pattern=[[1, 128]], compare_op=ALU.is_ge,
                        fill=0.0, base=0, channel_multiplier=-1)
        g.drain()
        g.tensor_copy(tri_b[:], f)
        g.drain()
        g.affine_select(f, f, pattern=[[-1, 128]], compare_op=ALU.is_ge,
                        fill=0.0, base=0, channel_multiplier=1)
        g.drain()
        g.tensor_copy(id_b[:], f)
        g.tensor_copy(id_f[:], f)
        g.drain()
        g.memset(f, 0.0)
        g.drain()
        g.affine_select(f, f, pattern=[[-1, 128]], compare_op=ALU.is_ge,
                        fill=-1.0e6, base=0, channel_multiplier=1)
        g.drain()
        g.tensor_copy(trin_b[:], f)
        for _i, _v in enumerate(_cvals):
            g.memset(cst[:, _i:_i + 1], _v)
        g.memset(onesr2[0:1, 0:128], 1.0)
        g.memset(ones2r[0:2, 0:128], 1.0)
        g.memset(neg_id32[0:32, 0:32], -1.0)
        g.drain()
        g.affine_select(neg_id32[:], neg_id32[:], pattern=[[1, 32]],
                        compare_op=ALU.is_ge, fill=0.0, base=0,
                        channel_multiplier=-1)
        g.drain()
        g.affine_select(neg_id32[:], neg_id32[:], pattern=[[-1, 32]],
                        compare_op=ALU.is_ge, fill=0.0, base=0,
                        channel_multiplier=1)
        g.drain()
        g.memset(ones_c[:], 1.0 / D)
        g.memset(s_bf[:], 0.0)
        for ct in range(CT):
            g.memset(xbcp[:, ct * (QC + 4):ct * (QC + 4) + 3], 0.0)
        g.drain()
        S.inc(g.drain(), "gpc", 1)

    blk.gpsimd(p_gp)

    def p_v0(v):
        S.wait(v, "gpc")
        S.wait(v, "wsm")
        v.tensor_tensor(
            ddiag[:].rearrange("p (i c) -> p i c", i=DIT),
            rep_outer(id_b[:], DIT),
            rep_inner(dvall[:], 128), ALU.mult)
        v.drain()
        S.inc(v.drain(), "dcon", 1)

    blk.vector(p_v0)

    def p_s0(s):
        _load_set(s, SET_EXP_LN)
        S.wait(s, "wsm")
        s.activation(aneg[:], aneg[:], AF.Exp)
        s.drain()
        s.mul(aneg[:], aneg[:], -1.0)
        s.mul(bffh[:], bffh[:], 0.5)
        s.drain()
        sc1 = land[1][:, 1152:1152 + FT]
        sc2 = land[1][:, 1152 + FT:1152 + 2 * FT]
        s.activation(sc1, lnw_f[:], AF.Abs)
        s.drain()
        s.activation(sc1, sc1, AF.Ln)
        s.drain()
        s.activation(sc1, sc1, AF.Exp, scale=-1.0)
        s.activation(sc2, lnw_f[:], AF.Sign)
        s.drain()
        S.inc(s.drain(), "acon", 1)

    blk.scalar(p_s0)

    w_jobs = [("ff", j) for j in range(FT)]
    w_jobs += [("ip", j) for j in range(35)]
    for j in range(FT):
        w_jobs += [("opA", j), ("opB", j)]

    def _wsrc(kind, j):
        if kind == "ip" and j == 34:
            return wip_h[34 * 128:DIP, :]
        if kind == "ff":
            return wff_h[j * 128:(j + 1) * 128, :]
        if kind == "ip":
            return wip_h[j * 128:(j + 1) * 128, :]
        if kind == "opA":
            return wop_h[j * 128:(j + 1) * 128, 0:D]
        return wop_h[j * 128:(j + 1) * 128, D:2 * D]

    def p_syncw(y):
        S.wait(y, "gpc")
        S.wait(y, "acon")
        S.wait(y, "lnbd", lag=-1)
        for n, (kind, j) in enumerate(w_jobs):
            if n >= 2:
                y.wait_ge(S._get("stgf"), n - 1)
            rows = 32 if (kind == "ip" and j == 34) else 128
            h = n % 2
            S.inc(y.dma_start(out=land[1][0:rows, h * D:h * D + D],
                              in_=_wsrc(kind, j)),
                  "wstA" if h == 0 else "wstB", 16)

    blk.sync(p_syncw)

    def p_tw(t):
        for n, (kind, j) in enumerate(w_jobs):
            if n % 2 == 0:
                t.wait_ge(S._get("wstA"), 16 * (n // 2 + 1))
            else:
                t.wait_ge(S._get("wstB"), 16 * ((n - 1) // 2 + 1))
            if n >= 2:
                t.wait_ge(S._get("wev"), n - 1)
            src = land[1][:, (n % 2) * D:(n % 2) * D + D]
            pp = pA if n % 2 == 0 else pB
            rows_in = 32 if (kind == "ip" and j == 34) else 128
            for i in range(FT):
                off = (i // 4) * 512 + (i % 4) * 128
                tr(t, pp[:, off:off + rows_in],
                            src[0:rows_in, i * 128:(i + 1) * 128],
                            id_f[0:rows_in, 0:rows_in])
            S.inc(_lt[0], "wtr", 1)
            S.inc(t.drain(), "stgf", 1)

        t.wait_ge(S._get("wev"), len(w_jobs))
        for tt2 in range(19):
            a0 = DI + tt2 * 128
            b0 = min(a0 + 128, DIP)
            for k in range(FT):
                mm(t, pA[0:b0 - a0, tt2:tt2 + 1], wip(k, a0, b0),
                   lnb_b[:, k:k + 1], first=(k == 0), last=(k == FT - 1))
        for nb in range(4):
            for k in range(FT):
                mm(t, pB[0:1, nb * 512:(nb + 1) * 512], lnb_b[:, k:k + 1],
                   wip(k, nb * 512, (nb + 1) * 512),
                   first=(k == 0), last=(k == FT - 1))
        S.inc(_lt[0], "badp", 1)

    blk.tensor(p_tw)

    def p_vw(v):
        S.wait(v, "acon")
        v.tensor_tensor(lnb_b[:], lnb_f[:], land[1][:, 1152:1152 + FT],
                        ALU.mult)
        v.drain()
        v.tensor_tensor(lnb_b[:], lnb_b[:],
                        land[1][:, 1152 + FT:1152 + 2 * FT], ALU.mult)
        v.drain()
        S.inc(v.drain(), "lnbd", 1)
        for n, (kind, j) in enumerate(w_jobs):
            v.wait_ge(S._get("wtr"), n + 1)
            pp = pA if n % 2 == 0 else pB
            rows_in = 32 if (kind == "ip" and j == 34) else 128
            for bank in range(2):
                base, i0 = bank * 512, bank * 4
                view = pp[:, base:base + 512].rearrange(
                    "p (i c) -> p i c", i=4)[:, :, 0:rows_in]
                if kind == "ff":
                    dstv = wfft[:].rearrange("p (i c) -> p i c", i=FT)[
                        :, i0:i0 + 4, j * 128:j * 128 + rows_in]
                    v.tensor_scalar(dstv, view, 448.0, -448.0,
                                    ALU.min, ALU.max)
                elif kind == "ip":
                    for ii in range(4):
                        dstv = wipt[:].rearrange("p (i c) -> p i c", i=FT)[
                            :, i0 + ii, j * 128:j * 128 + rows_in]
                        v.tensor_scalar(
                            dstv, pp[:, base + ii * 128:
                                     base + ii * 128 + rows_in],
                            lnw_f[:, i0 + ii:i0 + ii + 1], None, ALU.mult)
                else:
                    dk0 = (0 if kind == "opA" else 8) + i0
                    dstv = wopt[:].rearrange("p (i c) -> p i c", i=DIT)[
                        :, dk0:dk0 + 4, j * 128:j * 128 + rows_in]
                    v.tensor_tensor(dstv, view,
                                    rep_inner(gam_f[:, dk0:dk0 + 4], rows_in),
                                    ALU.mult)
            v.drain()
            S.inc(v.drain(), "wev", 1)
        S.wait(v, "badp")
        S.wait(v, "gpc")
        v.tensor_copy(badd_fm[:], pA[:, 0:19])
        for nb in range(4):
            v.tensor_copy(badd_z[0:1, nb * 512:(nb + 1) * 512],
                          pB[0:1, nb * 512:(nb + 1) * 512])
        v.drain()
        cwv = convw[:].rearrange("p (i k) -> p i k", i=CT)
        ws = land[1][:, 1024:1024 + CT]
        ws2 = land[1][:, 1024 + CT:1024 + 2 * CT]
        v.tensor_tensor(ws, cwv[:, :, 0], cwv[:, :, 1], ALU.add)
        v.tensor_tensor(ws2, cwv[:, :, 2], cwv[:, :, 3], ALU.add)
        v.drain()
        v.tensor_tensor(ws, ws, ws2, ALU.add)
        v.drain()
        v.tensor_tensor(ws, ws, badd_fm[:, 0:CT], ALU.mult)
        v.drain()
        v.tensor_tensor(convb[:], convb[:], ws, ALU.add)
        v.tensor_tensor(dtb_c[:], dtb_c[:], badd_fm[0:H, CT:CT + 1], ALU.add)
        v.tensor_scalar(
            xbcp[:].rearrange("p (i t) -> p i t", i=CT)[:, :, 0:3],
            rep_inner(badd_fm[:, 0:CT], 3), -1.0, None, ALU.mult)
        v.drain()
        S.inc(v.drain(), "wrdy", 1)

    blk.vector(p_vw)

    def xv(buf):
        return buf[:].rearrange("p (i t) -> p i t", i=FT)

    def emit_ln_stats(xmark, xf, po=0):
        def s_x2(s):
            S.wait(s, xmark)
            S.wait(s, "upre", lag=0)
            S.wait(s, "xres", lag=0)
            s.copy(x_bf, xf[:])
            s.drain()
            S.inc(s.drain(), "xbf", 1)
            S.wait(s, "stp", lag=0)
            s.activation(scrA[:, 0:FT * QC], x_bf, AF.Square)
            s.drain()
            S.inc(s.drain(), "x2", 1)

        def t_stats(t):
            S.wait(t, "xbf")
            S.wait(t, "x2")
            S.wait(t, "pBfree", lag=0)
            for k in range(FT):
                mm(t, pB[0:1, po:po + 512], ones_c[:],
                   bass.AP(scrA, k * QC,
                           [[H * QS, 128], [FT * QC, 2], [1, QC]]),
                   first=(k == 0), last=(k == FT - 1))
            S.inc(_lt[0], "stp", 1)

        blk.scalar(s_x2)
        blk.tensor(t_stats)

    def emit_ln_rest(fused, udst_f8, xf, po=0, ugate=None):
        def s_m1(s):
            S.wait(s, "stp")
            s.activation(rows[0:1, 0:256], pB[0:1, po + 256:po + 512],
                         AF.Square)
            s.drain()
            S.inc(s.drain(), "m1", 1)

        def v_m1(v):
            S.wait(v, "m1")
            v.tensor_tensor(rows[0:1, 256:512], pB[0:1, po:po + 256],
                            rows[0:1, 0:256], ALU.subtract)
            v.drain()
            S.inc(v.drain(), "m2", 1)

        blk.scalar(s_m1)
        blk.vector(v_m1)

        if not fused:
            def s_m2(s):
                S.wait(s, "m2")
                s.activation(rows[0:1, 0:256], rows[0:1, 256:512],
                             AF.Ln, bias=EPS)
                s.drain()
                s.activation(rows[0:1, 0:256], rows[0:1, 0:256],
                             AF.Exp, scale=-0.5)
                s.copy(rows[0:1, 256:512], pB[0:1, po + 256:po + 512])
                s.drain()
                S.inc(s.drain(), "m4", 1)

            blk.scalar(s_m2)
        else:
            def s_m2(s):
                S.wait(s, "m2")
                s.activation(rows[0:1, 512:768], rows[0:1, 256:512],
                             AF.Ln, bias=EPS)
                s.drain()
                s.activation(rows[0:1, 256:512], rows[0:1, 256:512],
                             AF.Ln, scale=1.0 + EPS, bias=EPS * EPS)
                s.drain()
                s.activation(rows[0:1, 512:768], rows[0:1, 512:768],
                             AF.Exp, scale=-0.5)
                s.activation(rows[0:1, 256:512], rows[0:1, 256:512],
                             AF.Exp, scale=-0.5)
                s.drain()
                s.copy(rows[0:1, 0:256], pB[0:1, po + 256:po + 512])
                s.drain()
                S.inc(s.drain(), "m5", 1)

            def t_bcf(t):
                S.wait(t, "m5")
                mm(t, pB[:, 512:1024], onesr2[0:1, 0:128],
                   rows[0:1, 0:512], first=True)
                mm(t, pB[:, 1024:1280], onesr2[0:1, 0:128],
                   rows[0:1, 512:768], first=True)
                S.inc(_lt[0], "bcp", 1)
                S.inc(t.drain(), "pBfree", 1)

            def v_bcf(v):
                S.wait(v, "bcp")
                v.tensor_copy(mu1_t[:], pB[:, 512:768])
                v.tensor_copy(rb_t[:], pB[:, 768:1024])
                v.tensor_copy(rb1_t[:], pB[:, 1024:1280])
                v.drain()
                v.tensor_tensor(mrb_t[:], mu1_t[:], rb_t[:], ALU.mult)
                v.drain()
                S.inc(v.drain(), "bce", 1)
                S.inc(v.drain(), "pBfree", 1)

            blk.scalar(s_m2)
            blk.tensor(t_bcf)
            blk.vector(v_bcf)

        if not fused:
            def t_bc(t):
                S.wait(t, "m4")
                mm(t, pB[:, po + 512:po + 1024], onesr2[0:1, 0:128],
                   rows[0:1, 0:512], first=True)
                S.inc(_lt[0], "bcp", 1)
                S.inc(t.drain(), "pBfree", 1)

            def v_bc(v):
                S.wait(v, "bcp")
                v.tensor_copy(rb_t[:], pB[:, po + 512:po + 768])
                v.tensor_copy(mu1_t[:], pB[:, po + 768:po + 1024])
                v.drain()
                v.tensor_tensor(mrb_t[:], mu1_t[:], rb_t[:], ALU.mult)
                v.drain()
                S.inc(v.drain(), "bce", 1)
                S.inc(v.drain(), "pBfree", 1)

            blk.tensor(t_bc)
            blk.vector(v_bc)

        def v_u(v):
            S.wait(v, "bce")
            if ugate is not None:
                S.wait(v, ugate)
            for i in range(FT):
                v.tensor_tensor(scr2[:, i * QC:(i + 1) * QC],
                                x_bf[:, i * QC:(i + 1) * QC], rb_t[:],
                                ALU.mult)
            v.drain()
            if udst_f8:
                for i in range(FT):
                    v.tensor_tensor(u1_f8[:, i * QC:(i + 1) * QC],
                                    scr2[:, i * QC:(i + 1) * QC], mrb_t[:],
                                    ALU.subtract)
                v.drain()
                S.inc(v.drain(), "ur", 1)
            else:
                for i in range(FT):
                    v.tensor_tensor(u2_bf[:, i * QC:(i + 1) * QC],
                                    scr2[:, i * QC:(i + 1) * QC], mrb_t[:],
                                    ALU.subtract)
                v.drain()
                S.inc(v.drain(), "ur", 1)
            S.inc(v.drain(), "upre", 1)

        blk.vector(v_u)

    def emit_ln(tag, fused, udst_f8, xmark, xf, po=0, ugate=None):
        emit_ln_stats(xmark, xf, po)
        emit_ln_rest(fused, udst_f8, xf, po, ugate)

    def emit_xtr(cn):
        lb = land[cn % 2]
        xfn = x_fms[cn % 2]
        for w in range(2):
            def xw_t(t, w=w, cn=cn):
                if w == 0:
                    S.wait(t, "xinA" if cn % 2 == 0 else "xinB")
                    if cn == 0:
                        S.wait(t, "wrdy")
                        S.wait(t, "dcon")
                    else:
                        S.wait(t, "supd", lag=0)
                else:
                    S.wait(t, "xev")
                for i in range(FT):
                    tr(t, pB[:, 1024 + i * 128:1024 + (i + 1) * 128],
                       lb[:, w * D + i * 128:w * D + (i + 1) * 128], id_f[:])
                S.inc(_lt[0], "xw", 1)
                if w == 1:
                    S.inc(t.drain(), "xtrp", 1)

            def xw_s(s, w=w, cn=cn):
                S.wait(s, "xw")
                dst = xfn[:].rearrange("p (i t) -> p i t", i=FT)[
                    :, :, w * 128:(w + 1) * 128]
                src_ = pB[:, 1024:2048].rearrange("p (i t) -> p i t", i=FT)
                s.copy(dst, src_)
                s.drain()
                S.inc(s.drain(), "xev", 1)
                if w == 1:
                    S.inc(s.drain(), "xin1", 1)

            blk.tensor(xw_t)
            blk.scalar(xw_s)

    def emit_front2(cn, c0=False):
        xfn = x_fms[cn % 2]

        f1c_b = S.c.get("f1c", 0)
        f1ec_b = S.c.get("f1ec", 0)
        xin2c_b = S.c.get("xin2c", 0)
        x2c2_b = S.c.get("x2c2", 0)

        def ph_ffn1_t(t, cn=cn):
            S.wait(t, "ur")
            if c0:
                S.wait(t, "pBfree", lag=0)
            else:
                S.wait(t, "bce", lag=0)
                S.wait(t, "pBfree", lag=0)
            for ct in range(FT):
                for k2 in range(4):
                    mm(t, pB[:, ct * QC:(ct + 1) * QC],
                       bass.AP(wfft, 2 * k2 * D + ct * 128,
                               [[FT * D, 128], [D, 2], [1, 128]]),
                       bass.AP(u1_f8, 2 * k2 * QC,
                               [[FT * QC, 128], [QC, 2], [1, QC]]),
                       first=(ct % 2 == 0 and k2 == 0), last=(k2 == 3),
                       perf_mode=DR)
                if ct % 2 == 1:
                    S.inc(_lt[0], "f1c", 1)

        def ph_ffn1_ep(s, cn=cn, f1c_b=f1c_b):
            lt = land[cn % 2]
            for p in range(4):
                s.wait_ge(S._get("f1c"), f1c_b + p + 1)
                for ct in (2 * p, 2 * p + 1):
                    s.activation(lt[:, ct * QC:(ct + 1) * QC],
                                 pB[:, ct * QC:(ct + 1) * QC], AF.Identity,
                                 scale=0.5, bias=bffh[:, ct:ct + 1])
                s.drain()
                S.inc(s.drain(), "f1ec", 1)
            S.inc(s.drain(), "pBfree", 1)

        def ph_ffn1_v(v, cn=cn, f1ec_b=f1ec_b):
            lt = land[cn % 2]
            for p in range(4):
                v.wait_ge(S._get("f1ec"), f1ec_b + p + 1)
                v.tensor_tensor(xfn[:, p * 512:(p + 1) * 512],
                                xfn[:, p * 512:(p + 1) * 512],
                                lt[:, p * 512:(p + 1) * 512], ALU.add)
                v.drain()
                S.inc(v.drain(), "xin2c", 1)
            S.inc(v.drain(), "xin2", 1)

        def ln2_x2_s(s, xin2c_b=xin2c_b):
            S.wait(s, "upre", lag=0)
            S.wait(s, "xres", lag=0)
            S.wait(s, "stp", lag=0)
            for p in range(4):
                s.wait_ge(S._get("xin2c"), xin2c_b + p + 1)
                s.copy(x_bf[:, p * 512:(p + 1) * 512],
                       xfn[:, p * 512:(p + 1) * 512])
                s.drain()
                s.activation(scrA[:, p * 512:(p + 1) * 512],
                             x_bf[:, p * 512:(p + 1) * 512], AF.Square)
                s.drain()
                S.inc(s.drain(), "x2c2", 1)
            S.inc(s.drain(), "xbf", 1)
            S.inc(s.drain(), "x2", 1)

        def ln2_stats_t(t, x2c2_b=x2c2_b):
            for p in range(4):
                t.wait_ge(S._get("x2c2"), x2c2_b + p + 1)
                for k in (2 * p, 2 * p + 1):
                    mm(t, pB[0:1, 0:512], ones_c[:],
                       bass.AP(scrA, k * QC,
                               [[H * QS, 128], [FT * QC, 2], [1, QC]]),
                       first=(k == 0), last=(k == FT - 1))
            S.inc(_lt[0], "stp", 1)

        blk.tensor(ph_ffn1_t)
        blk.scalar(ph_ffn1_ep)
        blk.vector(ph_ffn1_v)
        blk.scalar(ln2_x2_s)
        blk.tensor(ln2_stats_t)

    XB = QC + 4
    for c in range(nch):
        tagc = str(c)
        xf = x_fms[c % 2]

        if c == 0 and on("xtr"):
            emit_xtr(0)
            if on("ln1"):
                emit_ln("1", fused=False, udst_f8=True, xmark="xin1",
                        xf=xf, po=1024, ugate="f2p")
            emit_front2(0, c0=True)
            emit_ln_rest(fused=True, udst_f8=False, xf=xf, po=0)

        if c + 1 < nch:
            def ph_load(y, c=c):
                S.wait(y, "st", lag=0)
                S.wait(y, "xtrp", lag=0)
                if c == 0:
                    S.wait(y, "wrdy")
                for tt in range(2):
                    r0 = (c + 1) * QC + tt * 128
                    S.inc(y.dma_start(
                        out=land[(c + 1) % 2][:, tt * D:(tt + 1) * D],
                        in_=x_h[r0:r0 + 128, :]),
                        "xinA" if (c + 1) % 2 == 0 else "xinB", 16)
            blk.sync(ph_load)

        if not on("ipx"):
            continue
        if not on("dtc"):
            continue
        def ph_dt_t(t):
            S.wait(t, "ur")
            S.wait(t, "pAfree", lag=0)
            for k in range(FT):
                mm(t, pA[0:H, 0:QC], wip(k, DIP - H, DIP),
                   xv(u2_bf)[:, k, :], first=(k == 0), last=(k == FT - 1))
            S.inc(_lt[0], "dtp", 1)

        blk.tensor(ph_dt_t)

        groups = [list(range(4 * g, min(4 * g + 4, CT))) for g in range(5)]
        ipxe_b = S.c.get("ipxe", 0)
        ipxp_b = S.c.get("ipxp", 0)

        def ph_ipx_t_a(t):
            S.wait(t, "ur")
            S.wait(t, "pBfree", lag=0)
            for g in (0, 1):
                base = (g % 2) * 1024
                for n, ct in enumerate(groups[g]):
                    for k in range(FT):
                        mm(t, pB[:, base + n * QC:base + (n + 1) * QC],
                           wip(k, DI + ct * 128, DI + (ct + 1) * 128),
                           xv(u2_bf)[:, k, :],
                           first=(n % 2 == 0 and k == 0),
                           last=(k == FT - 1))
                S.inc(_lt[0], "ipxp", 1)

        blk.tensor(ph_ipx_t_a)

        def ph_dt_s(s):
            S.wait(s, "dtp")
            s.activation(a_f[:], pA[0:H, 0:QC], AF.Abs, bias=dtb_c[:, 0:1])
            s.drain()
            s.activation(a_f[:], a_f[:], AF.Exp, scale=-1.0)
            s.drain()
            s.activation(a_f[:], a_f[:], AF.Ln, bias=1.0)
            s.drain()
            s.activation(dt_f[:], pA[0:H, 0:QC], AF.Relu, bias=dtb_c[:, 0:1])
            s.drain()
            S.inc(s.drain(), "dtsp", 1)

        def ph_dt_v(v):
            S.wait(v, "dtsp")
            v.tensor_tensor(dt_f[:], dt_f[:], a_f[:], ALU.add)
            v.drain()
            S.inc(v.drain(), "pAfree", 1)
            v.tensor_scalar(a_f[:], dt_f[:], aneg[:, 0:1], None, ALU.mult)
            v.drain()
            for sbi in range(NSUB):
                sl = slice(sbi * QS, (sbi + 1) * QS)
                v.tensor_tensor_scan(cum_f[:, sl], a_f[:, sl], a_f[:, sl],
                                     0.0, ALU.add, ALU.bypass)
            v.drain()
            S.inc(v.drain(), "cumr", 1)

        blk.scalar(ph_dt_s)
        blk.vector(ph_dt_v)

        def ph_xres_v(v, c=c):
            S.wait(v, "ur", lag=0)
            S.wait(v, "st", lag=0)
            lt = land[c % 2][:].rearrange("p (i t) -> p i t", i=FT)
            xb3 = x_bf.rearrange("p (i t) -> p i t", i=FT)
            v.tensor_tensor(lt, xb3, rep_outer(mu1_t[:], FT), ALU.subtract)
            v.drain()
            v.tensor_tensor(lt, lt, rep_outer(rb1_t[:], FT), ALU.mult)
            v.drain()
            v.tensor_tensor(xf[:], xf[:], land[c % 2][:], ALU.add)
            v.drain()
            S.inc(v.drain(), "xres", 1)

        blk.vector(ph_xres_v)

        def ph_chl_g(g):
            S.wait(g, "cumr")
            g.tensor_copy(cumhi[:], cum_f[:])
            g.drain()
            g.tensor_tensor(cumlo[:], cum_f[:], cumhi[:], ALU.subtract)
            g.drain()

        blk.gpsimd(ph_chl_g)

        def ph_cc_s(s):
            S.wait(s, "cumr")
            for sbi in range(NSUB):
                sl = slice(sbi * QS, (sbi + 1) * QS)
                s.activation(ccf[:, sl], cum_f[:, sl], AF.Exp, scale=-1.0,
                             bias=cum_f[:, (sbi + 1) * QS - 1:(sbi + 1) * QS])
            s.drain()
            S.inc(s.drain(), "ccfr", 1)

        blk.scalar(ph_cc_s)

        def ph_tt_t(t):
            S.wait(t, "ccfr")
            S.wait(t, "pAfree", lag=0)
            for sbi in range(NSUB):
                sl0 = sbi * QS
                tr(t, pA[:, sbi * 512 + 0:sbi * 512 + H],
                            dt_f[:, sl0:sl0 + QS], id_f[0:H, 0:H])
                tr(t, pA[:, sbi * 512 + 128:sbi * 512 + 128 + H],
                            cum_f[:, sl0:sl0 + QS], id_f[0:H, 0:H])
                tr(t, pA[:, sbi * 512 + 256:sbi * 512 + 256 + H],
                            ccf[:, sl0:sl0 + QS], id_f[0:H, 0:H])
                tr(t, pA[0:1, sbi * 512 + 384:sbi * 512 + 384 + H],
                            cum_f[:, sl0 + QS - 1:sl0 + QS], id_f[0:H, 0:H])
            S.inc(_lt[0], "dttp", 1)

        def ph_tt_v(v):
            S.wait(v, "dttp")
            for sbi in range(NSUB):
                hs = slice(sbi * H, (sbi + 1) * H)
                v.tensor_copy(dt_tm[:, hs], pA[:, sbi * 512 + 0:sbi * 512 + H])
                v.tensor_copy(cum_tm[:, hs],
                              pA[:, sbi * 512 + 128:sbi * 512 + 128 + H])
                v.tensor_copy(cc_tm[:, hs],
                              pA[:, sbi * 512 + 256:sbi * 512 + 256 + H])
                v.tensor_copy(rows[0:1, 512 + 32 * sbi:512 + 32 * sbi + H],
                              pA[0:1, sbi * 512 + 384:sbi * 512 + 384 + H])
            v.drain()
            S.inc(v.drain(), "dtte", 1)

        blk.tensor(ph_tt_t)
        blk.vector(ph_tt_v)

        def ph_cq_s(s):
            S.wait(s, "dtte")
            s.activation(cexp_tm[:], cum_tm[:], AF.Exp)
            for sbi in range(NSUB):
                s.activation(rows[0:1, 512 + 32 * sbi:512 + 32 * sbi + H],
                             rows[0:1, 512 + 32 * sbi:512 + 32 * sbi + H],
                             AF.Exp)
            s.drain()
            S.inc(s.drain(), "cqe", 1)

        def ph_cq_t(t):
            S.wait(t, "cqe")
            for sbi in range(NSUB):
                mm(t, pA[:, 1024 + sbi * 128:1024 + sbi * 128 + H],
                   onesr2[0:1, 0:128],
                   rows[0:1, 512 + 32 * sbi:512 + 32 * sbi + H],
                   first=True)
            S.inc(_lt[0], "cqp", 1)

        def ph_cq_v(v):
            S.wait(v, "cqp")
            for sbi in range(NSUB):
                v.tensor_copy(ccq_b[:, sbi * H:(sbi + 1) * H],
                              pA[:, 1024 + sbi * 128:1024 + sbi * 128 + H])
            v.drain()
            S.inc(v.drain(), "cqv", 1)
            S.inc(v.drain(), "pAfree", 1)

        blk.scalar(ph_cq_s)
        blk.tensor(ph_cq_t)
        blk.vector(ph_cq_v)

        def ph_halo_v(v, c=c):
            if c > 0:
                S.wait(v, "cvp", lag=0)
                v.tensor_copy(
                    xbcp[:].rearrange("p (i t) -> p i t", i=CT)[:, :, 0:3],
                    tl3[:].rearrange("p (i t) -> p i t", i=CT))
                v.drain()
            S.inc(v.drain(), "halo", 1)

        blk.vector(ph_halo_v)

        def ph_ipx_t(t, ipxe_b=ipxe_b):
            for g in (2, 3, 4):
                t.wait_ge(S._get("ipxe"), ipxe_b + g - 1)
                base = (g % 2) * 1024
                for n, ct in enumerate(groups[g]):
                    for k in range(FT):
                        mm(t, pB[:, base + n * QC:base + (n + 1) * QC],
                           wip(k, DI + ct * 128, DI + (ct + 1) * 128),
                           xv(u2_bf)[:, k, :],
                           first=(n % 2 == 0 and k == 0),
                           last=(k == FT - 1))
                S.inc(_lt[0], "ipxp", 1)

        def ph_ipx_s(s, ipxp_b=ipxp_b):
            S.wait(s, "halo")
            S.wait(s, "opjp", lag=0)
            for g, cts in enumerate(groups):
                s.wait_ge(S._get("ipxp"), ipxp_b + g + 1)
                base = (g % 2) * 1024
                dst = xbcp[:].rearrange("p (i t) -> p i t", i=CT)[
                    :, cts[0]:cts[0] + len(cts), 3:3 + QC]
                src_ = pB[:, base:base + len(cts) * QC].rearrange(
                    "p (i t) -> p i t", i=len(cts))
                s.copy(dst, src_)
                s.drain()
                S.inc(s.drain(), "ipxe", 1)

        blk.tensor(ph_ipx_t)
        blk.scalar(ph_ipx_s)

        if not on("z"):
            continue
        def ph_z_t(t):
            S.wait(t, "ur")
            S.wait(t, "pAfree", lag=0)
            for zb in range(4):
                for k in range(FT):
                    mm(t, pA[:, zb * 512:(zb + 1) * 512],
                       u2_bf[:, k * QC:k * QC + QS],
                       wip(k, zb * 512, (zb + 1) * 512),
                       first=(k == 0), last=False)
                mm(t, pA[:, zb * 512:(zb + 1) * 512], ones2r[0:1, 0:128],
                   badd_z[0:1, zb * 512:(zb + 1) * 512],
                   first=False, last=True)
            S.inc(_lt[0], "z0p", 1)
            S.wait(t, "ipxe", lag=0)
            S.wait(t, "dtte", lag=0)
            for zb in range(4):
                for k in range(FT):
                    mm(t, pB[:, zb * 512:(zb + 1) * 512],
                       u2_bf[:, k * QC + QS:k * QC + 2 * QS],
                       wip(k, zb * 512, (zb + 1) * 512),
                       first=(k == 0), last=False)
                mm(t, pB[:, zb * 512:(zb + 1) * 512], ones2r[0:1, 0:128],
                   badd_z[0:1, zb * 512:(zb + 1) * 512],
                   first=False, last=True)
            S.inc(_lt[0], "z1p", 1)

        blk.tensor(ph_z_t)

        def ph_zev_s(s):
            S.wait(s, "z0p")
            _load_set(s, SET_SILU)
            s.activation(z_si[:, 0:DI], pA[:, :], AF.Silu)
            s.drain()
            S.inc(s.drain(), "z0e", 1)
            S.inc(s.drain(), "pAfree", 1)
            S.wait(s, "z1p")
            s.activation(z_si[:, DI:2 * DI], pB[:, :], AF.Silu)
            s.drain()
            S.inc(s.drain(), "z1e", 1)
            S.inc(s.drain(), "pBfree", 1)

        blk.scalar(ph_zev_s)

        if not on("conv"):
            continue
        def ph_conv_v(v, ipxe_b=ipxe_b):
            S.wait(v, "halo")
            xb = xbcp[:].rearrange("p (i t) -> p i t", i=CT)
            ca = cacc[:].rearrange("p (i t) -> p i t", i=CT)
            for g, cts in enumerate(groups):
                v.wait_ge(S._get("ipxe"), ipxe_b + g + 1)
                for k in range(4):
                    for ct in cts:
                        in1 = (rep_inner(convb[:, ct:ct + 1], QC)[:, 0, :]
                               if k == 0 else ca[:, ct, :])
                        v.scalar_tensor_tensor(
                            ca[:, ct, :], xb[:, ct, k:k + QC],
                            convw[:, ct * 4 + k:ct * 4 + k + 1], in1,
                            ALU.mult, ALU.add)
                    v.drain()
            v.tensor_copy(tl3[:].rearrange("p (i t) -> p i t", i=CT),
                          xbcp[:].rearrange("p (i t) -> p i t", i=CT)
                          [:, :, QC:QC + 3])
            v.drain()
            S.inc(v.drain(), "convd", 1)
            S.inc(v.drain(), "cvp", 1)

        blk.vector(ph_conv_v)

        if not on("silu"):
            continue
        def ph_silu_s(s):
            S.wait(s, "convd")
            s.activation(cacc[:], cacc[:], AF.Silu, bias=0.0)
            s.drain()
            S.inc(s.drain(), "xcsr", 1)
            _load_set(s, SET_EXP_LN)
            S.inc(s.drain(), "sexp", 1)

        blk.scalar(ph_silu_s)

        if not on("ssd"):
            continue
        Bts = [cacc[:, 16 * QC + i * QS:16 * QC + (i + 1) * QS]
               for i in range(NSUB)]
        Cts = [cacc[:, 17 * QC + i * QS:17 * QC + (i + 1) * QS]
               for i in range(NSUB)]

        def emit_cfl(sbi, extra=()):
            def ssd_cfl(g, sbi=sbi, extra=tuple(extra)):
                S.wait(g, "dtte")
                for nm in extra:
                    S.wait(g, nm, lag=0)
                for half in range(2):
                    hsl = slice(half * 16, (half + 1) * 16)
                    S.inc(g.dma_start(
                        out=scrA[0:1, half * DI:half * DI + 2048],
                        in_=cumhi[hsl, sbi * QS:(sbi + 1) * QS]), "cfl", 16)
                    S.inc(g.dma_start(
                        out=scrA[1:2, half * DI:half * DI + 2048],
                        in_=cumlo[hsl, sbi * QS:(sbi + 1) * QS]), "cfl", 16)
            blk.gpsimd(ssd_cfl)

        def mk_mm(t, sbi, half, pp):
            frows = scrA[0:2, half * DI:half * DI + 2048]
            tsl = slice(sbi * QS, (sbi + 1) * QS)
            for q in range(4):
                h0 = half * 16 + q * 4
                mm(t, pp[:, q * 512:(q + 1) * 512], ones2r[0:2, 0:128],
                   frows[0:2, q * 512:(q + 1) * 512], first=True, last=False)
                mm(t, pp[:, q * 512:(q + 1) * 512], trin_b[:],
                   rep_outer(id_b[:], 4), first=False, last=False)
                mm(t, pp[:, q * 512:(q + 1) * 512], cumhi[0:H, tsl],
                   rep_inner(neg_id32[:, h0:h0 + 4], 128),
                   first=False, last=False)
                mm(t, pp[:, q * 512:(q + 1) * 512], cumlo[0:H, tsl],
                   rep_inner(neg_id32[:, h0:h0 + 4], 128),
                   first=False, last=True)
            S.inc(_lt[0], "mkp%d" % sbi, 1)

        def mk_exp(s, sbi, half, pp):
            S.wait(s, "mkp%d" % sbi, lag=0)
            s.activation(scrA[:, half * DI:(half + 1) * DI], pp[:, :], AF.Exp)
            s.drain()
            S.inc(s.drain(), "mke%d" % sbi, 1)

        def mk0_t(t):
            S.wait(t, "cfl", lag=0)
            S.wait(t, "z0e", lag=0)
            mk_mm(t, 0, 0, pA)
            S.wait(t, "z1e", lag=0)
            mk_mm(t, 0, 1, pB)

        def mk0_s(s):
            S.wait(s, "sexp")
            S.wait(s, "stp", lag=0)
            S.wait(s, "upre", lag=0)
            S.wait(s, "xres", lag=0)
            mk_exp(s, 0, 0, pA)
            mk_exp(s, 0, 1, pB)

        emit_cfl(0, extra=("intrp", "x2", "stp", "upre", "xres"))
        blk.tensor(mk0_t)
        blk.scalar(mk0_s)

        def emit_sub(sbi):
            hs = slice(sbi * H, (sbi + 1) * H)
            Bt, Ctl = Bts[sbi], Cts[sbi]
            mk = "mke%d" % sbi

            def t1a(t, sbi=sbi, Bt=Bt, Ctl=Ctl, mk=mk):
                if sbi == 0:
                    S.wait(t, "xcsr")
                    S.wait(t, mk, lag=1)
                else:
                    S.wait(t, "yfme", lag=0)
                mm(t, pA[:, 0:QS], Bt, Ctl, first=True)
                S.inc(_lt[0], "scp", 1)
                mm(t, pA[:, 512:512 + QS], Bt, id_b[:], first=True)
                S.inc(_lt[0], "btp", 1)
                if sbi == 0:
                    S.wait(t, mk, lag=0)
                    S.wait(t, "z1e", lag=0)
                else:
                    S.wait(t, "supd", lag=0)
                    S.wait(t, "mke1", lag=0)
                for i in range(DIT):
                    mm(t, pB[:, i * 128:(i + 1) * 128],
                       cacc[:, i * QC + sbi * QS:i * QC + (sbi + 1) * QS],
                       id_b[:], first=(i % 4 == 0))
                S.inc(_lt[0], "xsp", 1)

            def v1a(v):
                S.wait(v, "scp")
                v.tensor_tensor(stri[:, 0:QS], pA[:, 0:QS], tri_b[:],
                                ALU.mult)
                v.drain()
                S.inc(v.drain(), "strir", 1)
                S.wait(v, "btp")
                v.tensor_copy(btm[:], pA[:, 512:512 + QS])
                v.drain()
                S.inc(v.drain(), "btme", 1)

            def v2w(v, mk=mk):
                S.wait(v, mk, lag=0)
                S.wait(v, "strir", lag=0)
                v.tensor_tensor(scrA[:].rearrange("p (h i) -> p h i", h=H),
                                scrA[:].rearrange("p (h i) -> p h i", h=H),
                                rep_outer(stri[:, 0:QS], H), ALU.mult)
                v.drain()
                S.inc(v.drain(), "wres", 1)

            def v1b(v, hs=hs):
                S.wait(v, "xsp")
                v.tensor_tensor(
                    xdt[:].rearrange("p (h e) -> p h e", h=H),
                    pB[:].rearrange("p (h e) -> p h e", h=H),
                    rep_inner(dt_tm[:, hs], HD), ALU.mult)
                v.drain()
                S.inc(v.drain(), "xdtr", 1)

            def t4(t, sbi=sbi, Ctl=Ctl):
                S.wait(t, "strir", lag=0)
                S.wait(t, "btme", lag=0)
                S.wait(t, "supd", lag=0)
                for b in range(4):
                    mm(t, pA[:, b * 512:(b + 1) * 512], Ctl,
                       s_bf[:, b * 512:(b + 1) * 512], first=True)
                S.inc(_lt[0], "intp", 1)
                S.wait(t, "wres", lag=0)
                S.wait(t, "xdtr", lag=0)
                for h in range(H):
                    mm(t, pB[:, h * HD:(h + 1) * HD],
                       scrA[:, h * 128:(h + 1) * 128],
                       xdt[:, h * HD:(h + 1) * HD],
                       first=(h % 8 == 0), last=False)
                for i in range(DIT):
                    mm(t, pB[:, i * 128:(i + 1) * 128],
                       cacc[:, i * QC + sbi * QS:i * QC + (sbi + 1) * QS],
                       ddiag[:, i * 128:(i + 1) * 128],
                       first=False, last=(i == DIT - 1))
                S.inc(_lt[0], "intrp", 1)

            def v4(v, sbi=sbi, hs=hs):
                S.wait(v, "intp")
                v.tensor_tensor(
                    scr2[:].rearrange("p (h e) -> p h e", h=H),
                    pA[:].rearrange("p (h e) -> p h e", h=H),
                    rep_inner(cexp_tm[:, hs], HD), ALU.mult)
                v.drain()
                S.inc(v.drain(), "s1r", 1)
                S.wait(v, "intrp")
                v.tensor_tensor(y_bf[:], pB[:, :], scr2[:], ALU.add)
                v.drain()
                S.inc(v.drain(), "yr", 1)
                v.tensor_tensor(scr2[:], y_bf[:],
                                z_si[:, sbi * DI:(sbi + 1) * DI], ALU.mult)
                v.drain()
                S.inc(v.drain(), "ygr", 1)
                v.tensor_tensor(
                    xdt[:].rearrange("p (h e) -> p h e", h=H),
                    xdt[:].rearrange("p (h e) -> p h e", h=H),
                    rep_inner(cc_tm[:, hs], HD), ALU.mult)
                v.drain()
                S.inc(v.drain(), "xdt2r", 1)

            def s5(s, c=c):
                S.wait(s, "ygr")
                S.wait(s, "xres", lag=0)
                s.activation(land[c % 2][:, 0:DI], scr2[:], AF.Square,
                             accum_out=ssq[:, 0:1])
                s.drain()
                s.activation(rms_c[:], ssq[:], AF.Ln, scale=1.0 / DI,
                             bias=EPS)
                s.drain()
                s.activation(rms_c[:], rms_c[:], AF.Exp, scale=-0.5)
                s.drain()
                s.activation(y_bf[:], scr2[:], AF.Identity,
                             scale=rms_c[:, 0:1])
                s.drain()
                S.inc(s.drain(), "ynr", 1)

            def t6a(t, sbi=sbi):
                S.wait(t, "ynr")
                S.wait(t, "s1r", lag=0)
                if sbi == 0:
                    S.wait(t, "mke1", lag=0)
                for i in range(DIT):
                    mm(t, pA[:, i * 128:(i + 1) * 128],
                       y_bf[:, i * 128:(i + 1) * 128], id_b[:],
                       first=(i % 4 == 0))
                S.inc(_lt[0], "ytp", 1)

            def t6b(t, sbi=sbi):
                S.wait(t, "xdt2r", lag=0)
                S.wait(t, "yr", lag=0)
                for b in range(4):
                    mm(t, pB[:, b * 512:(b + 1) * 512], btm[:],
                       xdt[:, b * 512:(b + 1) * 512], first=True)
                S.inc(_lt[0], "updp", 1)

            def s6(s, sbi=sbi):
                S.wait(s, "ytp")
                s.copy(yfm.rearrange("p (i t) -> p i t", i=DIT)
                       [:, :, sbi * QS:(sbi + 1) * QS],
                       pA[:].rearrange("p (i t) -> p i t", i=DIT))
                s.drain()
                S.inc(s.drain(), "yfme", 1)
                S.inc(s.drain(), "pAfree", 1)

            def v7a(v, hs=hs):
                v.tensor_tensor(
                    sdec[:].rearrange("p (h e) -> p h e", h=H),
                    s_bf[:].rearrange("p (h e) -> p h e", h=H),
                    rep_inner(ccq_b[:, hs], HD), ALU.mult)
                v.drain()
                S.inc(v.drain(), "sdcr", 1)

            def v7b(v):
                S.wait(v, "updp")
                v.tensor_tensor(s_bf[:], sdec[:], pB[:, :], ALU.add)
                v.drain()
                S.inc(v.drain(), "supd", 1)
                S.inc(v.drain(), "pBfree", 1)

            return t1a, v1a, v2w, v1b, t4, v4, s5, t6a, t6b, s6, v7a, v7b

        s0 = emit_sub(0)
        s1_ = emit_sub(1)

        blk.tensor(s0[0])
        blk.vector(s0[1])
        blk.vector(s0[2])
        blk.vector(s0[3])
        blk.tensor(s0[4])
        emit_cfl(1, extra=("intrp",))
        blk.vector(s0[5])
        def mk1_t0(t):
            S.wait(t, "cfl", lag=0)
            S.wait(t, "s1r", lag=0)
            mk_mm(t, 1, 0, pA)
        blk.tensor(mk1_t0)
        blk.scalar(s0[6])
        def mk1_s0(s):
            mk_exp(s, 1, 0, pA)
        blk.scalar(mk1_s0)
        blk.tensor(s0[8])
        blk.tensor(s0[7])
        blk.vector(s0[10])
        blk.vector(s0[11])
        def mk1_t1(t):
            S.wait(t, "supd", lag=0)
            mk_mm(t, 1, 1, pB)
        blk.tensor(mk1_t1)
        blk.scalar(s0[9])
        def mk1_s1(s):
            mk_exp(s, 1, 1, pB)
        blk.scalar(mk1_s1)

        blk.tensor(s1_[0])
        blk.vector(s1_[1])
        blk.vector(s1_[2])
        blk.vector(s1_[3])
        blk.tensor(s1_[4])
        blk.vector(s1_[5])
        blk.scalar(s1_[6])
        blk.tensor(s1_[7])
        blk.tensor(s1_[8])
        blk.scalar(s1_[9])
        blk.vector(s1_[10])
        blk.vector(s1_[11])

        opjc_b = S.c.get("opjc", 0)
        xin3c_b = S.c.get("xin3c", 0)
        x2c_b = S.c.get("x2c", 0)

        def ph_opj_t(t):
            S.wait(t, "yfme")
            S.wait(t, "pAfree", lag=0)
            for ct in range(FT):
                for k in range(DIT):
                    mm(t, pA[:, ct * QC:(ct + 1) * QC],
                       wopd(k, ct * 128, (ct + 1) * 128),
                       yfm[:, k * QC:(k + 1) * QC],
                       first=(ct % 2 == 0 and k == 0),
                       last=(k == DIT - 1))
                if ct % 2 == 1:
                    S.inc(_lt[0], "opjc", 1)
            S.inc(t.drain(), "opjp", 1)

        def ph_opj_v(v, opjc_b=opjc_b):
            S.wait(v, "xres", lag=0)
            for p in range(4):
                v.wait_ge(S._get("opjc"), opjc_b + p + 1)
                v.tensor_tensor(xf[:, p * 512:(p + 1) * 512],
                                xf[:, p * 512:(p + 1) * 512],
                                pA[:, p * 512:(p + 1) * 512], ALU.add)
                v.drain()
                S.inc(v.drain(), "xin3c", 1)
            S.inc(v.drain(), "xin3", 1)
            S.inc(v.drain(), "pAfree", 1)

        blk.tensor(ph_opj_t)
        blk.vector(ph_opj_v)

        if not on("ln3"):
            continue
        def ln3_x2_s(s, xin3c_b=xin3c_b):
            S.wait(s, "upre", lag=0)
            S.wait(s, "xres", lag=0)
            S.wait(s, "stp", lag=0)
            for p in range(4):
                s.wait_ge(S._get("xin3c"), xin3c_b + p + 1)
                s.copy(x_bf[:, p * 512:(p + 1) * 512],
                       xf[:, p * 512:(p + 1) * 512])
                s.drain()
                s.activation(scrA[:, p * 512:(p + 1) * 512],
                             x_bf[:, p * 512:(p + 1) * 512], AF.Square)
                s.drain()
                S.inc(s.drain(), "x2c", 1)
            S.inc(s.drain(), "xbf", 1)
            S.inc(s.drain(), "x2", 1)

        def ln3_stats_t(t, x2c_b=x2c_b):
            S.wait(t, "pBfree", lag=0)
            for p in range(4):
                t.wait_ge(S._get("x2c"), x2c_b + p + 1)
                for k in (2 * p, 2 * p + 1):
                    mm(t, pB[0:1, 0:512], ones_c[:],
                       bass.AP(scrA, k * QC,
                               [[H * QS, 128], [FT * QC, 2], [1, QC]]),
                       first=(k == 0), last=(k == FT - 1))
            S.inc(_lt[0], "stp", 1)

        blk.scalar(ln3_x2_s)
        blk.tensor(ln3_stats_t)
        emit_ln_rest(fused=False, udst_f8=True, xf=xf, po=0)
        pipe = c + 1 < nch and on("xtr") and on("ln1")
        if pipe:
            emit_xtr(c + 1)

        if not on("ffn2"):
            continue
        def ph_ffn2_t(t):
            S.wait(t, "ur")
            S.wait(t, "pAfree", lag=0)
            ffn_mms(t, pA)
            S.inc(_lt[0], "f2p", 1)

        blk.tensor(ph_ffn2_t)
        if pipe:
            emit_ln_stats("xin1", x_fms[(c + 1) % 2], po=1024)

        def ph_ffn2_v_s(s, c=c):
            S.wait(s, "f2p")
            lt = land[c % 2]
            for ct in range(FT):
                s.activation(lt[:, ct * QC:(ct + 1) * QC],
                             pA[:, ct * QC:(ct + 1) * QC], AF.Identity,
                             scale=0.5, bias=bffh[:, ct:ct + 1])
            s.drain()
            S.inc(s.drain(), "f2pE", 1)
            S.inc(s.drain(), "pAfree", 1)

        def ph_ffn2_v(v, c=c):
            S.wait(v, "f2pE")
            lt = land[c % 2]
            v.tensor_tensor(xf[:], xf[:], lt[:, 0:FT * QC], ALU.add)
            v.drain()
            S.inc(v.drain(), "x3r", 1)

        blk.scalar(ph_ffn2_v_s)
        blk.vector(ph_ffn2_v)
        if pipe:
            emit_ln_rest(fused=False, udst_f8=True, xf=x_fms[(c + 1) % 2],
                         po=1024, ugate="f2p")
            emit_front2(c + 1)

        if not on("out"):
            continue
        if pipe:
            emit_ln_rest(fused=True, udst_f8=False, xf=x_fms[(c + 1) % 2],
                         po=0)
        def ph_out_t(t):
            S.wait(t, "x3r")
            S.wait(t, "pAfree", lag=0)
            for tt in range(2):
                for i in range(FT):
                    tr(t, pA[:, tt * D + i * 128:tt * D + (i + 1) * 128],
                                xf[:, i * QC + tt * 128:
                                   i * QC + tt * 128 + 128], id_f[:])
            S.inc(_lt[0], "outp", 1)

        def ph_out_v(s, c=c):
            S.wait(s, "outp")
            s.copy(land[c % 2][:, :], pA[:, :])
            s.drain()
            S.inc(s.drain(), "oute", 1)
            S.inc(s.drain(), "pAfree", 1)

        blk.tensor(ph_out_t)
        blk.scalar(ph_out_v)

        def ph_store_y(y, c=c):
            S.wait(y, "oute")
            for tt in range(2):
                r0 = c * QC + tt * 128
                S.inc(y.dma_start(out=out_h[r0:r0 + 128, :],
                                  in_=land[c % 2][:, tt * D:(tt + 1) * D]),
                      "st", 16)
            if c == nch - 1:
                S.wait(y, "st")

        blk.sync(ph_store_y)

    es.close()
    nc.finalize()
    return nc


def kernel(L0_attn_x, L0_attn_ln_w, L0_attn_ln_b, L0_attn_in_proj_w,
           L0_attn_conv_w, L0_attn_conv_b, L0_attn_A_log, L0_attn_D,
           L0_attn_dt_bias, L0_attn_gamma, L0_attn_out_proj_w,
           L0_ffn_weight, L0_ffn_bias):
    B, T, _ = L0_attn_x.shape
    nc = build_nc(T)
    common = {
        "w_ip": np.ascontiguousarray(L0_attn_in_proj_w),
        "w_op": np.ascontiguousarray(L0_attn_out_proj_w),
        "w_ff": np.ascontiguousarray(L0_ffn_weight),
        "b_ff": np.ascontiguousarray(L0_ffn_bias),
        "ln_w": np.ascontiguousarray(L0_attn_ln_w),
        "ln_b": np.ascontiguousarray(L0_attn_ln_b),
        "conv_w": np.ascontiguousarray(L0_attn_conv_w),
        "conv_b": np.ascontiguousarray(L0_attn_conv_b),
        "a_log": np.ascontiguousarray(L0_attn_A_log),
        "d_vec": np.ascontiguousarray(L0_attn_D),
        "dt_bias": np.ascontiguousarray(L0_attn_dt_bias),
        "gamma": np.ascontiguousarray(L0_attn_gamma),
    }
    in_maps = [dict(common, x=np.ascontiguousarray(L0_attn_x[b]))
               for b in range(B)]
    res = run_bass_kernel_spmd(nc, in_maps, core_ids=list(range(B)))
    return np.stack([r["out"] for r in res.results], axis=0)
